# version 15
# baseline (speedup 1.0000x reference)
import math
import numpy as np
import ml_dtypes
import concourse.bass as bass
import concourse.mybir as mybir
from concourse.bass_utils import run_bass_kernel_spmd

F32 = mybir.dt.float32
BF16 = mybir.dt.bfloat16
U16 = mybir.dt.uint16
U32 = mybir.dt.uint32
I8 = mybir.dt.int8
AF = mybir.ActivationFunctionType
ALU = mybir.AluOpType
AX = mybir.AxisListType
ds = bass.ds

NEG = -1.0e30


class Cfg:
    def __init__(self, D=4096, SEQ=8192):
        self.NC = 8
        self.D = D
        self.SEQ = SEQ
        self.DC = D // 128
        self.T = SEQ // self.NC
        self.NJ = self.T // 128
        self.SPG = min(4, self.NJ)
        self.NG = self.NJ // self.SPG
        self.GQ = self.SPG * 128
        self.HA = D // 128
        self.HB = D // 256
        self.OA = self.DC + 2
        self.NSLOT = SEQ // 128
        self.scale_a = 1.0 / math.sqrt(128.0)
        self.scale_b = 1.0 / 16.0
        self.ROT = 64


def alibi_slopes(n):
    cp2 = 2 ** int(math.floor(math.log2(n)))
    base = 2.0 ** (-(2.0 ** (-(math.log2(cp2) - 3))))
    slopes = base ** np.arange(1, cp2 + 1, dtype=np.float64)
    if cp2 != n:
        extra_base = 2.0 ** (-(2.0 ** (-(math.log2(2 * cp2) - 3))))
        rem = min(cp2, n - cp2)
        extra = extra_base ** np.arange(1, 1 + 2 * rem, 2, dtype=np.float64)
        slopes = np.concatenate([slopes, extra])
    return slopes.astype(np.float64)


def host_tables(cfg, core):
    t = {}
    sl = alibi_slopes(cfg.HA)
    k = (np.arange(cfg.NSLOT)[None, :] * 128 + np.arange(128)[:, None]).astype(np.float64)
    ali = k[:, None, :] * sl[None, :, None]
    ali = ali.reshape(128, cfg.HA * cfg.NSLOT)
    t["alibi_post"] = ali.astype(np.float32)
    t["alibi_raw"] = (ali / cfg.scale_a).astype(np.float32)
    m1 = np.zeros((128, cfg.GQ), np.float32)
    if core == 0:
        p = np.arange(128)[:, None]
        q = np.arange(128)[None, :]
        m1[:, 0:128] = np.where(p > q, np.float32(NEG), np.float32(0.0))
    t["p1mask"] = m1.astype(ml_dtypes.bfloat16)
    am = np.zeros((128, 8 * 128), np.uint16)
    p = np.arange(128)[:, None]
    q = np.arange(128)[None, :]
    for dpos in range(8):
        blk = am[:, dpos * 128:(dpos + 1) * 128]
        if dpos < core:
            blk[:] = 0xFFFF
        elif dpos == core:
            blk[:] = np.where(p <= q, 0xFFFF, 0).astype(np.uint16)
    t["andmask"] = am
    pos = np.zeros((cfg.T,), np.int64)
    for j in range(cfg.NJ):
        pos[j * 128:(j + 1) * 128] = (8 * j + core) * 128 + np.arange(128)
    inv = 1.0 / (10000.0 ** (np.arange(0, cfg.ROT, 2, dtype=np.float64) / cfg.ROT))
    ang = pos[:, None].astype(np.float64) * inv[None, :]
    t["rot_sin"] = np.repeat(np.sin(ang), 2, axis=1).astype(ml_dtypes.bfloat16)
    t["rot_cos"] = np.repeat(np.cos(ang), 2, axis=1).astype(ml_dtypes.bfloat16)
    t["ident_bf"] = np.eye(128, dtype=ml_dtypes.bfloat16)
    t["ident_f32"] = np.eye(128, dtype=np.float32)
    t["ones_f32"] = np.ones((128, 1), np.float32)
    t["ones_row"] = np.ones((1, 128), np.float32)
    t["ones_row_bf"] = np.ones((1, 128), ml_dtypes.bfloat16)
    slq = (sl[:, None] * pos[None, :].astype(np.float64) / cfg.scale_a).reshape(1, cfg.HA * cfg.T)
    t["slq"] = slq.astype(np.float32)
    return t


_CTR_ID = [0]
_TN_ID = [0]


def tn(name):
    _TN_ID[0] += 1
    return f"{name}_{_TN_ID[0]}"


class Ctr:
    def __init__(self, nc, name):
        _CTR_ID[0] += 1
        self.s = nc.alloc_semaphore(name=f"{name}_{_CTR_ID[0]}")
        self.n = 0


def inc(instr, c, k=1):
    instr.then_inc(c.s, k)
    c.n += k
    return instr


DBG = False
DBG_SLOT = 0


def build_program(cfg, n_sub=4):
    nc = bass.Bass("TRN2", target_bir_lowering=False, dynamic_dma_scratch_size=32)
    C = cfg
    NC, D, DC, T, NJ, SPG, NG, GQ = C.NC, C.D, C.DC, C.T, C.NJ, C.SPG, C.NG, C.GQ
    HA, HB, OA, NSLOT = C.HA, C.HB, C.OA, C.NSLOT
    TW = min(512, T)
    NTW = T // TW
    NTT = T // 128

    xT = nc.dram_tensor("xT", [D, T], F32, kind="ExternalInput")
    RSL = D // NC
    w_qkv = nc.dram_tensor("w_qkv", [RSL, OA * 128], F32, kind="ExternalInput")
    w_oa = nc.dram_tensor("w_oa", [RSL, D], F32, kind="ExternalInput")
    w_q = nc.dram_tensor("w_q", [RSL, D], F32, kind="ExternalInput")
    w_k = nc.dram_tensor("w_k", [RSL, D], F32, kind="ExternalInput")
    w_v = nc.dram_tensor("w_v", [RSL, D], F32, kind="ExternalInput")
    w_ob = nc.dram_tensor("w_ob", [RSL, D], F32, kind="ExternalInput")
    w_f8 = nc.dram_tensor("w_f8", [D, RSL], I8, kind="ExternalInput")
    f_scale = nc.dram_tensor("f_scale", [DC, 128], F32, kind="ExternalInput")
    f_bias = nc.dram_tensor("f_bias", [DC, 128], F32, kind="ExternalInput")
    alibi_raw_t = nc.dram_tensor("alibi_raw", [128, HA * NSLOT], F32, kind="ExternalInput")
    alibi_post_t = nc.dram_tensor("alibi_post", [128, HA * NSLOT], F32, kind="ExternalInput")
    p1mask_in = nc.dram_tensor("p1mask", [128, GQ], BF16, kind="ExternalInput")
    andmask_in = nc.dram_tensor("andmask", [128, 8 * 128], U16, kind="ExternalInput")
    rot_sin_in = nc.dram_tensor("rot_sin", [T, 64], BF16, kind="ExternalInput")
    rot_cos_in = nc.dram_tensor("rot_cos", [T, 64], BF16, kind="ExternalInput")
    ident_bf_in = nc.dram_tensor("ident_bf", [128, 128], BF16, kind="ExternalInput")
    ones_f32_in = nc.dram_tensor("ones_f32", [128, 1], F32, kind="ExternalInput")
    ones_row_in = nc.dram_tensor("ones_row", [1, 128], F32, kind="ExternalInput")
    ones_row_bf_in = nc.dram_tensor("ones_row_bf", [1, 128], BF16, kind="ExternalInput")
    slq_in = nc.dram_tensor("slq", [1, HA * T], F32, kind="ExternalInput")
    outT = nc.dram_tensor("outT", [D, T], F32, kind="ExternalOutput")
    dbg_acc = nc.dram_tensor("dbg_acc", [128, GQ], F32)
    dbg_maxB = nc.dram_tensor("dbg_maxB", [128, GQ], F32)
    dbg_stg = nc.dram_tensor("dbg_stg", [128, GQ], F32)
    dbg_PT = nc.dram_tensor("dbg_PT", [128, GQ], BF16)
    dbg_pO = nc.dram_tensor("dbg_pO", [128, 260], F32)

    cv_sl, cv_full = {}, {}
    for nm, ocols in (("qkv", OA * 128), ("oa", D), ("q", D), ("k", D), ("v", D), ("ob", D), ("ft", D)):
        cv_sl[nm] = nc.dram_tensor(f"cv_sl_{nm}", [D // NC, ocols], BF16)
        cv_full[nm] = nc.dram_tensor(f"cv_{nm}", [D, ocols], BF16)
    qkv_sp = nc.dram_tensor("qkv_sp", [OA * 128, T], BF16)
    va_loc = nc.dram_tensor("va_loc", [T, 128], BF16)
    kta_g = nc.dram_tensor("kta_g", [NC * 128, T], BF16)
    va_g = nc.dram_tensor("va_g", [NC * T, 128], BF16)
    qb_sp = nc.dram_tensor("qb_sp", [D, T], BF16)
    ktb_loc = nc.dram_tensor("ktb_loc", [D, T], BF16)
    vtb_sp = nc.dram_tensor("vtb_sp", [D, T], BF16)
    vb_loc = nc.dram_tensor("vb_loc", [T, D], BF16)
    ktb_g = nc.dram_tensor("ktb_g", [NC * D, T], BF16)
    vb_g = nc.dram_tensor("vb_g", [NC * T, D], BF16)
    RG = [list(range(NC))]

    persist = []
    def sbp(name, shape, dtype):
        g = nc.sbuf_tensor(name, shape, dtype)
        h = g.__enter__()
        persist.append(g)
        return h

    X = sbp("X", [128, DC * T], F32)
    identb = sbp("identb", [128, 128], BF16)
    onesf = sbp("onesf", [128, 1], F32)
    onesr = sbp("onesr", [1, 128], F32)
    onesrb = sbp("onesrb", [1, 128], BF16)
    amask = sbp("amask", [128, 8 * 128], U16)
    p1m = sbp("p1m", [128, GQ], BF16)
    eps_col = sbp("eps_col", [128, 1], F32)

    pool = [nc.alloc_semaphore(name=f"pool{i}") for i in range(nc.free_len())]
    pool_ids = sorted(h.num for h in pool)
    assert pool_ids == list(range(pool_ids[0], pool_ids[0] + len(pool))), "pool ids not contiguous"
    pool_range = range(pool_ids[0], pool_ids[0] + len(pool))
    free_sems = list(pool)
    used_sems = []

    def ctr(name):
        h = free_sems.pop()
        used_sems.append(h)
        c = Ctr.__new__(Ctr)
        c.s = h
        c.n = 0
        return c

    def barrier():
        nc.all_engine_barrier()
        if used_sems:
            nc.gpsimd.dma_reset(pool_range)
            nc.gpsimd.sem_clear(pool_range)
            free_sems.extend(used_sems)
            used_sems.clear()
            nc.all_engine_barrier()

    s0 = ctr("s0")
    for dc in range(DC):
        inc(nc.sync.dma_start(X[:, dc * T:(dc + 1) * T], xT[dc * 128:(dc + 1) * 128, :]), s0, 16)
    for dst, src in ((identb, ident_bf_in), (onesf, ones_f32_in),
                     (onesr, ones_row_in), (onesrb, ones_row_bf_in),
                     (amask, andmask_in), (p1m, p1mask_in)):
        inc(nc.sync.dma_start(dst[:, :], src[:, :]), s0, 16)
    nc.sync.wait_ge(s0.s, s0.n)
    nc.vector.memset(eps_col[:, :], 1e-5)
    barrier()

    RS = D // NC
    NRT = RS // 128
    with (
        nc.sbuf_tensor("cwi", [128, 2 * (OA * 128)], F32) as cwi,
        nc.sbuf_tensor("cwo", [128, 2 * (OA * 128)], BF16) as cwo,
        nc.sbuf_tensor("ci8", [128, 2 * RS], I8) as ci8,
        nc.sbuf_tensor("cb16", [128, 2 * RS], BF16) as cb16,
        nc.sbuf_tensor("ctr8", [128, 2 * RS], BF16) as ctr8,
        nc.psum_tensor("ptr0", [128, 128], BF16) as ptr0,
        nc.psum_tensor("ptr1", [128, 128], BF16) as ptr1,
    ):
        ld = [ctr("cvl0"), ctr("cvl1")]
        cvs = [ctr("cvc0"), ctr("cvc1")]
        st = [ctr("cvs0"), ctr("cvs1")]
        for wi, (nm, src, ocols) in enumerate((("qkv", w_qkv, OA * 128), ("oa", w_oa, D),
                                               ("q", w_q, D), ("k", w_k, D), ("v", w_v, D), ("ob", w_ob, D))):
            for r in range(NRT):
                par = (wi * NRT + r) % 2
                nc.sync.wait_ge(st[par].s, st[par].n)
                inc(nc.sync.dma_start(cwi[:, par * (OA * 128): par * (OA * 128) + ocols],
                                      src[r * 128:(r + 1) * 128, :]), ld[par], 16)
                nc.vector.wait_ge(ld[par].s, ld[par].n)
                inc(nc.vector.tensor_copy(cwo[:, par * (OA * 128): par * (OA * 128) + ocols],
                                          cwi[:, par * (OA * 128): par * (OA * 128) + ocols]), cvs[par])
                nc.scalar.wait_ge(cvs[par].s, cvs[par].n)
                inc(nc.scalar.dma_start(cv_sl[nm][r * 128:(r + 1) * 128, :],
                                        cwo[:, par * (OA * 128): par * (OA * 128) + ocols]), st[par], 16)
        l8 = [ctr("l80"), ctr("l81")]
        c8 = [ctr("c80"), ctr("c81")]
        t8 = [ctr("t80"), ctr("t81")]
        e8 = [ctr("e80"), ctr("e81")]
        sp8 = [ctr("sp80"), ctr("sp81")]
        for oc in range(DC):
            par = oc % 2
            nc.sync.wait_ge(sp8[par].s, sp8[par].n)
            inc(nc.sync.dma_start(ci8[:, par * RS: par * RS + RS],
                                  w_f8[oc * 128:(oc + 1) * 128, :]), l8[par], 16)
            nc.vector.wait_ge(l8[par].s, l8[par].n)
            nc.vector.wait_ge(t8[par].s, t8[par].n)
            inc(nc.vector.tensor_copy(cb16[:, par * RS: par * RS + RS],
                                      ci8[:, par * RS: par * RS + RS]), c8[par])
            nc.tensor.wait_ge(c8[par].s, c8[par].n)
            ptr = ptr0 if par == 0 else ptr1
            for tt in range(NRT):
                nc.tensor.wait_ge(e8[par].s, e8[par].n)
                inc(nc.tensor.transpose(ptr[:, :], cb16[:, par * RS + tt * 128: par * RS + (tt + 1) * 128],
                                        identb[:, :]), t8[par])
                nc.vector.wait_ge(t8[par].s, t8[par].n)
                inc(nc.vector.tensor_copy(ctr8[:, par * RS + tt * 128: par * RS + (tt + 1) * 128],
                                          ptr[:, :]), e8[par])
            nc.scalar.wait_ge(e8[par].s, e8[par].n)
            inc(nc.scalar.dma_start(
                cv_sl["ft"].ap().rearrange("(t p) o -> p t o", p=128)[:, :, oc * 128:(oc + 1) * 128],
                ctr8[:, par * RS: par * RS + RS].rearrange("p (t q) -> p t q", q=128),
            ), sp8[par], 16)
        for par in range(2):
            nc.gpsimd.wait_ge(st[par].s, st[par].n)
            nc.gpsimd.wait_ge(sp8[par].s, sp8[par].n)
        for nm in ("qkv", "oa", "q", "k", "v", "ob", "ft"):
            gs = ctr("g_" + nm)
            inc(nc.gpsimd.collective_compute("AllGather", ALU.bypass, replica_groups=RG,
                                             ins=[cv_sl[nm].ap()], outs=[cv_full[nm].ap()]), gs)
            nc.gpsimd.wait_ge(gs.s, gs.n)
    barrier()

    def emit_ln(ln_buf):
        with (
            nc.sbuf_tensor(tn("x2"), [128, 2 * T], F32) as x2,
            nc.sbuf_tensor(tn("stats"), [128, 2 * NTT], F32) as stats,
            nc.sbuf_tensor(tn("abrow"), [1, 2 * T], BF16) as abrow,
            nc.sbuf_tensor(tn("abB"), [128, 2 * T], BF16) as abB,
            nc.sbuf_tensor(tn("acol"), [128, 2 * NTT], F32) as acol,
            nc.sbuf_tensor(tn("acolb"), [128, 2 * NTT], BF16) as acolb,
            nc.sbuf_tensor(tn("lt"), [128, T], F32) as lt,
            nc.psum_tensor(tn("pstt"), [128, 2 * NTT * DC], F32) as pstt,
            nc.psum_tensor(tn("prow"), [1, 128], BF16) as prow,
            nc.psum_tensor(tn("pbc0"), [128, TW], F32) as pbc0,
            nc.psum_tensor(tn("pbc1"), [128, TW], F32) as pbc1,
        ):
            sq = ctr("lnsq"); mm = ctr("lnmm"); ev = ctr("lnev"); tr = ctr("lntr")
            rw = ctr("lnrw"); bc = ctr("lnbc"); nm_ = ctr("lnnm")
            for dc in range(DC):
                par = dc % 2
                if dc >= 2:
                    nc.scalar.wait_ge(mm.s, (dc - 1) * NTT)
                inc(nc.scalar.activation(x2[:, par * T:(par + 1) * T], X[:, dc * T:(dc + 1) * T],
                                         AF.Square), sq)
                nc.tensor.wait_ge(sq.s, sq.n)
                for tt in range(NTT):
                    nc.tensor.matmul(pstt[:, (2 * tt) * DC + dc:(2 * tt) * DC + dc + 1],
                                     X[:, dc * T + tt * 128: dc * T + (tt + 1) * 128],
                                     onesf[:, :], start=True, stop=True)
                    inc(nc.tensor.matmul(pstt[:, (2 * tt + 1) * DC + dc:(2 * tt + 1) * DC + dc + 1],
                                         x2[:, par * T + tt * 128: par * T + (tt + 1) * 128],
                                         onesf[:, :], start=True, stop=True), mm)
            nc.vector.wait_ge(mm.s, mm.n)
            inc(nc.vector.reduce_sum(stats[:, :], pstt.ap().rearrange("p (s d) -> p s d", d=DC), AX.X), ev)
            nc.vector.wait_ge(ev.s, ev.n)
            inc(nc.vector.tensor_scalar_mul(stats[:, :], stats[:, :], 1.0 / D), ev)
            nc.vector.wait_ge(ev.s, ev.n)
            inc(nc.vector.tensor_mul(acol[:, 0:2 * NTT:2], stats[:, 0:2 * NTT:2], stats[:, 0:2 * NTT:2]), ev)
            nc.vector.wait_ge(ev.s, ev.n)
            inc(nc.vector.tensor_sub(acol[:, 1:2 * NTT:2], stats[:, 1:2 * NTT:2], acol[:, 0:2 * NTT:2]), ev)
            nc.scalar.wait_ge(ev.s, ev.n)
            inc(nc.scalar.activation(acol[:, 1:2 * NTT:2], acol[:, 1:2 * NTT:2], AF.Ln,
                                     bias=eps_col[:, 0:1], scale=1.0), ev)
            nc.scalar.wait_ge(ev.s, ev.n)
            inc(nc.scalar.activation(acol[:, 1:2 * NTT:2], acol[:, 1:2 * NTT:2], AF.Exp,
                                     bias=0.0, scale=-0.5), ev)
            nc.vector.wait_ge(ev.s, ev.n)
            inc(nc.vector.tensor_mul(acol[:, 0:2 * NTT:2], stats[:, 0:2 * NTT:2], acol[:, 1:2 * NTT:2]), ev)
            nc.vector.wait_ge(ev.s, ev.n)
            inc(nc.vector.tensor_scalar_mul(acol[:, 0:2 * NTT:2], acol[:, 0:2 * NTT:2], -1.0), ev)
            nc.vector.wait_ge(ev.s, ev.n)
            inc(nc.vector.tensor_copy(acolb[:, :], acol[:, :]), ev)
            nc.tensor.wait_ge(ev.s, ev.n)
            for tt in range(NTT):
                for half, col in ((0, 2 * tt + 1), (1, 2 * tt)):
                    nc.tensor.wait_ge(rw.s, rw.n)
                    inc(nc.tensor.transpose(prow[:, :], acolb[:, col:col + 1], identb[:, :]), tr)
                    nc.vector.wait_ge(tr.s, tr.n)
                    inc(nc.vector.tensor_copy(abrow[0:1, half * T + tt * 128: half * T + (tt + 1) * 128],
                                              prow[0:1, :]), rw)
            nc.tensor.wait_ge(rw.s, rw.n)
            nb = 0
            for half in range(2):
                for w0 in range(0, T, TW):
                    pb = pbc0 if nb % 2 == 0 else pbc1
                    if nb >= 2:
                        nc.tensor.wait_ge(bc.s, nb - 1)
                    inc(nc.tensor.matmul(pb[:, :], onesrb[0:1, :], abrow[0:1, half * T + w0: half * T + w0 + TW],
                                         start=True, stop=True), mm)
                    nc.vector.wait_ge(mm.s, mm.n)
                    inc(nc.vector.tensor_copy(abB[:, half * T + w0: half * T + w0 + TW], pb[:, :]), bc)
                    nb += 1
            for dc in range(DC):
                par = dc % 2
                nc.vector.wait_ge(bc.s, bc.n)
                if dc >= 1:
                    nc.vector.wait_ge(nm_.s, 2 * dc)
                inc(nc.vector.tensor_mul(lt[:, :], X[:, dc * T:(dc + 1) * T],
                                         abB[:, 0:T]), nm_)
                nc.vector.wait_ge(nm_.s, nm_.n)
                inc(nc.vector.tensor_add(ln_buf[:, dc * T:(dc + 1) * T], lt[:, :],
                                         abB[:, T:2 * T]), nm_)
            nc.vector.wait_ge(nm_.s, nm_.n)
        barrier()

    def emit_proj_spill(wname, n_oc, spill, ln_buf):
        assert n_oc % 2 == 0
        npair = n_oc // 2
        with (
            nc.sbuf_tensor(tn("pw"), [128, 2 * DC * 128], BF16) as pw,
            nc.sbuf_tensor(tn("pev"), [128, 2 * T], BF16) as pev,
            nc.psum_tensor(tn("pq0"), [128, TW], F32) as pq0,
            nc.psum_tensor(tn("pq1"), [128, TW], F32) as pq1,
            nc.psum_tensor(tn("pq2"), [128, TW], F32) as pq2,
            nc.psum_tensor(tn("pq3"), [128, TW], F32) as pq3,
        ):
            pps = [[pq0, pq1], [pq2, pq3]]
            wl = [ctr("wl0"), ctr("wl1")]
            mm = ctr("pm")
            evd = [ctr("pe0"), ctr("pe1")]
            std = [ctr("ps0"), ctr("ps1")]
            wsrc = cv_full[wname]
            for ip in range(npair):
                for par in range(2):
                    nc.sync.wait_ge(mm.s, max(0, 2 * ip + par - 1))
                    nc.sync.dma_start(
                        pw[:, par * DC * 128:(par + 1) * DC * 128].rearrange("p (c o) -> p c o", o=128),
                        wsrc[:, ds((2 * ip + par) * 128, 128)].rearrange("(c p) o -> p c o", p=128),
                    ).then_inc(wl[par].s, 16)
                    nc.tensor.wait_ge(wl[par].s, 16 * (ip + 1))
                    nc.tensor.wait_ge(evd[par].s, NTW * ip)
                    for tw in range(NTW):
                        for dc in range(DC):
                            mmi = nc.tensor.matmul(pps[par][tw][:, :],
                                             pw[:, par * DC * 128 + dc * 128: par * DC * 128 + (dc + 1) * 128],
                                             ln_buf[:, dc * T + tw * TW: dc * T + tw * TW + TW],
                                             start=(dc == 0), stop=(dc == DC - 1))
                    mmi.then_inc(mm.s, 1)
                    nc.scalar.wait_ge(mm.s, 2 * ip + par + 1)
                    nc.scalar.wait_ge(std[par].s, 16 * ip)
                    for tw in range(NTW):
                        nc.scalar.activation(pev[:, par * T + tw * TW: par * T + (tw + 1) * TW],
                                             pps[par][tw][:, :], AF.Copy, bias=0.0,
                                             scale=1.0).then_inc(evd[par].s, 1)
                    nc.scalar.wait_ge(evd[par].s, NTW * (ip + 1))
                    nc.scalar.dma_start(spill[ds((2 * ip + par) * 128, 128), :],
                                        pev[:, par * T:(par + 1) * T]).then_inc(std[par].s, 16)
            nc.sync.wait_ge(std[0].s, 16 * npair)
            nc.sync.wait_ge(std[1].s, 16 * npair)
        barrier()

    def emit_wo_resid(wname, OT, g):
        with (
            nc.sbuf_tensor(tn("ww"), [128, 2 * DC * 128], BF16) as ww,
            nc.psum_tensor(tn("pw0"), [128, GQ], F32) as pw0,
            nc.psum_tensor(tn("pw1"), [128, GQ], F32) as pw1,
        ):
            wl = [ctr("owl0"), ctr("owl1")]
            mm = ctr("omm")
            ad = [ctr("oad0"), ctr("oad1")]
            wsrc = cv_full[wname]
            pps = [pw0, pw1]
            for ip in range(DC // 2):
                for par in range(2):
                    nc.sync.wait_ge(mm.s, max(0, 2 * ip + par - 1))
                    nc.sync.dma_start(
                        ww[:, par * DC * 128:(par + 1) * DC * 128].rearrange("p (c o) -> p c o", o=128),
                        wsrc[:, ds((2 * ip + par) * 128, 128)].rearrange("(c p) o -> p c o", p=128),
                    ).then_inc(wl[par].s, 16)
                    nc.tensor.wait_ge(wl[par].s, 16 * (ip + 1))
                    nc.tensor.wait_ge(ad[par].s, ip)
                    for dc in range(DC):
                        mmi = nc.tensor.matmul(pps[par][:, :],
                                         ww[:, par * DC * 128 + dc * 128: par * DC * 128 + (dc + 1) * 128],
                                         OT[:, dc * GQ:(dc + 1) * GQ],
                                         start=(dc == 0), stop=(dc == DC - 1))
                    mmi.then_inc(mm.s, 1)
                    nc.vector.wait_ge(mm.s, 2 * ip + par + 1)
                    nc.vector.tensor_add(X[:, ds((2 * ip + par) * T + g * GQ, GQ)],
                                         X[:, ds((2 * ip + par) * T + g * GQ, GQ)],
                                         pps[par][:, :]).then_inc(ad[par].s, 1)
            nc.sync.wait_ge(ad[0].s, DC // 2)
            nc.sync.wait_ge(ad[1].s, DC // 2)
        barrier()

    def emit_ffn():
        with nc.sbuf_tensor(tn("lnf"), [128, DC * T], BF16) as ln_buf:
            emit_ffn_body(ln_buf)

    def emit_ffn_body(ln_buf):
        emit_ln(ln_buf)
        with (
            nc.sbuf_tensor(tn("fw"), [128, 2 * DC * 128], BF16) as fw,
            nc.sbuf_tensor(tn("fsb"), [128, 4], F32) as fsb,
            nc.sbuf_tensor(tn("fst"), [128, 2 * T], F32) as fst,
            nc.psum_tensor(tn("pf0"), [128, TW], F32) as pf0,
            nc.psum_tensor(tn("pf1"), [128, TW], F32) as pf1,
            nc.psum_tensor(tn("pf2"), [128, TW], F32) as pf2,
            nc.psum_tensor(tn("pf3"), [128, TW], F32) as pf3,
        ):
            pps = [[pf0, pf1], [pf2, pf3]]
            wl = [ctr("fwl0"), ctr("fwl1")]
            mm = ctr("fmm")
            ep = [ctr("fep0"), ctr("fep1")]
            ad = [ctr("fad0"), ctr("fad1")]
            for ip in range(DC // 2):
                for par in range(2):
                    nc.sync.wait_ge(mm.s, max(0, 2 * ip + par - 1))
                    nc.sync.dma_start(
                        fw[:, par * DC * 128:(par + 1) * DC * 128].rearrange("p (c o) -> p c o", o=128),
                        cv_full["ft"][:, ds((2 * ip + par) * 128, 128)].rearrange("(c p) o -> p c o", p=128),
                    ).then_inc(wl[par].s, 16)
                    nc.sync.wait_ge(ep[par].s, NTW * ip)
                    nc.sync.dma_start(fsb[:, 2 * par:2 * par + 1], f_scale[ds(2 * ip + par, 1), :]).then_inc(wl[par].s, 16)
                    nc.sync.dma_start(fsb[:, 2 * par + 1:2 * par + 2], f_bias[ds(2 * ip + par, 1), :]).then_inc(wl[par].s, 16)
                    nc.tensor.wait_ge(wl[par].s, 48 * (ip + 1))
                    nc.tensor.wait_ge(ad[par].s, NTW * ip)
                    for tw in range(NTW):
                        for dc in range(DC):
                            mmi = nc.tensor.matmul(pps[par][tw][:, :],
                                             fw[:, par * DC * 128 + dc * 128: par * DC * 128 + (dc + 1) * 128],
                                             ln_buf[:, dc * T + tw * TW: dc * T + tw * TW + TW],
                                             start=(dc == 0), stop=(dc == DC - 1))
                    mmi.then_inc(mm.s, 1)
                    nc.vector.wait_ge(mm.s, 2 * ip + par + 1)
                    nc.vector.wait_ge(ad[par].s, NTW * ip)
                    nc.vector.wait_ge(wl[par].s, 48 * (ip + 1))
                    for tw in range(NTW):
                        nc.vector.tensor_scalar(fst[:, par * T + tw * TW: par * T + (tw + 1) * TW],
                                                pps[par][tw][:, :],
                                                fsb[:, 2 * par:2 * par + 1],
                                                fsb[:, 2 * par + 1:2 * par + 2],
                                                ALU.mult, ALU.add).then_inc(ep[par].s, 1)
                    nc.vector.wait_ge(ep[par].s, (ip + 1) * NTW)
                    for tw in range(NTW):
                        nc.vector.tensor_add(X[:, ds((2 * ip + par) * T + tw * TW, TW)],
                                             X[:, ds((2 * ip + par) * T + tw * TW, TW)],
                                             fst[:, par * T + tw * TW: par * T + (tw + 1) * TW]).then_inc(ad[par].s, 1)
            nc.sync.wait_ge(ad[0].s, NTW * (DC // 2))
            nc.sync.wait_ge(ad[1].s, NTW * (DC // 2))
        barrier()

    def emit_rotary(sp):
        with (
            nc.sbuf_tensor(tn("rin"), [64, T], BF16) as rin,
            nc.sbuf_tensor(tn("rtm"), [128, NJ * 64], BF16) as rtm,
            nc.sbuf_tensor(tn("rt2"), [128, NJ * 64], BF16) as rt2,
            nc.sbuf_tensor(tn("rt3"), [128, NJ * 64], BF16) as rt3,
            nc.sbuf_tensor(tn("rsin"), [128, NJ * 64], BF16) as rsin,
            nc.sbuf_tensor(tn("rcos"), [128, NJ * 64], BF16) as rcos,
            nc.sbuf_tensor(tn("rout"), [64, T], BF16) as rout,
            nc.psum_tensor(tn("prta"), [128, 64], BF16) as prta,
            nc.psum_tensor(tn("prtb"), [64, 128], BF16) as prtb,
        ):
            ld = ctr("rld"); tra = ctr("rtra"); eva = ctr("reva"); mth = ctr("rmth")
            trb = ctr("rtrb"); evb = ctr("revb"); sto = ctr("rsto")
            inc(nc.sync.dma_start(rsin[:, :], rot_sin_in.ap().rearrange("(t p) r -> p t r", p=128)), ld, 16)
            inc(nc.sync.dma_start(rcos[:, :], rot_cos_in.ap().rearrange("(t p) r -> p t r", p=128)), ld, 16)
            for h in range(HB):
                nc.sync.wait_ge(sto.s, sto.n)
                inc(nc.sync.dma_start(rin[:, :], sp[h * 256: h * 256 + 64, :]), ld, 16)
                nc.tensor.wait_ge(ld.s, ld.n)
                for tt in range(NJ):
                    nc.tensor.wait_ge(eva.s, eva.n)
                    inc(nc.tensor.transpose(prta[:, :], rin[:, tt * 128:(tt + 1) * 128], identb[0:64, 0:64]), tra)
                    nc.vector.wait_ge(tra.s, tra.n)
                    inc(nc.vector.tensor_copy(rtm[:, tt * 64:(tt + 1) * 64], prta[0:128, 0:64]), eva)
                nc.vector.wait_ge(eva.s, eva.n)
                r4a = rtm.ap().rearrange("p (u v) -> p u v", v=2)
                r4b = rt2.ap().rearrange("p (u v) -> p u v", v=2)
                inc(nc.vector.tensor_copy(r4b[:, :, 0:1], r4a[:, :, 1:2]), mth)
                nc.vector.wait_ge(mth.s, mth.n)
                inc(nc.vector.tensor_copy(r4b[:, :, 1:2], r4a[:, :, 0:1]), mth)
                nc.vector.wait_ge(mth.s, mth.n)
                inc(nc.vector.tensor_mul(rt3[:, :], rtm[:, :], rcos[:, :]), mth)
                nc.vector.wait_ge(mth.s, mth.n)
                inc(nc.vector.tensor_mul(rt2[:, :], rt2[:, :], rsin[:, :]), mth)
                nc.vector.wait_ge(mth.s, mth.n)
                inc(nc.vector.tensor_add(rtm[:, :], rt3[:, :], rt2[:, :]), mth)
                nc.tensor.wait_ge(mth.s, mth.n)
                for tt in range(NJ):
                    nc.tensor.wait_ge(evb.s, evb.n)
                    inc(nc.tensor.transpose(prtb[:, :], rtm[:, tt * 64:(tt + 1) * 64], identb[:, :]), trb)
                    nc.vector.wait_ge(trb.s, trb.n)
                    inc(nc.vector.tensor_copy(rout[:, tt * 128:(tt + 1) * 128], prtb[0:64, :]), evb)
                nc.scalar.wait_ge(evb.s, evb.n)
                inc(nc.scalar.dma_start(sp[h * 256: h * 256 + 64, :], rout[:, :]), sto, 16)
            nc.sync.wait_ge(sto.s, sto.n)
        barrier()

    def emit_to_token_major(src_ap, dst, nrc):
        with (
            nc.sbuf_tensor(tn("tmi"), [128, T], BF16) as tmi,
            nc.sbuf_tensor(tn("tmo"), [128, 128], BF16) as tmo,
            nc.psum_tensor(tn("ptm"), [128, 128], BF16) as ptm,
        ):
            ld = ctr("tml"); tr = ctr("tmt"); ev = ctr("tme"); st = ctr("tms")
            for dc in range(nrc):
                nc.sync.wait_ge(tr.s, tr.n)
                inc(nc.sync.dma_start(tmi[:, :], src_ap[dc * 128:(dc + 1) * 128, :]), ld, 16)
                nc.tensor.wait_ge(ld.s, ld.n)
                for tt in range(NJ):
                    nc.tensor.wait_ge(ev.s, ev.n)
                    inc(nc.tensor.transpose(ptm[:, :], tmi[:, tt * 128:(tt + 1) * 128], identb[:, :]), tr)
                    nc.vector.wait_ge(tr.s, tr.n)
                    nc.vector.wait_ge(st.s, st.n)
                    inc(nc.vector.tensor_copy(tmo[:, :], ptm[:, :]), ev)
                    nc.scalar.wait_ge(ev.s, ev.n)
                    inc(nc.scalar.dma_start(dst[tt * 128:(tt + 1) * 128, dc * 128:(dc + 1) * 128],
                                            tmo[:, :]), st, 16)
            nc.sync.wait_ge(st.s, st.n)
        barrier()

    def emit_gather(in_ap, out_ap):
        gs = ctr("gth")
        inc(nc.gpsimd.collective_compute("AllGather", ALU.bypass, replica_groups=RG,
                                         ins=[in_ap], outs=[out_ap]), gs)
        nc.gpsimd.wait_ge(gs.s, gs.n)
        barrier()

    def slot_block(s):
        return s % NC, s // NC

    def emit_attn_group(g, is_a, OT, KT=None, VO=None):
        HD = 128 if is_a else 256
        NDC = HD // 128
        NH = HA if is_a else HB
        scale = C.scale_a if is_a else C.scale_b
        PVW = HD + 1
        slots1 = [0]
        slots2 = list(range(8 * (SPG * g + SPG - 1) + 8))
        geo = {}
        for s in slots2:
            jmin = max(SPG * g, s // 8)
            w = (SPG * g + SPG - jmin) * 128
            qoff = (jmin - SPG * g) * 128
            bdry = (s // 8) >= SPG * g
            geo[s] = (jmin, w, qoff, bdry)
        b1 = [slots1[i:i + 2] for i in range(0, len(slots1), 2)]
        b2 = [slots2[i:i + 2] for i in range(0, len(slots2), 2)]
        from contextlib import ExitStack
        with ExitStack() as _es:
            def A(g_):
                return _es.enter_context(g_)
            qT = A(nc.sbuf_tensor(tn("qT"), [128, 2 * NDC * GQ], BF16))
            acc = A(nc.sbuf_tensor(tn("acc"), [128, GQ], BF16))
            accb = acc
            crowf = A(nc.sbuf_tensor(tn("crowf"), [1, GQ], F32))
            slqr = A(nc.sbuf_tensor(tn("slqr"), [1, GQ], F32))
            ccol = A(nc.sbuf_tensor(tn("ccol"), [128, SPG], BF16))
            maxB2 = A(nc.sbuf_tensor(tn("maxB2"), [128, 1024], F32))
            stg = A(nc.sbuf_tensor(tn("stg"), [128, 2 * 1024], F32))
            PT = A(nc.sbuf_tensor(tn("PT"), [128, 2 * 1024], BF16))
            alp = A(nc.sbuf_tensor(tn("alp"), [128, 2 * NSLOT], F32)) if is_a else None
            rcp = A(nc.sbuf_tensor(tn("rcp"), [128, SPG], F32))
            osb2 = A(nc.sbuf_tensor(tn("osb2"), [128, SPG * HD], BF16))
            ktb = A(nc.sbuf_tensor(tn("ktb"), [128, 8 * NDC * 128], BF16)) if not is_a else None
            vob = A(nc.sbuf_tensor(tn("vob"), [128, 8 * 260], BF16)) if not is_a else None
            psA = A(nc.psum_tensor(tn("psA"), [128, 1024], F32))
            psB = A(nc.psum_tensor(tn("psB"), [128, 1024], F32))
            pO_guards = [nc.psum_tensor(tn(f"pO{j2}"), [128, 512], F32) for j2 in range(SPG)]
            pOs = [gd.__enter__() for gd in pO_guards]
            pasbs = [pOs[k].ap()[:, 0:64].bitcast(BF16) for k in range(SPG)]
            pasb = pasbs[0]
            pbcv = pOs[0].ap()[:, 0:GQ]
            pss = [psA, psB]
            qld = [ctr("aql0"), ctr("aql1")]; ald = [ctr("aal0"), ctr("aal1")]; sld = [ctr("asl0"), ctr("asl1")]
            kvl = [ctr(f"akl{i}") for i in range(8)]
            m1 = ctr("am1"); v1 = ctr("av1"); ca = ctr("aca"); tr1 = ctr("atr")
            rd1 = ctr("ard"); rw1 = ctr("arw"); bcx = ctr("abc"); mbe = ctr("amb")
            m2 = ctr("am2"); v2 = ctr("av2"); ex = ctr("aex"); msk = ctr("ams")
            pv = ctr("apv"); rc = ctr("arc"); oe = ctr("aoe"); otr = ctr("aot"); ovv = ctr("aov")
            if not is_a:
                nc.vector.memset(vob[:, :], 1.0)
                vm0 = ctr("avm")
                inc(nc.vector.engine_nop(), vm0)
                nc.sync.wait_ge(vm0.s, 1)
            m2_end = []; ex_end = []; rw1_end = []
            kv_m_end = {}
            kv_pv_end = {}
            for ih in range(NH):
                hp = ih % 2
                if ih >= 2:
                    nc.sync.wait_ge(m2.s, m2_end[ih - 2])
                if is_a:
                    inc(nc.sync.dma_start(qT[:, hp * GQ:(hp + 1) * GQ],
                                          qkv_sp[ih * 128:(ih + 1) * 128, g * GQ:(g + 1) * GQ]), qld[hp], 16)
                    if ih >= 2:
                        nc.sync.wait_ge(ex.s, ex_end[ih - 2])
                    inc(nc.sync.dma_start(alp[:, hp * NSLOT:(hp + 1) * NSLOT],
                                          alibi_post_t[:, ih * NSLOT:(ih + 1) * NSLOT]), ald[hp], 16)
                    if ih >= 1:
                        nc.sync.wait_ge(rw1.s, rw1_end[ih - 1])
                    inc(nc.sync.dma_start(slqr[0:1, :],
                                          slq_in[0:1, ih * T + g * GQ: ih * T + g * GQ + GQ]), sld[hp], 16)
                else:
                    inc(nc.sync.dma_start(qT[:, hp * NDC * GQ: hp * NDC * GQ + GQ],
                                          qb_sp[ih * 256: ih * 256 + 128, g * GQ:(g + 1) * GQ]), qld[hp], 16)
                    inc(nc.sync.dma_start(qT[:, hp * NDC * GQ + GQ: hp * NDC * GQ + 2 * GQ],
                                          qb_sp[ih * 256 + 128: ih * 256 + 256, g * GQ:(g + 1) * GQ]), qld[hp], 16)
                qbase = hp * NDC * GQ if not is_a else hp * GQ
                qld_h = qld[hp].n
                ald_h = ald[hp].n
                sld_h = sld[hp].n
                s = 0
                if not is_a:
                    kb = 0
                    cc, jj = slot_block(s)
                    if kb in kv_m_end:
                        nc.sync.wait_ge(m1.s if kv_m_end[kb][0] == 1 else m2.s, kv_m_end[kb][1])
                    for half in range(NDC):
                        inc(nc.sync.dma_start(
                            ktb[:, kb * NDC * 128 + half * 128: kb * NDC * 128 + (half + 1) * 128],
                            ktb_g[cc * D + ih * 256 + half * 128: cc * D + ih * 256 + (half + 1) * 128,
                                  jj * 128:(jj + 1) * 128]), kvl[kb], 16)
                nc.tensor.wait_ge(qld[hp].s, qld_h)
                nc.tensor.wait_ge(oe.s, oe.n)
                nc.tensor.wait_ge(ovv.s, ovv.n)
                if is_a:
                    mmi = nc.tensor.matmul(pOs[0][:, 0:GQ], KT[:, 0:128],
                                           qT[:, qbase:qbase + GQ], start=True, stop=True)
                else:
                    kb = 0
                    nc.tensor.wait_ge(kvl[kb].s, kvl[kb].n)
                    for half in range(NDC):
                        mmi = nc.tensor.matmul(pOs[0][:, 0:GQ],
                                               ktb[:, kb * NDC * 128 + half * 128: kb * NDC * 128 + (half + 1) * 128],
                                               qT[:, qbase + half * GQ: qbase + half * GQ + GQ],
                                               start=(half == 0), stop=(half == NDC - 1))
                    kv_m_end[kb] = (1, m1.n + 1)
                inc(mmi, m1)
                nc.vector.wait_ge(m1.s, m1.n)
                nc.vector.wait_ge(tr1.s, tr1.n)
                inc(nc.vector.tensor_add(acc[:, :], pOs[0][:, 0:GQ], p1m[:, :]), v1)
                def emit_assembly():
                    nc.vector.wait_ge(v1.s, v1.n)
                    inc(nc.vector.engine_nop(), ca)
                    nc.tensor.wait_ge(ca.s, ca.n)
                    nc.tensor.wait_ge(rw1.s, rw1.n)
                    for j2 in range(SPG):
                        inc(nc.tensor.transpose(pasbs[j2][:, :], accb[:, j2 * 128:(j2 + 1) * 128], identb[:, :]), tr1)
                    nc.vector.wait_ge(tr1.s, tr1.n)
                    for j2 in range(SPG):
                        inc(nc.vector.reduce_max(ccol[:, j2:j2 + 1], pasbs[j2][:, :], AX.X), rd1)
                    crow = accb
                    nc.tensor.wait_ge(rd1.s, rd1.n)
                    for j2 in range(SPG):
                        inc(nc.tensor.transpose(pasbs[j2][0:1, :], ccol[:, j2:j2 + 1], identb[:, :]), tr1)
                    nc.vector.wait_ge(tr1.s, tr1.n)
                    for j2 in range(SPG):
                        inc(nc.vector.tensor_copy(crow[0:1, j2 * 128:(j2 + 1) * 128], pasbs[j2][0:1, :]), rw1)
                    nc.vector.wait_ge(rw1.s, rw1.n)
                    nc.vector.wait_ge(bcx.s, bcx.n)
                    if is_a:
                        nc.vector.wait_ge(sld[hp].s, sld_h)
                        inc(nc.vector.tensor_add(crowf[0:1, :], crow[0:1, :], slqr[0:1, :]), rw1)
                    else:
                        inc(nc.vector.tensor_copy(crowf[0:1, :], crow[0:1, :]), rw1)
                    rw1_end.append(rw1.n)
                    nc.tensor.wait_ge(rw1.s, rw1.n)
                    nc.tensor.wait_ge(mbe.s, mbe.n)
                    inc(nc.tensor.matmul(pbcv, onesr[0:1, :], crowf[0:1, :],
                                             start=True, stop=True), bcx)
                    nc.vector.wait_ge(bcx.s, bcx.n)
                    inc(nc.vector.tensor_copy(maxB2[:, 0:GQ], pbcv), mbe)
                    nc.vector.wait_ge(mbe.s, mbe.n)
                    inc(nc.vector.tensor_copy(maxB2[:, 512:512 + GQ], maxB2[:, 0:GQ]), mbe)
                NB = len(b2)

                def emit_mm2(bi):
                    par = bi % 2
                    ps = pss[par]
                    bb = b2[bi]
                    jmin, w, qoff, bdry = geo[bb[0]]
                    if not is_a:
                        for i, s in enumerate(bb):
                            kb = s % 8
                            cc, jj = slot_block(s)
                            if kb in kv_m_end:
                                nc.sync.wait_ge(m1.s if kv_m_end[kb][0] == 1 else m2.s, kv_m_end[kb][1])
                            for half in range(NDC):
                                inc(nc.sync.dma_start(
                                    ktb[:, kb * NDC * 128 + half * 128: kb * NDC * 128 + (half + 1) * 128],
                                    ktb_g[cc * D + ih * 256 + half * 128: cc * D + ih * 256 + (half + 1) * 128,
                                          jj * 128:(jj + 1) * 128]), kvl[kb], 16)
                            if kb in kv_pv_end:
                                nc.sync.wait_ge(pv.s, kv_pv_end[kb])
                            inc(nc.sync.dma_start(vob[:, kb * 260: kb * 260 + 256],
                                                  vb_g[cc * T + jj * 128: cc * T + jj * 128 + 128,
                                                       ih * 256:(ih + 1) * 256]), kvl[kb], 16)
                    if bi == 0 and ih > 0:
                        nc.tensor.wait_ge(v2.s, v2.n)
                    if bi >= 2:
                        nc.tensor.wait_ge(v2.s, v2_mark[bi - 2])
                    for i, s in enumerate(bb):
                        if is_a:
                            mmi = nc.tensor.matmul(ps[:, i * 512: i * 512 + w], KT[:, s * 128:(s + 1) * 128],
                                                   qT[:, qbase + qoff: qbase + qoff + w], start=True, stop=True)
                        else:
                            kb = s % 8
                            nc.tensor.wait_ge(kvl[kb].s, kvl[kb].n)
                            for half in range(NDC):
                                mmi = nc.tensor.matmul(ps[:, i * 512: i * 512 + w],
                                                       ktb[:, kb * NDC * 128 + half * 128: kb * NDC * 128 + (half + 1) * 128],
                                                       qT[:, qbase + half * GQ + qoff: qbase + half * GQ + qoff + w],
                                                       start=(half == 0), stop=(half == NDC - 1))
                            kv_m_end[kb] = (2, m2.n + 1)
                    inc(mmi, m2)

                def emit_subexp(bi):
                    par = bi % 2
                    ps = pss[par]
                    bb = b2[bi]
                    jmin, w, qoff, bdry = geo[bb[0]]
                    nb_ = len(bb)
                    nc.vector.wait_ge(m2.s, m2_mark[bi])
                    if bi == 0:
                        nc.vector.wait_ge(mbe.s, mbe.n)
                    if bi >= 2:
                        nc.vector.wait_ge(ex.s, ex_hist[bi - 2])
                    inc(nc.vector.tensor_tensor(
                        stg.ap().rearrange("p (b s q) -> p b s q", b=2, s=2)[:, par, 0:nb_, qoff:qoff + w],
                        ps.ap().rearrange("p (s q) -> p s q", s=2)[:, 0:nb_, 0:w],
                        maxB2.ap().rearrange("p (s q) -> p s q", s=2)[:, 0:nb_, qoff:qoff + w],
                        ALU.subtract), v2)
                    v2_mark[bi] = v2.n
                    nc.scalar.wait_ge(v2.s, v2.n)
                    if is_a and bi == 0:
                        nc.scalar.wait_ge(ald[hp].s, ald_h)
                    if bi >= 2:
                        nc.scalar.wait_ge(pv.s, pv_hist[bi - 2])
                    for i, s in enumerate(bb):
                        if is_a:
                            inc(nc.scalar.activation(PT[:, par * 1024 + i * 512 + qoff: par * 1024 + i * 512 + qoff + w],
                                                     stg[:, par * 1024 + i * 512 + qoff: par * 1024 + i * 512 + qoff + w],
                                                     AF.Exp, bias=alp[:, hp * NSLOT + s: hp * NSLOT + s + 1],
                                                     scale=scale), ex)
                        else:
                            inc(nc.scalar.activation(PT[:, par * 1024 + i * 512 + qoff: par * 1024 + i * 512 + qoff + w],
                                                     stg[:, par * 1024 + i * 512 + qoff: par * 1024 + i * 512 + qoff + w],
                                                     AF.Exp, bias=0.0, scale=scale), ex)
                    ex_hist[bi] = ex.n

                def emit_masks(bi):
                    par = bi % 2
                    bb = b2[bi]
                    for i, s in enumerate(bb):
                        if geo[s][3]:
                            nc.vector.wait_ge(ex.s, ex_hist[bi])
                            _, w, qoff, _bd = geo[s]
                            dp_ = s % 8
                            inc(nc.vector.tensor_tensor(
                                PT.ap().bitcast(U16)[:, par * 1024 + i * 512 + qoff: par * 1024 + i * 512 + qoff + 128],
                                PT.ap().bitcast(U16)[:, par * 1024 + i * 512 + qoff: par * 1024 + i * 512 + qoff + 128],
                                amask[:, dp_ * 128:(dp_ + 1) * 128], ALU.bitwise_and), msk)
                    msk_hist[bi] = msk.n

                def emit_pv2(bi):
                    par = bi % 2
                    bb = b2[bi]
                    nc.tensor.wait_ge(ex.s, ex_hist[bi])
                    nc.tensor.wait_ge(msk.s, msk_hist[bi])
                    if bb[0] == 0:
                        nc.tensor.wait_ge(mbe.s, mbe.n)
                    for i_, s_ in enumerate(bb):
                        jm_, w_, qoff_, _b = geo[s_]
                        for j_ in range(jm_, SPG * g + SPG):
                            j2_ = j_ - SPG * g
                            rhs_ = vob[:, (s_ % 8) * 260: (s_ % 8) * 260 + PVW] if not is_a else VO[:, s_ * 129: s_ * 129 + PVW]
                            mmi_ = nc.tensor.matmul(pOs[j2_][:, 0:PVW],
                                                    PT[:, par * 1024 + i_ * 512 + qoff_ + (j_ - jm_) * 128:
                                                       par * 1024 + i_ * 512 + qoff_ + (j_ - jm_ + 1) * 128],
                                                    rhs_, start=(s_ == 0), stop=(s_ == 8 * j_ + 7))
                            inc(mmi_, pv)
                        if not is_a:
                            kv_pv_end[s_ % 8] = pv.n
                    pv_hist[bi] = pv.n

                m2_mark = [0] * NB
                msk_hist = [0] * NB
                v2_mark = [0] * NB
                ex_hist = [0] * NB
                pv_hist = [0] * NB
                for k in range(NB + 3):
                    if k < NB:
                        emit_mm2(k)
                        m2_mark[k] = m2.n
                    if k == 1:
                        emit_assembly()
                    if 0 <= k - 3 < NB:
                        emit_pv2(k - 3)
                    if 0 <= k - 1 < NB:
                        emit_subexp(k - 1)
                    if 0 <= k - 2 < NB:
                        emit_masks(k - 2)
                m2_end.append(m2.n)
                ex_end.append(ex.n)
                nc.vector.wait_ge(pv.s, pv.n)
                for j2 in range(SPG):
                    inc(nc.vector.reciprocal(rcp[:, j2:j2 + 1], pOs[j2][:, HD:HD + 1]), rc)
                nc.scalar.wait_ge(rc.s, rc.n)
                nc.scalar.wait_ge(otr.s, otr.n)
                for j2 in range(SPG):
                    inc(nc.scalar.activation(osb2[:, j2 * HD:(j2 + 1) * HD], pOs[j2][:, 0:HD], AF.Copy,
                                             bias=0.0, scale=rcp[:, j2:j2 + 1]), oe)
                nc.tensor.wait_ge(oe.s, oe.n)
                tiles = [(j2, half) for j2 in range(SPG) for half in range(NDC)]
                for r0 in range(0, len(tiles), SPG):
                    nc.tensor.wait_ge(ovv.s, ovv.n)
                    for bk, (j2, half) in enumerate(tiles[r0:r0 + SPG]):
                        inc(nc.tensor.transpose(pasbs[bk][:, :],
                                                osb2[:, j2 * HD + half * 128: j2 * HD + (half + 1) * 128],
                                                identb[:, :]), otr)
                    nc.vector.wait_ge(otr.s, otr.n)
                    for bk, (j2, half) in enumerate(tiles[r0:r0 + SPG]):
                        dcx = ih * NDC + half
                        inc(nc.vector.tensor_copy(OT[:, dcx * GQ + j2 * 128: dcx * GQ + (j2 + 1) * 128],
                                                  pasbs[bk][:, :]), ovv)
            nc.sync.wait_ge(ovv.s, ovv.n)
            for gd in reversed(pO_guards):
                gd.__exit__(None, None, None)
        barrier()

    with nc.sbuf_tensor(tn("ln1"), [128, DC * T], BF16) as ln_buf:
        emit_ln(ln_buf)
        emit_proj_spill("qkv", OA, qkv_sp, ln_buf)
    emit_to_token_major(qkv_sp.ap()[(DC + 1) * 128:(DC + 2) * 128, :], va_loc.ap(), 1)
    emit_gather(qkv_sp.ap()[DC * 128:(DC + 1) * 128, :], kta_g.ap())
    emit_gather(va_loc.ap(), va_g.ap())
    with (
        nc.sbuf_tensor(tn("KTa"), [128, NSLOT * 128], BF16) as KTa,
        nc.sbuf_tensor(tn("VOa"), [128, NSLOT * 130], BF16) as VOa,
        nc.sbuf_tensor(tn("OTa"), [128, DC * GQ], BF16) as OTa,
    ):
        kvc = ctr("kvc")
        vmc = ctr("vmc")
        inc(nc.vector.memset(VOa[:, :], 1.0), vmc)
        nc.sync.wait_ge(vmc.s, 1)
        for s in range(NSLOT):
            cc, jj = slot_block(s)
            inc(nc.sync.dma_start(KTa[:, s * 128:(s + 1) * 128],
                                  kta_g[cc * 128:(cc + 1) * 128, jj * 128:(jj + 1) * 128]), kvc, 16)
            inc(nc.sync.dma_start(VOa[:, s * 129: s * 129 + 128],
                                  va_g[cc * T + jj * 128: cc * T + jj * 128 + 128, :]), kvc, 16)
        nc.sync.wait_ge(kvc.s, kvc.n)
        barrier()
        for g in range(NG):
            emit_attn_group(g, True, OTa, KT=KTa, VO=VOa)
            emit_wo_resid("oa", OTa, g)

    if n_sub >= 2:
        emit_ffn()

    if n_sub >= 3:
        with nc.sbuf_tensor(tn("ln3"), [128, DC * T], BF16) as ln_buf:
            emit_ln(ln_buf)
            emit_proj_spill("q", DC, qb_sp, ln_buf)
            emit_proj_spill("k", DC, ktb_loc, ln_buf)
            emit_proj_spill("v", DC, vtb_sp, ln_buf)
        emit_rotary(qb_sp)
        emit_rotary(ktb_loc)
        emit_to_token_major(vtb_sp.ap(), vb_loc.ap(), DC)
        emit_gather(ktb_loc.ap(), ktb_g.ap())
        emit_gather(vb_loc.ap(), vb_g.ap())
        with nc.sbuf_tensor(tn("OTb"), [128, DC * GQ], BF16) as OTb:
            for g in range(NG):
                emit_attn_group(g, False, OTb)
                emit_wo_resid("ob", OTb, g)

    if n_sub >= 4:
        emit_ffn()

    so = ctr("so")
    for dc in range(DC):
        inc(nc.scalar.dma_start(outT[dc * 128:(dc + 1) * 128, :], X[:, dc * T:(dc + 1) * T]), so, 16)
    nc.scalar.wait_ge(so.s, so.n)
    for p in reversed(persist):
        p.__exit__(None, None, None)
    return nc


def kernel(L0_attn_a_hidden_states, L0_attn_a_Wqkv, L0_attn_a_Wo,
           L0_attn_b_Wq, L0_attn_b_Wk, L0_attn_b_Wv, L0_attn_b_Wo,
           L0_ffn_int8_weight, L0_ffn_scale, L0_ffn_bias, _cfg=None, _n_sub=4):
    cfg = _cfg or Cfg()
    NCn, D, T, NJ, DC = cfg.NC, cfg.D, cfg.T, cfg.NJ, cfg.DC
    x = np.asarray(L0_attn_a_hidden_states).reshape(cfg.SEQ, D)
    wqkv = np.ascontiguousarray(np.asarray(L0_attn_a_Wqkv, dtype=np.float32))
    woa = np.ascontiguousarray(np.asarray(L0_attn_a_Wo, dtype=np.float32))
    wq = np.ascontiguousarray(np.asarray(L0_attn_b_Wq, dtype=np.float32))
    wk = np.ascontiguousarray(np.asarray(L0_attn_b_Wk, dtype=np.float32))
    wv = np.ascontiguousarray(np.asarray(L0_attn_b_Wv, dtype=np.float32))
    wob = np.ascontiguousarray(np.asarray(L0_attn_b_Wo, dtype=np.float32))
    w8 = np.ascontiguousarray(np.asarray(L0_ffn_int8_weight))
    fs = np.asarray(L0_ffn_scale, dtype=np.float32).reshape(D)
    fb = np.asarray(L0_ffn_bias, dtype=np.float32).reshape(D)
    fs_t = np.ascontiguousarray(fs.reshape(DC, 128))
    fb_t = np.ascontiguousarray(fb.reshape(DC, 128))

    nc = build_program(cfg, n_sub=_n_sub)
    in_maps = []
    RSL = D // NCn
    for c in range(NCn):
        tks = np.concatenate([np.arange(128) + (8 * j + c) * 128 for j in range(NJ)])
        xc = np.ascontiguousarray(x[tks, :].T.astype(np.float32))
        t = host_tables(cfg, c)
        rsl = slice(c * RSL, (c + 1) * RSL)
        in_maps.append({
            "xT": xc, "w_qkv": np.ascontiguousarray(wqkv[rsl]), "w_oa": np.ascontiguousarray(woa[rsl]),
            "w_q": np.ascontiguousarray(wq[rsl]), "w_k": np.ascontiguousarray(wk[rsl]),
            "w_v": np.ascontiguousarray(wv[rsl]),
            "w_ob": np.ascontiguousarray(wob[rsl]), "w_f8": np.ascontiguousarray(w8[:, rsl]),
            "f_scale": fs_t, "f_bias": fb_t,
            "alibi_raw": t["alibi_raw"], "alibi_post": t["alibi_post"],
            "p1mask": t["p1mask"], "andmask": t["andmask"],
            "rot_sin": t["rot_sin"], "rot_cos": t["rot_cos"],
            "ident_bf": t["ident_bf"],
            "ones_f32": t["ones_f32"], "ones_row": t["ones_row"], "ones_row_bf": t["ones_row_bf"], "slq": t["slq"],
        })
    res = run_bass_kernel_spmd(nc, in_maps, core_ids=list(range(NCn)))
    out = np.zeros((cfg.SEQ, D), dtype=np.float32)
    for c in range(NCn):
        tks = np.concatenate([np.arange(128) + (8 * j + c) * 128 for j in range(NJ)])
        out[tks, :] = np.asarray(res.results[c]["outT"]).T
    return out.reshape(1, cfg.SEQ, D)


# version 17
# speedup vs baseline: 1.0421x; 1.0421x over previous
import math
import numpy as np
import ml_dtypes
import concourse.bass as bass
import concourse.mybir as mybir
from concourse.bass_utils import run_bass_kernel_spmd

F32 = mybir.dt.float32
BF16 = mybir.dt.bfloat16
U16 = mybir.dt.uint16
U32 = mybir.dt.uint32
I8 = mybir.dt.int8
AF = mybir.ActivationFunctionType
ALU = mybir.AluOpType
AX = mybir.AxisListType
ds = bass.ds

NEG = -1.0e30


class Cfg:
    def __init__(self, D=4096, SEQ=8192):
        self.NC = 8
        self.D = D
        self.SEQ = SEQ
        self.DC = D // 128
        self.T = SEQ // self.NC
        self.NJ = self.T // 128
        self.SPG = min(4, self.NJ)
        self.NG = self.NJ // self.SPG
        self.GQ = self.SPG * 128
        self.HA = D // 128
        self.HB = D // 256
        self.OA = self.DC + 2
        self.NSLOT = SEQ // 128
        self.scale_a = 1.0 / math.sqrt(128.0)
        self.scale_b = 1.0 / 16.0
        self.ROT = 64


def alibi_slopes(n):
    cp2 = 2 ** int(math.floor(math.log2(n)))
    base = 2.0 ** (-(2.0 ** (-(math.log2(cp2) - 3))))
    slopes = base ** np.arange(1, cp2 + 1, dtype=np.float64)
    if cp2 != n:
        extra_base = 2.0 ** (-(2.0 ** (-(math.log2(2 * cp2) - 3))))
        rem = min(cp2, n - cp2)
        extra = extra_base ** np.arange(1, 1 + 2 * rem, 2, dtype=np.float64)
        slopes = np.concatenate([slopes, extra])
    return slopes.astype(np.float64)


def host_tables(cfg, core):
    t = {}
    sl = alibi_slopes(cfg.HA)
    k = (np.arange(cfg.NSLOT)[None, :] * 128 + np.arange(128)[:, None]).astype(np.float64)
    ali = k[:, None, :] * sl[None, :, None]
    ali = ali.reshape(128, cfg.HA * cfg.NSLOT)
    t["alibi_post"] = ali.astype(np.float32)
    t["alibi_raw"] = (ali / cfg.scale_a).astype(np.float32)
    m1 = np.zeros((128, cfg.GQ), np.float32)
    if core == 0:
        p = np.arange(128)[:, None]
        q = np.arange(128)[None, :]
        m1[:, 0:128] = np.where(p > q, np.float32(NEG), np.float32(0.0))
    t["p1mask"] = m1.astype(ml_dtypes.bfloat16)
    am = np.zeros((128, 8 * 128), np.uint16)
    p = np.arange(128)[:, None]
    q = np.arange(128)[None, :]
    for dpos in range(8):
        blk = am[:, dpos * 128:(dpos + 1) * 128]
        if dpos < core:
            blk[:] = 0xFFFF
        elif dpos == core:
            blk[:] = np.where(p <= q, 0xFFFF, 0).astype(np.uint16)
    t["andmask"] = am
    pos = np.zeros((cfg.T,), np.int64)
    for j in range(cfg.NJ):
        pos[j * 128:(j + 1) * 128] = (8 * j + core) * 128 + np.arange(128)
    inv = 1.0 / (10000.0 ** (np.arange(0, cfg.ROT, 2, dtype=np.float64) / cfg.ROT))
    ang = pos[:, None].astype(np.float64) * inv[None, :]
    t["rot_sin"] = np.repeat(np.sin(ang), 2, axis=1).astype(ml_dtypes.bfloat16)
    t["rot_cos"] = np.repeat(np.cos(ang), 2, axis=1).astype(ml_dtypes.bfloat16)
    t["ident_bf"] = np.eye(128, dtype=ml_dtypes.bfloat16)
    t["ident_f32"] = np.eye(128, dtype=np.float32)
    t["ones_f32"] = np.ones((128, 1), np.float32)
    t["ones_row"] = np.ones((1, 128), np.float32)
    t["ones_row_bf"] = np.ones((1, 128), ml_dtypes.bfloat16)
    slq = (sl[:, None] * pos[None, :].astype(np.float64) / cfg.scale_a).reshape(1, cfg.HA * cfg.T)
    t["slq"] = slq.astype(np.float32)
    return t


_CTR_ID = [0]
_TN_ID = [0]


def tn(name):
    _TN_ID[0] += 1
    return f"{name}_{_TN_ID[0]}"


class Ctr:
    def __init__(self, nc, name):
        _CTR_ID[0] += 1
        self.s = nc.alloc_semaphore(name=f"{name}_{_CTR_ID[0]}")
        self.n = 0


def inc(instr, c, k=1):
    instr.then_inc(c.s, k)
    c.n += k
    return instr


DBG = False
DBG_SLOT = 0


def build_program(cfg, n_sub=4):
    nc = bass.Bass("TRN2", target_bir_lowering=False, dynamic_dma_scratch_size=32)
    C = cfg
    NC, D, DC, T, NJ, SPG, NG, GQ = C.NC, C.D, C.DC, C.T, C.NJ, C.SPG, C.NG, C.GQ
    HA, HB, OA, NSLOT = C.HA, C.HB, C.OA, C.NSLOT
    TW = min(512, T)
    NTW = T // TW
    NTT = T // 128

    xT = nc.dram_tensor("xT", [D, T], F32, kind="ExternalInput")
    RSL = D // NC
    w_qkv = nc.dram_tensor("w_qkv", [RSL, OA * 128], F32, kind="ExternalInput")
    w_oa = nc.dram_tensor("w_oa", [RSL, D], F32, kind="ExternalInput")
    w_q = nc.dram_tensor("w_q", [RSL, D], F32, kind="ExternalInput")
    w_k = nc.dram_tensor("w_k", [RSL, D], F32, kind="ExternalInput")
    w_v = nc.dram_tensor("w_v", [RSL, D], F32, kind="ExternalInput")
    w_ob = nc.dram_tensor("w_ob", [RSL, D], F32, kind="ExternalInput")
    w_f8 = nc.dram_tensor("w_f8", [D, RSL], I8, kind="ExternalInput")
    f_scale = nc.dram_tensor("f_scale", [DC, 128], F32, kind="ExternalInput")
    f_bias = nc.dram_tensor("f_bias", [DC, 128], F32, kind="ExternalInput")
    alibi_raw_t = nc.dram_tensor("alibi_raw", [128, HA * NSLOT], F32, kind="ExternalInput")
    alibi_post_t = nc.dram_tensor("alibi_post", [128, HA * NSLOT], F32, kind="ExternalInput")
    p1mask_in = nc.dram_tensor("p1mask", [128, GQ], BF16, kind="ExternalInput")
    andmask_in = nc.dram_tensor("andmask", [128, 8 * 128], U16, kind="ExternalInput")
    rot_sin_in = nc.dram_tensor("rot_sin", [T, 64], BF16, kind="ExternalInput")
    rot_cos_in = nc.dram_tensor("rot_cos", [T, 64], BF16, kind="ExternalInput")
    ident_bf_in = nc.dram_tensor("ident_bf", [128, 128], BF16, kind="ExternalInput")
    ones_f32_in = nc.dram_tensor("ones_f32", [128, 1], F32, kind="ExternalInput")
    ones_row_in = nc.dram_tensor("ones_row", [1, 128], F32, kind="ExternalInput")
    ones_row_bf_in = nc.dram_tensor("ones_row_bf", [1, 128], BF16, kind="ExternalInput")
    slq_in = nc.dram_tensor("slq", [1, HA * T], F32, kind="ExternalInput")
    outT = nc.dram_tensor("outT", [D, T], F32, kind="ExternalOutput")
    dbg_acc = nc.dram_tensor("dbg_acc", [128, GQ], F32)
    dbg_maxB = nc.dram_tensor("dbg_maxB", [128, GQ], F32)
    dbg_stg = nc.dram_tensor("dbg_stg", [128, GQ], F32)
    dbg_PT = nc.dram_tensor("dbg_PT", [128, GQ], BF16)
    dbg_pO = nc.dram_tensor("dbg_pO", [128, 260], F32)

    cv_sl, cv_full = {}, {}
    for nm, ocols in (("qkv", OA * 128), ("oa", D), ("q", D), ("k", D), ("v", D), ("ob", D), ("ft", D)):
        cv_sl[nm] = nc.dram_tensor(f"cv_sl_{nm}", [D // NC, ocols], BF16)
        cv_full[nm] = nc.dram_tensor(f"cv_{nm}", [D, ocols], BF16)
    qkv_sp = nc.dram_tensor("qkv_sp", [OA * 128, T], BF16)
    va_loc = nc.dram_tensor("va_loc", [T, 128], BF16)
    kta_g = nc.dram_tensor("kta_g", [NC * 128, T], BF16)
    va_g = nc.dram_tensor("va_g", [NC * T, 128], BF16)
    qb_sp = nc.dram_tensor("qb_sp", [D, T], BF16)
    ktb_loc = nc.dram_tensor("ktb_loc", [D, T], BF16)
    vtb_sp = nc.dram_tensor("vtb_sp", [D, T], BF16)
    vb_loc = nc.dram_tensor("vb_loc", [T, D], BF16)
    ktb_g = nc.dram_tensor("ktb_g", [NC * D, T], BF16)
    vb_g = nc.dram_tensor("vb_g", [NC * T, D], BF16)
    RG = [list(range(NC))]
    gsems = {}

    persist = []
    def sbp(name, shape, dtype):
        g = nc.sbuf_tensor(name, shape, dtype)
        h = g.__enter__()
        persist.append(g)
        return h

    X = sbp("X", [128, DC * T], F32)
    identb = sbp("identb", [128, 128], BF16)
    onesf = sbp("onesf", [128, 1], F32)
    onesr = sbp("onesr", [1, 128], F32)
    onesrb = sbp("onesrb", [1, 128], BF16)
    amask = sbp("amask", [128, 8 * 128], U16)
    p1m = sbp("p1m", [128, GQ], BF16)
    eps_col = sbp("eps_col", [128, 1], F32)

    allh = [nc.alloc_semaphore(name=f"pool{i}") for i in range(nc.free_len())]
    allh.sort(key=lambda h: h.num)
    NRES = 14
    reserved = allh[:NRES]
    pool = allh[NRES:]
    pool_ids = sorted(h.num for h in pool)
    assert pool_ids == list(range(pool_ids[0], pool_ids[0] + len(pool))), "pool ids not contiguous"
    pool_range = range(pool_ids[0], pool_ids[0] + len(pool))
    free_sems = list(pool)
    used_sems = []

    def rctr(name):
        h = reserved.pop()
        c = Ctr.__new__(Ctr)
        c.s = h
        c.n = 0
        return c

    def ctr(name):
        h = free_sems.pop()
        used_sems.append(h)
        c = Ctr.__new__(Ctr)
        c.s = h
        c.n = 0
        return c

    def barrier():
        nc.all_engine_barrier()
        if used_sems:
            nc.gpsimd.dma_reset(pool_range)
            nc.gpsimd.sem_clear(pool_range)
            free_sems.extend(used_sems)
            used_sems.clear()
            nc.all_engine_barrier()

    s0 = ctr("s0")
    for dc in range(DC):
        inc(nc.sync.dma_start(X[:, dc * T:(dc + 1) * T], xT[dc * 128:(dc + 1) * 128, :]), s0, 16)
    for dst, src in ((identb, ident_bf_in), (onesf, ones_f32_in),
                     (onesr, ones_row_in), (onesrb, ones_row_bf_in),
                     (amask, andmask_in), (p1m, p1mask_in)):
        inc(nc.sync.dma_start(dst[:, :], src[:, :]), s0, 16)
    nc.sync.wait_ge(s0.s, s0.n)
    nc.vector.memset(eps_col[:, :], 1e-5)
    barrier()

    RS = D // NC
    NRT = RS // 128
    with (
        nc.sbuf_tensor("cwi", [128, 2 * (OA * 128)], F32) as cwi,
        nc.sbuf_tensor("cwo", [128, 2 * (OA * 128)], BF16) as cwo,
        nc.sbuf_tensor("ci8", [128, 2 * RS], I8) as ci8,
        nc.sbuf_tensor("cb16", [128, 2 * RS], BF16) as cb16,
        nc.sbuf_tensor("ctr8", [128, 2 * RS], BF16) as ctr8,
        nc.psum_tensor("ptr0", [128, 128], BF16) as ptr0,
        nc.psum_tensor("ptr1", [128, 128], BF16) as ptr1,
    ):
        ld = [ctr("cvl0"), ctr("cvl1")]
        cvs = [ctr("cvc0"), ctr("cvc1")]
        st = [ctr("cvs0"), ctr("cvs1")]
        for wi, (nm, src, ocols) in enumerate((("qkv", w_qkv, OA * 128), ("oa", w_oa, D),
                                               ("q", w_q, D), ("k", w_k, D), ("v", w_v, D), ("ob", w_ob, D))):
            for r in range(NRT):
                par = (wi * NRT + r) % 2
                nc.sync.wait_ge(st[par].s, st[par].n)
                inc(nc.sync.dma_start(cwi[:, par * (OA * 128): par * (OA * 128) + ocols],
                                      src[r * 128:(r + 1) * 128, :]), ld[par], 16)
                nc.vector.wait_ge(ld[par].s, ld[par].n)
                inc(nc.vector.tensor_copy(cwo[:, par * (OA * 128): par * (OA * 128) + ocols],
                                          cwi[:, par * (OA * 128): par * (OA * 128) + ocols]), cvs[par])
                nc.scalar.wait_ge(cvs[par].s, cvs[par].n)
                inc(nc.scalar.dma_start(cv_sl[nm][r * 128:(r + 1) * 128, :],
                                        cwo[:, par * (OA * 128): par * (OA * 128) + ocols]), st[par], 16)
        l8 = [ctr("l80"), ctr("l81")]
        c8 = [ctr("c80"), ctr("c81")]
        t8 = [ctr("t80"), ctr("t81")]
        e8 = [ctr("e80"), ctr("e81")]
        sp8 = [ctr("sp80"), ctr("sp81")]
        for oc in range(DC):
            par = oc % 2
            nc.sync.wait_ge(sp8[par].s, sp8[par].n)
            inc(nc.sync.dma_start(ci8[:, par * RS: par * RS + RS],
                                  w_f8[oc * 128:(oc + 1) * 128, :]), l8[par], 16)
            nc.vector.wait_ge(l8[par].s, l8[par].n)
            nc.vector.wait_ge(t8[par].s, t8[par].n)
            inc(nc.vector.tensor_copy(cb16[:, par * RS: par * RS + RS],
                                      ci8[:, par * RS: par * RS + RS]), c8[par])
            nc.tensor.wait_ge(c8[par].s, c8[par].n)
            ptr = ptr0 if par == 0 else ptr1
            for tt in range(NRT):
                nc.tensor.wait_ge(e8[par].s, e8[par].n)
                inc(nc.tensor.transpose(ptr[:, :], cb16[:, par * RS + tt * 128: par * RS + (tt + 1) * 128],
                                        identb[:, :]), t8[par])
                nc.vector.wait_ge(t8[par].s, t8[par].n)
                inc(nc.vector.tensor_copy(ctr8[:, par * RS + tt * 128: par * RS + (tt + 1) * 128],
                                          ptr[:, :]), e8[par])
            nc.scalar.wait_ge(e8[par].s, e8[par].n)
            inc(nc.scalar.dma_start(
                cv_sl["ft"].ap().rearrange("(t p) o -> p t o", p=128)[:, :, oc * 128:(oc + 1) * 128],
                ctr8[:, par * RS: par * RS + RS].rearrange("p (t q) -> p t q", q=128),
            ), sp8[par], 16)
        for par in range(2):
            nc.gpsimd.wait_ge(st[par].s, st[par].n)
            nc.gpsimd.wait_ge(sp8[par].s, sp8[par].n)
        for nm in ("qkv", "oa", "ft", "q", "k", "v", "ob"):
            gs = rctr("g_" + nm)
            inc(nc.gpsimd.collective_compute("AllGather", ALU.bypass, replica_groups=RG,
                                             ins=[cv_sl[nm].ap()], outs=[cv_full[nm].ap()]), gs)
            gsems[nm] = gs
    barrier()

    def emit_ln(ln_buf):
        with (
            nc.sbuf_tensor(tn("x2"), [128, 2 * T], F32) as x2,
            nc.sbuf_tensor(tn("stats"), [128, 2 * NTT], F32) as stats,
            nc.sbuf_tensor(tn("abrow"), [1, 2 * T], BF16) as abrow,
            nc.sbuf_tensor(tn("abB"), [128, 2 * T], BF16) as abB,
            nc.sbuf_tensor(tn("acol"), [128, 2 * NTT], F32) as acol,
            nc.sbuf_tensor(tn("acolb"), [128, 2 * NTT], BF16) as acolb,
            nc.sbuf_tensor(tn("lt"), [128, T], F32) as lt,
            nc.psum_tensor(tn("pstt"), [128, 2 * NTT * DC], F32) as pstt,
            nc.psum_tensor(tn("prow"), [1, 128], BF16) as prow,
            nc.psum_tensor(tn("pbc0"), [128, TW], F32) as pbc0,
            nc.psum_tensor(tn("pbc1"), [128, TW], F32) as pbc1,
        ):
            sq = ctr("lnsq"); mm = ctr("lnmm"); ev = ctr("lnev"); tr = ctr("lntr")
            rw = ctr("lnrw"); bc = ctr("lnbc"); nm_ = ctr("lnnm")
            for dc in range(DC):
                par = dc % 2
                if dc >= 2:
                    nc.scalar.wait_ge(mm.s, (dc - 1) * NTT)
                inc(nc.scalar.activation(x2[:, par * T:(par + 1) * T], X[:, dc * T:(dc + 1) * T],
                                         AF.Square), sq)
                nc.tensor.wait_ge(sq.s, sq.n)
                for tt in range(NTT):
                    nc.tensor.matmul(pstt[:, (2 * tt) * DC + dc:(2 * tt) * DC + dc + 1],
                                     X[:, dc * T + tt * 128: dc * T + (tt + 1) * 128],
                                     onesf[:, :], start=True, stop=True)
                    inc(nc.tensor.matmul(pstt[:, (2 * tt + 1) * DC + dc:(2 * tt + 1) * DC + dc + 1],
                                         x2[:, par * T + tt * 128: par * T + (tt + 1) * 128],
                                         onesf[:, :], start=True, stop=True), mm)
            nc.vector.wait_ge(mm.s, mm.n)
            inc(nc.vector.reduce_sum(stats[:, :], pstt.ap().rearrange("p (s d) -> p s d", d=DC), AX.X), ev)
            nc.vector.wait_ge(ev.s, ev.n)
            inc(nc.vector.tensor_scalar_mul(stats[:, :], stats[:, :], 1.0 / D), ev)
            nc.vector.wait_ge(ev.s, ev.n)
            inc(nc.vector.tensor_mul(acol[:, 0:2 * NTT:2], stats[:, 0:2 * NTT:2], stats[:, 0:2 * NTT:2]), ev)
            nc.vector.wait_ge(ev.s, ev.n)
            inc(nc.vector.tensor_sub(acol[:, 1:2 * NTT:2], stats[:, 1:2 * NTT:2], acol[:, 0:2 * NTT:2]), ev)
            nc.scalar.wait_ge(ev.s, ev.n)
            inc(nc.scalar.activation(acol[:, 1:2 * NTT:2], acol[:, 1:2 * NTT:2], AF.Ln,
                                     bias=eps_col[:, 0:1], scale=1.0), ev)
            nc.scalar.wait_ge(ev.s, ev.n)
            inc(nc.scalar.activation(acol[:, 1:2 * NTT:2], acol[:, 1:2 * NTT:2], AF.Exp,
                                     bias=0.0, scale=-0.5), ev)
            nc.vector.wait_ge(ev.s, ev.n)
            inc(nc.vector.tensor_mul(acol[:, 0:2 * NTT:2], stats[:, 0:2 * NTT:2], acol[:, 1:2 * NTT:2]), ev)
            nc.vector.wait_ge(ev.s, ev.n)
            inc(nc.vector.tensor_scalar_mul(acol[:, 0:2 * NTT:2], acol[:, 0:2 * NTT:2], -1.0), ev)
            nc.vector.wait_ge(ev.s, ev.n)
            inc(nc.vector.tensor_copy(acolb[:, :], acol[:, :]), ev)
            nc.tensor.wait_ge(ev.s, ev.n)
            for tt in range(NTT):
                for half, col in ((0, 2 * tt + 1), (1, 2 * tt)):
                    nc.tensor.wait_ge(rw.s, rw.n)
                    inc(nc.tensor.transpose(prow[:, :], acolb[:, col:col + 1], identb[:, :]), tr)
                    nc.vector.wait_ge(tr.s, tr.n)
                    inc(nc.vector.tensor_copy(abrow[0:1, half * T + tt * 128: half * T + (tt + 1) * 128],
                                              prow[0:1, :]), rw)
            nc.tensor.wait_ge(rw.s, rw.n)
            nb = 0
            for half in range(2):
                for w0 in range(0, T, TW):
                    pb = pbc0 if nb % 2 == 0 else pbc1
                    if nb >= 2:
                        nc.tensor.wait_ge(bc.s, nb - 1)
                    inc(nc.tensor.matmul(pb[:, :], onesrb[0:1, :], abrow[0:1, half * T + w0: half * T + w0 + TW],
                                         start=True, stop=True), mm)
                    nc.vector.wait_ge(mm.s, mm.n)
                    inc(nc.vector.tensor_copy(abB[:, half * T + w0: half * T + w0 + TW], pb[:, :]), bc)
                    nb += 1
            for dc in range(DC):
                par = dc % 2
                nc.vector.wait_ge(bc.s, bc.n)
                if dc >= 1:
                    nc.vector.wait_ge(nm_.s, 2 * dc)
                inc(nc.vector.tensor_mul(lt[:, :], X[:, dc * T:(dc + 1) * T],
                                         abB[:, 0:T]), nm_)
                nc.vector.wait_ge(nm_.s, nm_.n)
                inc(nc.vector.tensor_add(ln_buf[:, dc * T:(dc + 1) * T], lt[:, :],
                                         abB[:, T:2 * T]), nm_)
            nc.vector.wait_ge(nm_.s, nm_.n)
        barrier()

    def emit_proj_spill(wname, n_oc, spill, ln_buf):
        assert n_oc % 2 == 0
        npair = n_oc // 2
        if wname in gsems:
            nc.sync.wait_ge(gsems[wname].s, gsems[wname].n)
        with (
            nc.sbuf_tensor(tn("pw"), [128, 2 * DC * 128], BF16) as pw,
            nc.sbuf_tensor(tn("pev"), [128, 2 * T], BF16) as pev,
            nc.psum_tensor(tn("pq0"), [128, TW], F32) as pq0,
            nc.psum_tensor(tn("pq1"), [128, TW], F32) as pq1,
            nc.psum_tensor(tn("pq2"), [128, TW], F32) as pq2,
            nc.psum_tensor(tn("pq3"), [128, TW], F32) as pq3,
        ):
            pps = [[pq0, pq1], [pq2, pq3]]
            wl = [ctr("wl0"), ctr("wl1")]
            mm = ctr("pm")
            evd = [ctr("pe0"), ctr("pe1")]
            std = [ctr("ps0"), ctr("ps1")]
            wsrc = cv_full[wname]
            for ip in range(npair):
                for par in range(2):
                    nc.sync.wait_ge(mm.s, max(0, 2 * ip + par - 1))
                    nc.sync.dma_start(
                        pw[:, par * DC * 128:(par + 1) * DC * 128].rearrange("p (c o) -> p c o", o=128),
                        wsrc[:, ds((2 * ip + par) * 128, 128)].rearrange("(c p) o -> p c o", p=128),
                    ).then_inc(wl[par].s, 16)
                    nc.tensor.wait_ge(wl[par].s, 16 * (ip + 1))
                    nc.tensor.wait_ge(evd[par].s, NTW * ip)
                    for tw in range(NTW):
                        for dc in range(DC):
                            mmi = nc.tensor.matmul(pps[par][tw][:, :],
                                             pw[:, par * DC * 128 + dc * 128: par * DC * 128 + (dc + 1) * 128],
                                             ln_buf[:, dc * T + tw * TW: dc * T + tw * TW + TW],
                                             start=(dc == 0), stop=(dc == DC - 1))
                    mmi.then_inc(mm.s, 1)
                    nc.scalar.wait_ge(mm.s, 2 * ip + par + 1)
                    nc.scalar.wait_ge(std[par].s, 16 * ip)
                    for tw in range(NTW):
                        nc.scalar.activation(pev[:, par * T + tw * TW: par * T + (tw + 1) * TW],
                                             pps[par][tw][:, :], AF.Copy, bias=0.0,
                                             scale=1.0).then_inc(evd[par].s, 1)
                    nc.scalar.wait_ge(evd[par].s, NTW * (ip + 1))
                    nc.scalar.dma_start(spill[ds((2 * ip + par) * 128, 128), :],
                                        pev[:, par * T:(par + 1) * T]).then_inc(std[par].s, 16)
            nc.sync.wait_ge(std[0].s, 16 * npair)
            nc.sync.wait_ge(std[1].s, 16 * npair)
        barrier()

    def emit_wo_resid(wname, OT, g):
        with (
            nc.sbuf_tensor(tn("ww"), [128, 2 * DC * 128], BF16) as ww,
            nc.psum_tensor(tn("pw0"), [128, GQ], F32) as pw0,
            nc.psum_tensor(tn("pw1"), [128, GQ], F32) as pw1,
        ):
            wl = [ctr("owl0"), ctr("owl1")]
            mm = ctr("omm")
            if wname in gsems:
                nc.sync.wait_ge(gsems[wname].s, gsems[wname].n)
            ad = [ctr("oad0"), ctr("oad1")]
            wsrc = cv_full[wname]
            pps = [pw0, pw1]
            for ip in range(DC // 2):
                for par in range(2):
                    nc.sync.wait_ge(mm.s, max(0, 2 * ip + par - 1))
                    nc.sync.dma_start(
                        ww[:, par * DC * 128:(par + 1) * DC * 128].rearrange("p (c o) -> p c o", o=128),
                        wsrc[:, ds((2 * ip + par) * 128, 128)].rearrange("(c p) o -> p c o", p=128),
                    ).then_inc(wl[par].s, 16)
                    nc.tensor.wait_ge(wl[par].s, 16 * (ip + 1))
                    nc.tensor.wait_ge(ad[par].s, ip)
                    for dc in range(DC):
                        mmi = nc.tensor.matmul(pps[par][:, :],
                                         ww[:, par * DC * 128 + dc * 128: par * DC * 128 + (dc + 1) * 128],
                                         OT[:, dc * GQ:(dc + 1) * GQ],
                                         start=(dc == 0), stop=(dc == DC - 1))
                    mmi.then_inc(mm.s, 1)
                    nc.vector.wait_ge(mm.s, 2 * ip + par + 1)
                    nc.vector.tensor_add(X[:, ds((2 * ip + par) * T + g * GQ, GQ)],
                                         X[:, ds((2 * ip + par) * T + g * GQ, GQ)],
                                         pps[par][:, :]).then_inc(ad[par].s, 1)
            nc.sync.wait_ge(ad[0].s, DC // 2)
            nc.sync.wait_ge(ad[1].s, DC // 2)
        barrier()

    def emit_ffn():
        with nc.sbuf_tensor(tn("lnf"), [128, DC * T], BF16) as ln_buf:
            emit_ffn_body(ln_buf)

    def emit_ffn_body(ln_buf):
        emit_ln(ln_buf)
        with (
            nc.sbuf_tensor(tn("fw"), [128, 2 * DC * 128], BF16) as fw,
            nc.sbuf_tensor(tn("fsb"), [128, 4], F32) as fsb,
            nc.sbuf_tensor(tn("fst"), [128, 2 * T], F32) as fst,
            nc.psum_tensor(tn("pf0"), [128, TW], F32) as pf0,
            nc.psum_tensor(tn("pf1"), [128, TW], F32) as pf1,
            nc.psum_tensor(tn("pf2"), [128, TW], F32) as pf2,
            nc.psum_tensor(tn("pf3"), [128, TW], F32) as pf3,
        ):
            pps = [[pf0, pf1], [pf2, pf3]]
            wl = [ctr("fwl0"), ctr("fwl1")]
            mm = ctr("fmm")
            nc.sync.wait_ge(gsems["ft"].s, gsems["ft"].n)
            ep = [ctr("fep0"), ctr("fep1")]
            ad = [ctr("fad0"), ctr("fad1")]
            for ip in range(DC // 2):
                for par in range(2):
                    nc.sync.wait_ge(mm.s, max(0, 2 * ip + par - 1))
                    nc.sync.dma_start(
                        fw[:, par * DC * 128:(par + 1) * DC * 128].rearrange("p (c o) -> p c o", o=128),
                        cv_full["ft"][:, ds((2 * ip + par) * 128, 128)].rearrange("(c p) o -> p c o", p=128),
                    ).then_inc(wl[par].s, 16)
                    nc.sync.wait_ge(ep[par].s, NTW * ip)
                    nc.sync.dma_start(fsb[:, 2 * par:2 * par + 1], f_scale[ds(2 * ip + par, 1), :]).then_inc(wl[par].s, 16)
                    nc.sync.dma_start(fsb[:, 2 * par + 1:2 * par + 2], f_bias[ds(2 * ip + par, 1), :]).then_inc(wl[par].s, 16)
                    nc.tensor.wait_ge(wl[par].s, 48 * (ip + 1))
                    nc.tensor.wait_ge(ad[par].s, NTW * ip)
                    for tw in range(NTW):
                        for dc in range(DC):
                            mmi = nc.tensor.matmul(pps[par][tw][:, :],
                                             fw[:, par * DC * 128 + dc * 128: par * DC * 128 + (dc + 1) * 128],
                                             ln_buf[:, dc * T + tw * TW: dc * T + tw * TW + TW],
                                             start=(dc == 0), stop=(dc == DC - 1))
                    mmi.then_inc(mm.s, 1)
                    nc.vector.wait_ge(mm.s, 2 * ip + par + 1)
                    nc.vector.wait_ge(ad[par].s, NTW * ip)
                    nc.vector.wait_ge(wl[par].s, 48 * (ip + 1))
                    for tw in range(NTW):
                        nc.vector.tensor_scalar(fst[:, par * T + tw * TW: par * T + (tw + 1) * TW],
                                                pps[par][tw][:, :],
                                                fsb[:, 2 * par:2 * par + 1],
                                                fsb[:, 2 * par + 1:2 * par + 2],
                                                ALU.mult, ALU.add).then_inc(ep[par].s, 1)
                    nc.vector.wait_ge(ep[par].s, (ip + 1) * NTW)
                    for tw in range(NTW):
                        nc.vector.tensor_add(X[:, ds((2 * ip + par) * T + tw * TW, TW)],
                                             X[:, ds((2 * ip + par) * T + tw * TW, TW)],
                                             fst[:, par * T + tw * TW: par * T + (tw + 1) * TW]).then_inc(ad[par].s, 1)
            nc.sync.wait_ge(ad[0].s, NTW * (DC // 2))
            nc.sync.wait_ge(ad[1].s, NTW * (DC // 2))
        barrier()

    def emit_rotary(sp):
        with (
            nc.sbuf_tensor(tn("rin"), [64, T], BF16) as rin,
            nc.sbuf_tensor(tn("rtm"), [128, NJ * 64], BF16) as rtm,
            nc.sbuf_tensor(tn("rt2"), [128, NJ * 64], BF16) as rt2,
            nc.sbuf_tensor(tn("rt3"), [128, NJ * 64], BF16) as rt3,
            nc.sbuf_tensor(tn("rsin"), [128, NJ * 64], BF16) as rsin,
            nc.sbuf_tensor(tn("rcos"), [128, NJ * 64], BF16) as rcos,
            nc.sbuf_tensor(tn("rout"), [64, T], BF16) as rout,
            nc.psum_tensor(tn("prta"), [128, 64], BF16) as prta,
            nc.psum_tensor(tn("prtb"), [64, 128], BF16) as prtb,
        ):
            ld = ctr("rld"); tra = ctr("rtra"); eva = ctr("reva"); mth = ctr("rmth")
            trb = ctr("rtrb"); evb = ctr("revb"); sto = ctr("rsto")
            inc(nc.sync.dma_start(rsin[:, :], rot_sin_in.ap().rearrange("(t p) r -> p t r", p=128)), ld, 16)
            inc(nc.sync.dma_start(rcos[:, :], rot_cos_in.ap().rearrange("(t p) r -> p t r", p=128)), ld, 16)
            for h in range(HB):
                nc.sync.wait_ge(sto.s, sto.n)
                inc(nc.sync.dma_start(rin[:, :], sp[h * 256: h * 256 + 64, :]), ld, 16)
                nc.tensor.wait_ge(ld.s, ld.n)
                for tt in range(NJ):
                    nc.tensor.wait_ge(eva.s, eva.n)
                    inc(nc.tensor.transpose(prta[:, :], rin[:, tt * 128:(tt + 1) * 128], identb[0:64, 0:64]), tra)
                    nc.vector.wait_ge(tra.s, tra.n)
                    inc(nc.vector.tensor_copy(rtm[:, tt * 64:(tt + 1) * 64], prta[0:128, 0:64]), eva)
                nc.vector.wait_ge(eva.s, eva.n)
                r4a = rtm.ap().rearrange("p (u v) -> p u v", v=2)
                r4b = rt2.ap().rearrange("p (u v) -> p u v", v=2)
                inc(nc.vector.tensor_copy(r4b[:, :, 0:1], r4a[:, :, 1:2]), mth)
                nc.vector.wait_ge(mth.s, mth.n)
                inc(nc.vector.tensor_copy(r4b[:, :, 1:2], r4a[:, :, 0:1]), mth)
                nc.vector.wait_ge(mth.s, mth.n)
                inc(nc.vector.tensor_mul(rt3[:, :], rtm[:, :], rcos[:, :]), mth)
                nc.vector.wait_ge(mth.s, mth.n)
                inc(nc.vector.tensor_mul(rt2[:, :], rt2[:, :], rsin[:, :]), mth)
                nc.vector.wait_ge(mth.s, mth.n)
                inc(nc.vector.tensor_add(rtm[:, :], rt3[:, :], rt2[:, :]), mth)
                nc.tensor.wait_ge(mth.s, mth.n)
                for tt in range(NJ):
                    nc.tensor.wait_ge(evb.s, evb.n)
                    inc(nc.tensor.transpose(prtb[:, :], rtm[:, tt * 64:(tt + 1) * 64], identb[:, :]), trb)
                    nc.vector.wait_ge(trb.s, trb.n)
                    inc(nc.vector.tensor_copy(rout[:, tt * 128:(tt + 1) * 128], prtb[0:64, :]), evb)
                nc.scalar.wait_ge(evb.s, evb.n)
                inc(nc.scalar.dma_start(sp[h * 256: h * 256 + 64, :], rout[:, :]), sto, 16)
            nc.sync.wait_ge(sto.s, sto.n)
        barrier()

    def emit_to_token_major(src_ap, dst, nrc):
        with (
            nc.sbuf_tensor(tn("tmi"), [128, T], BF16) as tmi,
            nc.sbuf_tensor(tn("tmo"), [128, 128], BF16) as tmo,
            nc.psum_tensor(tn("ptm"), [128, 128], BF16) as ptm,
        ):
            ld = ctr("tml"); tr = ctr("tmt"); ev = ctr("tme"); st = ctr("tms")
            for dc in range(nrc):
                nc.sync.wait_ge(tr.s, tr.n)
                inc(nc.sync.dma_start(tmi[:, :], src_ap[dc * 128:(dc + 1) * 128, :]), ld, 16)
                nc.tensor.wait_ge(ld.s, ld.n)
                for tt in range(NJ):
                    nc.tensor.wait_ge(ev.s, ev.n)
                    inc(nc.tensor.transpose(ptm[:, :], tmi[:, tt * 128:(tt + 1) * 128], identb[:, :]), tr)
                    nc.vector.wait_ge(tr.s, tr.n)
                    nc.vector.wait_ge(st.s, st.n)
                    inc(nc.vector.tensor_copy(tmo[:, :], ptm[:, :]), ev)
                    nc.scalar.wait_ge(ev.s, ev.n)
                    inc(nc.scalar.dma_start(dst[tt * 128:(tt + 1) * 128, dc * 128:(dc + 1) * 128],
                                            tmo[:, :]), st, 16)
            nc.sync.wait_ge(st.s, st.n)
        barrier()

    def emit_gather_async(name, in_ap, out_ap):
        gs = rctr("g_" + name)
        inc(nc.gpsimd.collective_compute("AllGather", ALU.bypass, replica_groups=RG,
                                         ins=[in_ap], outs=[out_ap]), gs)
        gsems[name] = gs

    def slot_block(s):
        return s % NC, s // NC

    def emit_attn_group(g, is_a, OT, KT=None, VO=None):
        HD = 128 if is_a else 256
        NDC = HD // 128
        NH = HA if is_a else HB
        scale = C.scale_a if is_a else C.scale_b
        PVW = HD + 1
        slots1 = [0]
        slots2 = list(range(8 * (SPG * g + SPG - 1) + 8))
        geo = {}
        for s in slots2:
            jmin = max(SPG * g, s // 8)
            w = (SPG * g + SPG - jmin) * 128
            qoff = (jmin - SPG * g) * 128
            bdry = (s // 8) >= SPG * g
            geo[s] = (jmin, w, qoff, bdry)
        b1 = [slots1[i:i + 2] for i in range(0, len(slots1), 2)]
        b2 = [slots2[i:i + 2] for i in range(0, len(slots2), 2)]
        from contextlib import ExitStack
        with ExitStack() as _es:
            def A(g_):
                return _es.enter_context(g_)
            qT = A(nc.sbuf_tensor(tn("qT"), [128, 2 * NDC * GQ], BF16))
            acc = A(nc.sbuf_tensor(tn("acc"), [128, GQ], BF16))
            accb = acc
            crowf = A(nc.sbuf_tensor(tn("crowf"), [1, GQ], F32))
            slqr = A(nc.sbuf_tensor(tn("slqr"), [1, GQ], F32))
            ccol = A(nc.sbuf_tensor(tn("ccol"), [128, SPG], BF16))
            maxB2 = A(nc.sbuf_tensor(tn("maxB2"), [128, 1024], F32))
            stg = A(nc.sbuf_tensor(tn("stg"), [128, 2 * 1024], F32))
            PT = A(nc.sbuf_tensor(tn("PT"), [128, 2 * 1024], BF16))
            alp = A(nc.sbuf_tensor(tn("alp"), [128, 2 * NSLOT], F32)) if is_a else None
            rcp = A(nc.sbuf_tensor(tn("rcp"), [128, SPG], F32))
            osb2 = A(nc.sbuf_tensor(tn("osb2"), [128, SPG * HD], BF16))
            ktb = A(nc.sbuf_tensor(tn("ktb"), [128, 8 * NDC * 128], BF16)) if not is_a else None
            vob = A(nc.sbuf_tensor(tn("vob"), [128, 8 * 260], BF16)) if not is_a else None
            psA = A(nc.psum_tensor(tn("psA"), [128, 1024], F32))
            psB = A(nc.psum_tensor(tn("psB"), [128, 1024], F32))
            pO_guards = [nc.psum_tensor(tn(f"pO{j2}"), [128, 512], F32) for j2 in range(SPG)]
            pOs = [gd.__enter__() for gd in pO_guards]
            pasbs = [pOs[k].ap()[:, 0:64].bitcast(BF16) for k in range(SPG)]
            pasb = pasbs[0]
            pbcv = pOs[0].ap()[:, 0:GQ]
            pss = [psA, psB]
            qld = [ctr("aql0"), ctr("aql1")]; ald = [ctr("aal0"), ctr("aal1")]; sld = [ctr("asl0"), ctr("asl1")]
            kvl = [ctr(f"akl{i}") for i in range(8)]
            m1 = ctr("am1"); v1 = ctr("av1"); ca = ctr("aca"); tr1 = ctr("atr")
            rd1 = ctr("ard"); rw1 = ctr("arw"); bcx = ctr("abc"); mbe = ctr("amb")
            m2 = ctr("am2"); v2 = ctr("av2"); ex = ctr("aex"); msk = ctr("ams")
            pv = ctr("apv"); rc = ctr("arc"); oe = ctr("aoe"); otr = ctr("aot"); ovv = ctr("aov")
            if not is_a:
                nc.vector.memset(vob[:, :], 1.0)
                vm0 = ctr("avm")
                inc(nc.vector.engine_nop(), vm0)
                nc.sync.wait_ge(vm0.s, 1)
                nc.sync.wait_ge(gsems["ktb"].s, gsems["ktb"].n)
                nc.sync.wait_ge(gsems["vb"].s, gsems["vb"].n)
            m2_end = []; ex_end = []; rw1_end = []
            kv_m_end = {}
            kv_pv_end = {}
            for ih in range(NH):
                hp = ih % 2
                if ih >= 2:
                    nc.sync.wait_ge(m2.s, m2_end[ih - 2])
                if is_a:
                    inc(nc.sync.dma_start(qT[:, hp * GQ:(hp + 1) * GQ],
                                          qkv_sp[ih * 128:(ih + 1) * 128, g * GQ:(g + 1) * GQ]), qld[hp], 16)
                    if ih >= 2:
                        nc.sync.wait_ge(ex.s, ex_end[ih - 2])
                    inc(nc.sync.dma_start(alp[:, hp * NSLOT:(hp + 1) * NSLOT],
                                          alibi_post_t[:, ih * NSLOT:(ih + 1) * NSLOT]), ald[hp], 16)
                    if ih >= 1:
                        nc.sync.wait_ge(rw1.s, rw1_end[ih - 1])
                    inc(nc.sync.dma_start(slqr[0:1, :],
                                          slq_in[0:1, ih * T + g * GQ: ih * T + g * GQ + GQ]), sld[hp], 16)
                else:
                    inc(nc.sync.dma_start(qT[:, hp * NDC * GQ: hp * NDC * GQ + GQ],
                                          qb_sp[ih * 256: ih * 256 + 128, g * GQ:(g + 1) * GQ]), qld[hp], 16)
                    inc(nc.sync.dma_start(qT[:, hp * NDC * GQ + GQ: hp * NDC * GQ + 2 * GQ],
                                          qb_sp[ih * 256 + 128: ih * 256 + 256, g * GQ:(g + 1) * GQ]), qld[hp], 16)
                qbase = hp * NDC * GQ if not is_a else hp * GQ
                qld_h = qld[hp].n
                ald_h = ald[hp].n
                sld_h = sld[hp].n
                s = 0
                if not is_a:
                    kb = 0
                    cc, jj = slot_block(s)
                    if kb in kv_m_end:
                        nc.sync.wait_ge(m1.s if kv_m_end[kb][0] == 1 else m2.s, kv_m_end[kb][1])
                    for half in range(NDC):
                        inc(nc.sync.dma_start(
                            ktb[:, kb * NDC * 128 + half * 128: kb * NDC * 128 + (half + 1) * 128],
                            ktb_g[cc * D + ih * 256 + half * 128: cc * D + ih * 256 + (half + 1) * 128,
                                  jj * 128:(jj + 1) * 128]), kvl[kb], 16)
                nc.tensor.wait_ge(qld[hp].s, qld_h)
                nc.tensor.wait_ge(oe.s, oe.n)
                nc.tensor.wait_ge(ovv.s, ovv.n)
                if is_a:
                    mmi = nc.tensor.matmul(pOs[0][:, 0:GQ], KT[:, 0:128],
                                           qT[:, qbase:qbase + GQ], start=True, stop=True)
                else:
                    kb = 0
                    nc.tensor.wait_ge(kvl[kb].s, kvl[kb].n)
                    for half in range(NDC):
                        mmi = nc.tensor.matmul(pOs[0][:, 0:GQ],
                                               ktb[:, kb * NDC * 128 + half * 128: kb * NDC * 128 + (half + 1) * 128],
                                               qT[:, qbase + half * GQ: qbase + half * GQ + GQ],
                                               start=(half == 0), stop=(half == NDC - 1))
                    kv_m_end[kb] = (1, m1.n + 1)
                inc(mmi, m1)
                nc.vector.wait_ge(m1.s, m1.n)
                nc.vector.wait_ge(tr1.s, tr1.n)
                inc(nc.vector.tensor_add(acc[:, :], pOs[0][:, 0:GQ], p1m[:, :]), v1)
                def emit_assembly():
                    nc.vector.wait_ge(v1.s, v1.n)
                    inc(nc.vector.engine_nop(), ca)
                    nc.tensor.wait_ge(ca.s, ca.n)
                    nc.tensor.wait_ge(rw1.s, rw1.n)
                    for j2 in range(SPG):
                        inc(nc.tensor.transpose(pasbs[j2][:, :], accb[:, j2 * 128:(j2 + 1) * 128], identb[:, :]), tr1)
                    nc.vector.wait_ge(tr1.s, tr1.n)
                    for j2 in range(SPG):
                        inc(nc.vector.reduce_max(ccol[:, j2:j2 + 1], pasbs[j2][:, :], AX.X), rd1)
                    crow = accb
                    nc.tensor.wait_ge(rd1.s, rd1.n)
                    for j2 in range(SPG):
                        inc(nc.tensor.transpose(pasbs[j2][0:1, :], ccol[:, j2:j2 + 1], identb[:, :]), tr1)
                    nc.vector.wait_ge(tr1.s, tr1.n)
                    for j2 in range(SPG):
                        inc(nc.vector.tensor_copy(crow[0:1, j2 * 128:(j2 + 1) * 128], pasbs[j2][0:1, :]), rw1)
                    nc.vector.wait_ge(rw1.s, rw1.n)
                    nc.vector.wait_ge(bcx.s, bcx.n)
                    if is_a:
                        nc.vector.wait_ge(sld[hp].s, sld_h)
                        inc(nc.vector.tensor_add(crowf[0:1, :], crow[0:1, :], slqr[0:1, :]), rw1)
                    else:
                        inc(nc.vector.tensor_copy(crowf[0:1, :], crow[0:1, :]), rw1)
                    rw1_end.append(rw1.n)
                    nc.tensor.wait_ge(rw1.s, rw1.n)
                    nc.tensor.wait_ge(mbe.s, mbe.n)
                    inc(nc.tensor.matmul(pbcv, onesr[0:1, :], crowf[0:1, :],
                                             start=True, stop=True), bcx)
                    nc.vector.wait_ge(bcx.s, bcx.n)
                    inc(nc.vector.tensor_copy(maxB2[:, 0:GQ], pbcv), mbe)
                    nc.vector.wait_ge(mbe.s, mbe.n)
                    inc(nc.vector.tensor_copy(maxB2[:, 512:512 + GQ], maxB2[:, 0:GQ]), mbe)
                NB = len(b2)

                def emit_mm2(bi):
                    par = bi % 2
                    ps = pss[par]
                    bb = b2[bi]
                    jmin, w, qoff, bdry = geo[bb[0]]
                    if not is_a:
                        for i, s in enumerate(bb):
                            kb = s % 8
                            cc, jj = slot_block(s)
                            if kb in kv_m_end:
                                nc.sync.wait_ge(m1.s if kv_m_end[kb][0] == 1 else m2.s, kv_m_end[kb][1])
                            for half in range(NDC):
                                inc(nc.sync.dma_start(
                                    ktb[:, kb * NDC * 128 + half * 128: kb * NDC * 128 + (half + 1) * 128],
                                    ktb_g[cc * D + ih * 256 + half * 128: cc * D + ih * 256 + (half + 1) * 128,
                                          jj * 128:(jj + 1) * 128]), kvl[kb], 16)
                            if kb in kv_pv_end:
                                nc.sync.wait_ge(pv.s, kv_pv_end[kb])
                            inc(nc.sync.dma_start(vob[:, kb * 260: kb * 260 + 256],
                                                  vb_g[cc * T + jj * 128: cc * T + jj * 128 + 128,
                                                       ih * 256:(ih + 1) * 256]), kvl[kb], 16)
                    if bi == 0 and ih > 0:
                        nc.tensor.wait_ge(v2.s, v2.n)
                    if bi >= 2:
                        nc.tensor.wait_ge(v2.s, v2_mark[bi - 2])
                    for i, s in enumerate(bb):
                        if is_a:
                            mmi = nc.tensor.matmul(ps[:, i * 512: i * 512 + w], KT[:, s * 128:(s + 1) * 128],
                                                   qT[:, qbase + qoff: qbase + qoff + w], start=True, stop=True)
                        else:
                            kb = s % 8
                            nc.tensor.wait_ge(kvl[kb].s, kvl[kb].n)
                            for half in range(NDC):
                                mmi = nc.tensor.matmul(ps[:, i * 512: i * 512 + w],
                                                       ktb[:, kb * NDC * 128 + half * 128: kb * NDC * 128 + (half + 1) * 128],
                                                       qT[:, qbase + half * GQ + qoff: qbase + half * GQ + qoff + w],
                                                       start=(half == 0), stop=(half == NDC - 1))
                            kv_m_end[kb] = (2, m2.n + 1)
                    inc(mmi, m2)

                def emit_subexp(bi):
                    par = bi % 2
                    ps = pss[par]
                    bb = b2[bi]
                    jmin, w, qoff, bdry = geo[bb[0]]
                    nb_ = len(bb)
                    nc.vector.wait_ge(m2.s, m2_mark[bi])
                    if bi == 0:
                        nc.vector.wait_ge(mbe.s, mbe.n)
                    if bi >= 2:
                        nc.vector.wait_ge(ex.s, ex_hist[bi - 2])
                    inc(nc.vector.tensor_tensor(
                        stg.ap().rearrange("p (b s q) -> p b s q", b=2, s=2)[:, par, 0:nb_, qoff:qoff + w],
                        ps.ap().rearrange("p (s q) -> p s q", s=2)[:, 0:nb_, 0:w],
                        maxB2.ap().rearrange("p (s q) -> p s q", s=2)[:, 0:nb_, qoff:qoff + w],
                        ALU.subtract), v2)
                    v2_mark[bi] = v2.n
                    nc.scalar.wait_ge(v2.s, v2.n)
                    if is_a and bi == 0:
                        nc.scalar.wait_ge(ald[hp].s, ald_h)
                    if bi >= 2:
                        nc.scalar.wait_ge(pv.s, pv_hist[bi - 2])
                    for i, s in enumerate(bb):
                        if is_a:
                            inc(nc.scalar.activation(PT[:, par * 1024 + i * 512 + qoff: par * 1024 + i * 512 + qoff + w],
                                                     stg[:, par * 1024 + i * 512 + qoff: par * 1024 + i * 512 + qoff + w],
                                                     AF.Exp, bias=alp[:, hp * NSLOT + s: hp * NSLOT + s + 1],
                                                     scale=scale), ex)
                        else:
                            inc(nc.scalar.activation(PT[:, par * 1024 + i * 512 + qoff: par * 1024 + i * 512 + qoff + w],
                                                     stg[:, par * 1024 + i * 512 + qoff: par * 1024 + i * 512 + qoff + w],
                                                     AF.Exp, bias=0.0, scale=scale), ex)
                    ex_hist[bi] = ex.n

                def emit_masks(bi):
                    par = bi % 2
                    bb = b2[bi]
                    for i, s in enumerate(bb):
                        if geo[s][3]:
                            nc.vector.wait_ge(ex.s, ex_hist[bi])
                            _, w, qoff, _bd = geo[s]
                            dp_ = s % 8
                            inc(nc.vector.tensor_tensor(
                                PT.ap().bitcast(U16)[:, par * 1024 + i * 512 + qoff: par * 1024 + i * 512 + qoff + 128],
                                PT.ap().bitcast(U16)[:, par * 1024 + i * 512 + qoff: par * 1024 + i * 512 + qoff + 128],
                                amask[:, dp_ * 128:(dp_ + 1) * 128], ALU.bitwise_and), msk)
                    msk_hist[bi] = msk.n

                def emit_pv2(bi):
                    par = bi % 2
                    bb = b2[bi]
                    nc.tensor.wait_ge(ex.s, ex_hist[bi])
                    nc.tensor.wait_ge(msk.s, msk_hist[bi])
                    if bb[0] == 0:
                        nc.tensor.wait_ge(mbe.s, mbe.n)
                    for i_, s_ in enumerate(bb):
                        jm_, w_, qoff_, _b = geo[s_]
                        for j_ in range(jm_, SPG * g + SPG):
                            j2_ = j_ - SPG * g
                            rhs_ = vob[:, (s_ % 8) * 260: (s_ % 8) * 260 + PVW] if not is_a else VO[:, s_ * 129: s_ * 129 + PVW]
                            mmi_ = nc.tensor.matmul(pOs[j2_][:, 0:PVW],
                                                    PT[:, par * 1024 + i_ * 512 + qoff_ + (j_ - jm_) * 128:
                                                       par * 1024 + i_ * 512 + qoff_ + (j_ - jm_ + 1) * 128],
                                                    rhs_, start=(s_ == 0), stop=(s_ == 8 * j_ + 7))
                            inc(mmi_, pv)
                        if not is_a:
                            kv_pv_end[s_ % 8] = pv.n
                    pv_hist[bi] = pv.n

                m2_mark = [0] * NB
                msk_hist = [0] * NB
                v2_mark = [0] * NB
                ex_hist = [0] * NB
                pv_hist = [0] * NB
                for k in range(NB + 3):
                    if k < NB:
                        emit_mm2(k)
                        m2_mark[k] = m2.n
                    if k == 1:
                        emit_assembly()
                    if 0 <= k - 3 < NB:
                        emit_pv2(k - 3)
                    if 0 <= k - 1 < NB:
                        emit_subexp(k - 1)
                    if 0 <= k - 2 < NB:
                        emit_masks(k - 2)
                m2_end.append(m2.n)
                ex_end.append(ex.n)
                nc.vector.wait_ge(pv.s, pv.n)
                for j2 in range(SPG):
                    inc(nc.vector.reciprocal(rcp[:, j2:j2 + 1], pOs[j2][:, HD:HD + 1]), rc)
                nc.scalar.wait_ge(rc.s, rc.n)
                nc.scalar.wait_ge(otr.s, otr.n)
                for j2 in range(SPG):
                    inc(nc.scalar.activation(osb2[:, j2 * HD:(j2 + 1) * HD], pOs[j2][:, 0:HD], AF.Copy,
                                             bias=0.0, scale=rcp[:, j2:j2 + 1]), oe)
                nc.tensor.wait_ge(oe.s, oe.n)
                tiles = [(j2, half) for j2 in range(SPG) for half in range(NDC)]
                for r0 in range(0, len(tiles), SPG):
                    nc.tensor.wait_ge(ovv.s, ovv.n)
                    for bk, (j2, half) in enumerate(tiles[r0:r0 + SPG]):
                        inc(nc.tensor.transpose(pasbs[bk][:, :],
                                                osb2[:, j2 * HD + half * 128: j2 * HD + (half + 1) * 128],
                                                identb[:, :]), otr)
                    nc.vector.wait_ge(otr.s, otr.n)
                    for bk, (j2, half) in enumerate(tiles[r0:r0 + SPG]):
                        dcx = ih * NDC + half
                        inc(nc.vector.tensor_copy(OT[:, dcx * GQ + j2 * 128: dcx * GQ + (j2 + 1) * 128],
                                                  pasbs[bk][:, :]), ovv)
            nc.sync.wait_ge(ovv.s, ovv.n)
            for gd in reversed(pO_guards):
                gd.__exit__(None, None, None)
        barrier()

    with nc.sbuf_tensor(tn("ln1"), [128, DC * T], BF16) as ln_buf:
        emit_ln(ln_buf)
        emit_proj_spill("qkv", OA, qkv_sp, ln_buf)
    emit_gather_async("kta", qkv_sp.ap()[DC * 128:(DC + 1) * 128, :], kta_g.ap())
    emit_to_token_major(qkv_sp.ap()[(DC + 1) * 128:(DC + 2) * 128, :], va_loc.ap(), 1)
    emit_gather_async("va", va_loc.ap(), va_g.ap())
    with (
        nc.sbuf_tensor(tn("KTa"), [128, NSLOT * 128], BF16) as KTa,
        nc.sbuf_tensor(tn("VOa"), [128, NSLOT * 130], BF16) as VOa,
        nc.sbuf_tensor(tn("OTa"), [128, DC * GQ], BF16) as OTa,
    ):
        kvc = ctr("kvc")
        vmc = ctr("vmc")
        inc(nc.vector.memset(VOa[:, :], 1.0), vmc)
        nc.sync.wait_ge(vmc.s, 1)
        nc.sync.wait_ge(gsems["kta"].s, gsems["kta"].n)
        nc.sync.wait_ge(gsems["va"].s, gsems["va"].n)
        for s in range(NSLOT):
            cc, jj = slot_block(s)
            inc(nc.sync.dma_start(KTa[:, s * 128:(s + 1) * 128],
                                  kta_g[cc * 128:(cc + 1) * 128, jj * 128:(jj + 1) * 128]), kvc, 16)
            inc(nc.sync.dma_start(VOa[:, s * 129: s * 129 + 128],
                                  va_g[cc * T + jj * 128: cc * T + jj * 128 + 128, :]), kvc, 16)
        nc.sync.wait_ge(kvc.s, kvc.n)
        barrier()
        for g in range(NG):
            emit_attn_group(g, True, OTa, KT=KTa, VO=VOa)
            emit_wo_resid("oa", OTa, g)

    if n_sub >= 2:
        emit_ffn()

    if n_sub >= 3:
        with nc.sbuf_tensor(tn("ln3"), [128, DC * T], BF16) as ln_buf:
            emit_ln(ln_buf)
            emit_proj_spill("q", DC, qb_sp, ln_buf)
            emit_proj_spill("k", DC, ktb_loc, ln_buf)
            emit_proj_spill("v", DC, vtb_sp, ln_buf)
        emit_rotary(ktb_loc)
        emit_gather_async("ktb", ktb_loc.ap(), ktb_g.ap())
        emit_rotary(qb_sp)
        emit_to_token_major(vtb_sp.ap(), vb_loc.ap(), DC)
        emit_gather_async("vb", vb_loc.ap(), vb_g.ap())
        with nc.sbuf_tensor(tn("OTb"), [128, DC * GQ], BF16) as OTb:
            for g in range(NG):
                emit_attn_group(g, False, OTb)
                emit_wo_resid("ob", OTb, g)

    if n_sub >= 4:
        emit_ffn()

    so = ctr("so")
    for dc in range(DC):
        inc(nc.scalar.dma_start(outT[dc * 128:(dc + 1) * 128, :], X[:, dc * T:(dc + 1) * T]), so, 16)
    nc.scalar.wait_ge(so.s, so.n)
    for p in reversed(persist):
        p.__exit__(None, None, None)
    return nc


def kernel(L0_attn_a_hidden_states, L0_attn_a_Wqkv, L0_attn_a_Wo,
           L0_attn_b_Wq, L0_attn_b_Wk, L0_attn_b_Wv, L0_attn_b_Wo,
           L0_ffn_int8_weight, L0_ffn_scale, L0_ffn_bias, _cfg=None, _n_sub=4):
    cfg = _cfg or Cfg()
    NCn, D, T, NJ, DC = cfg.NC, cfg.D, cfg.T, cfg.NJ, cfg.DC
    x = np.asarray(L0_attn_a_hidden_states).reshape(cfg.SEQ, D)
    wqkv = np.ascontiguousarray(np.asarray(L0_attn_a_Wqkv, dtype=np.float32))
    woa = np.ascontiguousarray(np.asarray(L0_attn_a_Wo, dtype=np.float32))
    wq = np.ascontiguousarray(np.asarray(L0_attn_b_Wq, dtype=np.float32))
    wk = np.ascontiguousarray(np.asarray(L0_attn_b_Wk, dtype=np.float32))
    wv = np.ascontiguousarray(np.asarray(L0_attn_b_Wv, dtype=np.float32))
    wob = np.ascontiguousarray(np.asarray(L0_attn_b_Wo, dtype=np.float32))
    w8 = np.ascontiguousarray(np.asarray(L0_ffn_int8_weight))
    fs = np.asarray(L0_ffn_scale, dtype=np.float32).reshape(D)
    fb = np.asarray(L0_ffn_bias, dtype=np.float32).reshape(D)
    fs_t = np.ascontiguousarray(fs.reshape(DC, 128))
    fb_t = np.ascontiguousarray(fb.reshape(DC, 128))

    nc = build_program(cfg, n_sub=_n_sub)
    in_maps = []
    RSL = D // NCn
    for c in range(NCn):
        tks = np.concatenate([np.arange(128) + (8 * j + c) * 128 for j in range(NJ)])
        xc = np.ascontiguousarray(x[tks, :].T.astype(np.float32))
        t = host_tables(cfg, c)
        rsl = slice(c * RSL, (c + 1) * RSL)
        in_maps.append({
            "xT": xc, "w_qkv": np.ascontiguousarray(wqkv[rsl]), "w_oa": np.ascontiguousarray(woa[rsl]),
            "w_q": np.ascontiguousarray(wq[rsl]), "w_k": np.ascontiguousarray(wk[rsl]),
            "w_v": np.ascontiguousarray(wv[rsl]),
            "w_ob": np.ascontiguousarray(wob[rsl]), "w_f8": np.ascontiguousarray(w8[:, rsl]),
            "f_scale": fs_t, "f_bias": fb_t,
            "alibi_raw": t["alibi_raw"], "alibi_post": t["alibi_post"],
            "p1mask": t["p1mask"], "andmask": t["andmask"],
            "rot_sin": t["rot_sin"], "rot_cos": t["rot_cos"],
            "ident_bf": t["ident_bf"],
            "ones_f32": t["ones_f32"], "ones_row": t["ones_row"], "ones_row_bf": t["ones_row_bf"], "slq": t["slq"],
        })
    res = run_bass_kernel_spmd(nc, in_maps, core_ids=list(range(NCn)))
    out = np.zeros((cfg.SEQ, D), dtype=np.float32)
    for c in range(NCn):
        tks = np.concatenate([np.arange(128) + (8 * j + c) * 128 for j in range(NJ)])
        out[tks, :] = np.asarray(res.results[c]["outT"]).T
    return out.reshape(1, cfg.SEQ, D)


# version 18
# speedup vs baseline: 1.0999x; 1.0554x over previous
import math
import numpy as np
import ml_dtypes
import concourse.bass as bass
import concourse.mybir as mybir
from concourse.bass_utils import run_bass_kernel_spmd

F32 = mybir.dt.float32
BF16 = mybir.dt.bfloat16
U16 = mybir.dt.uint16
U32 = mybir.dt.uint32
I8 = mybir.dt.int8
AF = mybir.ActivationFunctionType
ALU = mybir.AluOpType
AX = mybir.AxisListType
ds = bass.ds

NEG = -1.0e30


class Cfg:
    def __init__(self, D=4096, SEQ=8192):
        self.NC = 8
        self.D = D
        self.SEQ = SEQ
        self.DC = D // 128
        self.T = SEQ // self.NC
        self.NJ = self.T // 128
        self.SPG = min(4, self.NJ)
        self.NG = self.NJ // self.SPG
        self.GQ = self.SPG * 128
        self.HA = D // 128
        self.HB = D // 256
        self.OA = self.DC + 2
        self.NSLOT = SEQ // 128
        self.scale_a = 1.0 / math.sqrt(128.0)
        self.scale_b = 1.0 / 16.0
        self.ROT = 64


def alibi_slopes(n):
    cp2 = 2 ** int(math.floor(math.log2(n)))
    base = 2.0 ** (-(2.0 ** (-(math.log2(cp2) - 3))))
    slopes = base ** np.arange(1, cp2 + 1, dtype=np.float64)
    if cp2 != n:
        extra_base = 2.0 ** (-(2.0 ** (-(math.log2(2 * cp2) - 3))))
        rem = min(cp2, n - cp2)
        extra = extra_base ** np.arange(1, 1 + 2 * rem, 2, dtype=np.float64)
        slopes = np.concatenate([slopes, extra])
    return slopes.astype(np.float64)


def host_tables(cfg, core):
    t = {}
    sl = alibi_slopes(cfg.HA)
    k = (np.arange(cfg.NSLOT)[None, :] * 128 + np.arange(128)[:, None]).astype(np.float64)
    ali = k[:, None, :] * sl[None, :, None]
    ali = ali.reshape(128, cfg.HA * cfg.NSLOT)
    t["alibi_post"] = ali.astype(np.float32)
    t["alibi_raw"] = (ali / cfg.scale_a).astype(np.float32)
    m1 = np.zeros((128, cfg.GQ), np.float32)
    if core == 0:
        p = np.arange(128)[:, None]
        q = np.arange(128)[None, :]
        m1[:, 0:128] = np.where(p > q, np.float32(NEG), np.float32(0.0))
    t["p1mask"] = m1.astype(ml_dtypes.bfloat16)
    am = np.zeros((128, 8 * 128), np.uint16)
    p = np.arange(128)[:, None]
    q = np.arange(128)[None, :]
    for dpos in range(8):
        blk = am[:, dpos * 128:(dpos + 1) * 128]
        if dpos < core:
            blk[:] = 0xFFFF
        elif dpos == core:
            blk[:] = np.where(p <= q, 0xFFFF, 0).astype(np.uint16)
    t["andmask"] = am
    pos = np.zeros((cfg.T,), np.int64)
    for j in range(cfg.NJ):
        pos[j * 128:(j + 1) * 128] = (8 * j + core) * 128 + np.arange(128)
    inv = 1.0 / (10000.0 ** (np.arange(0, cfg.ROT, 2, dtype=np.float64) / cfg.ROT))
    ang = pos[:, None].astype(np.float64) * inv[None, :]
    t["rot_sin"] = np.repeat(np.sin(ang), 2, axis=1).astype(ml_dtypes.bfloat16)
    t["rot_cos"] = np.repeat(np.cos(ang), 2, axis=1).astype(ml_dtypes.bfloat16)
    t["ident_bf"] = np.eye(128, dtype=ml_dtypes.bfloat16)
    t["ident_f32"] = np.eye(128, dtype=np.float32)
    t["ones_f32"] = np.ones((128, 1), np.float32)
    t["ones_row"] = np.ones((1, 128), np.float32)
    t["ones_row_bf"] = np.ones((1, 128), ml_dtypes.bfloat16)
    slq = (sl[:, None] * pos[None, :].astype(np.float64) / cfg.scale_a).reshape(1, cfg.HA * cfg.T)
    t["slq"] = slq.astype(np.float32)
    return t


_CTR_ID = [0]
_TN_ID = [0]


def tn(name):
    _TN_ID[0] += 1
    return f"{name}_{_TN_ID[0]}"


class Ctr:
    def __init__(self, nc, name):
        _CTR_ID[0] += 1
        self.s = nc.alloc_semaphore(name=f"{name}_{_CTR_ID[0]}")
        self.n = 0


def inc(instr, c, k=1):
    instr.then_inc(c.s, k)
    c.n += k
    return instr


DBG = False
DBG_SLOT = 0


def build_program(cfg, n_sub=4):
    nc = bass.Bass("TRN2", target_bir_lowering=False, dynamic_dma_scratch_size=32)
    C = cfg
    NC, D, DC, T, NJ, SPG, NG, GQ = C.NC, C.D, C.DC, C.T, C.NJ, C.SPG, C.NG, C.GQ
    HA, HB, OA, NSLOT = C.HA, C.HB, C.OA, C.NSLOT
    TW = min(512, T)
    NTW = T // TW
    NTT = T // 128

    xT = nc.dram_tensor("xT", [D, T], F32, kind="ExternalInput")
    RSL = D // NC
    w_qkv = nc.dram_tensor("w_qkv", [RSL, OA * 128], F32, kind="ExternalInput")
    w_oa = nc.dram_tensor("w_oa", [RSL, D], F32, kind="ExternalInput")
    w_q = nc.dram_tensor("w_q", [RSL, D], F32, kind="ExternalInput")
    w_k = nc.dram_tensor("w_k", [RSL, D], F32, kind="ExternalInput")
    w_v = nc.dram_tensor("w_v", [RSL, D], F32, kind="ExternalInput")
    w_ob = nc.dram_tensor("w_ob", [RSL, D], F32, kind="ExternalInput")
    w_f8 = nc.dram_tensor("w_f8", [D, RSL], I8, kind="ExternalInput")
    f_scale = nc.dram_tensor("f_scale", [DC, 128], F32, kind="ExternalInput")
    f_bias = nc.dram_tensor("f_bias", [DC, 128], F32, kind="ExternalInput")
    alibi_raw_t = nc.dram_tensor("alibi_raw", [128, HA * NSLOT], F32, kind="ExternalInput")
    alibi_post_t = nc.dram_tensor("alibi_post", [128, HA * NSLOT], F32, kind="ExternalInput")
    p1mask_in = nc.dram_tensor("p1mask", [128, GQ], BF16, kind="ExternalInput")
    andmask_in = nc.dram_tensor("andmask", [128, 8 * 128], U16, kind="ExternalInput")
    rot_sin_in = nc.dram_tensor("rot_sin", [T, 64], BF16, kind="ExternalInput")
    rot_cos_in = nc.dram_tensor("rot_cos", [T, 64], BF16, kind="ExternalInput")
    ident_bf_in = nc.dram_tensor("ident_bf", [128, 128], BF16, kind="ExternalInput")
    ones_f32_in = nc.dram_tensor("ones_f32", [128, 1], F32, kind="ExternalInput")
    ones_row_in = nc.dram_tensor("ones_row", [1, 128], F32, kind="ExternalInput")
    ones_row_bf_in = nc.dram_tensor("ones_row_bf", [1, 128], BF16, kind="ExternalInput")
    slq_in = nc.dram_tensor("slq", [1, HA * T], F32, kind="ExternalInput")
    outT = nc.dram_tensor("outT", [D, T], F32, kind="ExternalOutput")
    dbg_acc = nc.dram_tensor("dbg_acc", [128, GQ], F32)
    dbg_maxB = nc.dram_tensor("dbg_maxB", [128, GQ], F32)
    dbg_stg = nc.dram_tensor("dbg_stg", [128, GQ], F32)
    dbg_PT = nc.dram_tensor("dbg_PT", [128, GQ], BF16)
    dbg_pO = nc.dram_tensor("dbg_pO", [128, 260], F32)

    cv_sl, cv_full = {}, {}
    for nm, ocols in (("qkv", OA * 128), ("oa", D), ("q", D), ("k", D), ("v", D), ("ob", D), ("ft", D)):
        cv_sl[nm] = nc.dram_tensor(f"cv_sl_{nm}", [D // NC, ocols], BF16)
        cv_full[nm] = nc.dram_tensor(f"cv_{nm}", [D, ocols], BF16)
    qkv_sp = nc.dram_tensor("qkv_sp", [OA * 128, T], BF16)
    va_loc = nc.dram_tensor("va_loc", [T, 128], BF16)
    kta_g = nc.dram_tensor("kta_g", [NC * 128, T], BF16)
    va_g = nc.dram_tensor("va_g", [NC * T, 128], BF16)
    qb_sp = nc.dram_tensor("qb_sp", [D, T], BF16)
    ktb_loc = nc.dram_tensor("ktb_loc", [D, T], BF16)
    vtb_sp = nc.dram_tensor("vtb_sp", [D, T], BF16)
    vb_loc = nc.dram_tensor("vb_loc", [T, D], BF16)
    ktb_g = nc.dram_tensor("ktb_g", [NC * D, T], BF16)
    vb_g = nc.dram_tensor("vb_g", [NC * T, D], BF16)
    RG = [list(range(NC))]
    gsems = {}

    persist = []
    def sbp(name, shape, dtype):
        g = nc.sbuf_tensor(name, shape, dtype)
        h = g.__enter__()
        persist.append(g)
        return h

    X = sbp("X", [128, DC * T], F32)
    identb = sbp("identb", [128, 128], BF16)
    onesf = sbp("onesf", [128, 1], F32)
    onesr = sbp("onesr", [1, 128], F32)
    onesrb = sbp("onesrb", [1, 128], BF16)
    amask = sbp("amask", [128, 8 * 128], U16)
    p1m = sbp("p1m", [128, GQ], BF16)
    eps_col = sbp("eps_col", [128, 1], F32)

    allh = [nc.alloc_semaphore(name=f"pool{i}") for i in range(nc.free_len())]
    allh.sort(key=lambda h: h.num)
    NRES = 14
    reserved = allh[:NRES]
    pool = allh[NRES:]
    pool_ids = sorted(h.num for h in pool)
    assert pool_ids == list(range(pool_ids[0], pool_ids[0] + len(pool))), "pool ids not contiguous"
    pool_range = range(pool_ids[0], pool_ids[0] + len(pool))
    free_sems = list(pool)
    used_sems = []

    def rctr(name):
        h = reserved.pop()
        c = Ctr.__new__(Ctr)
        c.s = h
        c.n = 0
        return c

    def ctr(name):
        h = free_sems.pop()
        used_sems.append(h)
        c = Ctr.__new__(Ctr)
        c.s = h
        c.n = 0
        return c

    def barrier():
        nc.all_engine_barrier()
        if used_sems:
            nc.gpsimd.dma_reset(pool_range)
            nc.gpsimd.sem_clear(pool_range)
            free_sems.extend(used_sems)
            used_sems.clear()
            nc.all_engine_barrier()

    s0 = ctr("s0")
    for dc in range(DC):
        inc(nc.sync.dma_start(X[:, dc * T:(dc + 1) * T], xT[dc * 128:(dc + 1) * 128, :]), s0, 16)
    for dst, src in ((identb, ident_bf_in), (onesf, ones_f32_in),
                     (onesr, ones_row_in), (onesrb, ones_row_bf_in),
                     (amask, andmask_in), (p1m, p1mask_in)):
        inc(nc.sync.dma_start(dst[:, :], src[:, :]), s0, 16)
    nc.sync.wait_ge(s0.s, s0.n)
    nc.vector.memset(eps_col[:, :], 1e-5)
    barrier()

    RS = D // NC
    NRT = RS // 128
    with (
        nc.sbuf_tensor("cwi", [128, 2 * (OA * 128)], F32) as cwi,
        nc.sbuf_tensor("cwo", [128, 2 * (OA * 128)], BF16) as cwo,
        nc.sbuf_tensor("ci8", [128, 2 * RS], I8) as ci8,
        nc.sbuf_tensor("cb16", [128, 2 * RS], BF16) as cb16,
        nc.sbuf_tensor("ctr8", [128, 2 * RS], BF16) as ctr8,
        nc.psum_tensor("ptr0", [128, 128], BF16) as ptr0,
        nc.psum_tensor("ptr1", [128, 128], BF16) as ptr1,
    ):
        ld = [ctr("cvl0"), ctr("cvl1")]
        cvs = [ctr("cvc0"), ctr("cvc1")]
        st = [ctr("cvs0"), ctr("cvs1")]
        for wi, (nm, src, ocols) in enumerate((("qkv", w_qkv, OA * 128), ("oa", w_oa, D),
                                               ("q", w_q, D), ("k", w_k, D), ("v", w_v, D), ("ob", w_ob, D))):
            for r in range(NRT):
                par = (wi * NRT + r) % 2
                nc.sync.wait_ge(st[par].s, st[par].n)
                inc(nc.sync.dma_start(cwi[:, par * (OA * 128): par * (OA * 128) + ocols],
                                      src[r * 128:(r + 1) * 128, :]), ld[par], 16)
                nc.vector.wait_ge(ld[par].s, ld[par].n)
                inc(nc.vector.tensor_copy(cwo[:, par * (OA * 128): par * (OA * 128) + ocols],
                                          cwi[:, par * (OA * 128): par * (OA * 128) + ocols]), cvs[par])
                nc.scalar.wait_ge(cvs[par].s, cvs[par].n)
                inc(nc.scalar.dma_start(cv_sl[nm][r * 128:(r + 1) * 128, :],
                                        cwo[:, par * (OA * 128): par * (OA * 128) + ocols]), st[par], 16)
        l8 = [ctr("l80"), ctr("l81")]
        c8 = [ctr("c80"), ctr("c81")]
        t8 = [ctr("t80"), ctr("t81")]
        e8 = [ctr("e80"), ctr("e81")]
        sp8 = [ctr("sp80"), ctr("sp81")]
        for oc in range(DC):
            par = oc % 2
            nc.sync.wait_ge(sp8[par].s, sp8[par].n)
            inc(nc.sync.dma_start(ci8[:, par * RS: par * RS + RS],
                                  w_f8[oc * 128:(oc + 1) * 128, :]), l8[par], 16)
            nc.vector.wait_ge(l8[par].s, l8[par].n)
            nc.vector.wait_ge(t8[par].s, t8[par].n)
            inc(nc.vector.tensor_copy(cb16[:, par * RS: par * RS + RS],
                                      ci8[:, par * RS: par * RS + RS]), c8[par])
            nc.tensor.wait_ge(c8[par].s, c8[par].n)
            ptr = ptr0 if par == 0 else ptr1
            for tt in range(NRT):
                nc.tensor.wait_ge(e8[par].s, e8[par].n)
                inc(nc.tensor.transpose(ptr[:, :], cb16[:, par * RS + tt * 128: par * RS + (tt + 1) * 128],
                                        identb[:, :]), t8[par])
                nc.vector.wait_ge(t8[par].s, t8[par].n)
                inc(nc.vector.tensor_copy(ctr8[:, par * RS + tt * 128: par * RS + (tt + 1) * 128],
                                          ptr[:, :]), e8[par])
            nc.scalar.wait_ge(e8[par].s, e8[par].n)
            inc(nc.scalar.dma_start(
                cv_sl["ft"].ap().rearrange("(t p) o -> p t o", p=128)[:, :, oc * 128:(oc + 1) * 128],
                ctr8[:, par * RS: par * RS + RS].rearrange("p (t q) -> p t q", q=128),
            ), sp8[par], 16)
        for par in range(2):
            nc.gpsimd.wait_ge(st[par].s, st[par].n)
            nc.gpsimd.wait_ge(sp8[par].s, sp8[par].n)
        for nm in ("qkv", "oa", "ft", "q", "k", "v", "ob"):
            gs = rctr("g_" + nm)
            inc(nc.gpsimd.collective_compute("AllGather", ALU.bypass, replica_groups=RG,
                                             ins=[cv_sl[nm].ap()], outs=[cv_full[nm].ap()]), gs)
            gsems[nm] = gs
    barrier()

    def emit_ln(ln_buf):
        with (
            nc.sbuf_tensor(tn("x2"), [128, 2 * T], F32) as x2,
            nc.sbuf_tensor(tn("stats"), [128, 2 * NTT], F32) as stats,
            nc.sbuf_tensor(tn("abrow"), [1, 2 * T], BF16) as abrow,
            nc.sbuf_tensor(tn("abB"), [128, 2 * T], BF16) as abB,
            nc.sbuf_tensor(tn("acol"), [128, 2 * NTT], F32) as acol,
            nc.sbuf_tensor(tn("acolb"), [128, 2 * NTT], BF16) as acolb,
            nc.sbuf_tensor(tn("lt"), [128, T], F32) as lt,
            nc.psum_tensor(tn("pstt"), [128, 2 * NTT * DC], F32) as pstt,
            nc.psum_tensor(tn("prow"), [1, 128], BF16) as prow,
            nc.psum_tensor(tn("pbc0"), [128, TW], F32) as pbc0,
            nc.psum_tensor(tn("pbc1"), [128, TW], F32) as pbc1,
        ):
            sq = ctr("lnsq"); mm = ctr("lnmm"); ev = ctr("lnev"); tr = ctr("lntr")
            rw = ctr("lnrw"); bc = ctr("lnbc"); nm_ = ctr("lnnm")
            for dc in range(DC):
                par = dc % 2
                if dc >= 2:
                    nc.scalar.wait_ge(mm.s, (dc - 1) * NTT)
                inc(nc.scalar.activation(x2[:, par * T:(par + 1) * T], X[:, dc * T:(dc + 1) * T],
                                         AF.Square), sq)
                nc.tensor.wait_ge(sq.s, sq.n)
                for tt in range(NTT):
                    nc.tensor.matmul(pstt[:, (2 * tt) * DC + dc:(2 * tt) * DC + dc + 1],
                                     X[:, dc * T + tt * 128: dc * T + (tt + 1) * 128],
                                     onesf[:, :], start=True, stop=True)
                    inc(nc.tensor.matmul(pstt[:, (2 * tt + 1) * DC + dc:(2 * tt + 1) * DC + dc + 1],
                                         x2[:, par * T + tt * 128: par * T + (tt + 1) * 128],
                                         onesf[:, :], start=True, stop=True), mm)
            nc.vector.wait_ge(mm.s, mm.n)
            inc(nc.vector.reduce_sum(stats[:, :], pstt.ap().rearrange("p (s d) -> p s d", d=DC), AX.X), ev)
            nc.vector.wait_ge(ev.s, ev.n)
            inc(nc.vector.tensor_scalar_mul(stats[:, :], stats[:, :], 1.0 / D), ev)
            nc.vector.wait_ge(ev.s, ev.n)
            inc(nc.vector.tensor_mul(acol[:, 0:2 * NTT:2], stats[:, 0:2 * NTT:2], stats[:, 0:2 * NTT:2]), ev)
            nc.vector.wait_ge(ev.s, ev.n)
            inc(nc.vector.tensor_sub(acol[:, 1:2 * NTT:2], stats[:, 1:2 * NTT:2], acol[:, 0:2 * NTT:2]), ev)
            nc.scalar.wait_ge(ev.s, ev.n)
            inc(nc.scalar.activation(acol[:, 1:2 * NTT:2], acol[:, 1:2 * NTT:2], AF.Ln,
                                     bias=eps_col[:, 0:1], scale=1.0), ev)
            nc.scalar.wait_ge(ev.s, ev.n)
            inc(nc.scalar.activation(acol[:, 1:2 * NTT:2], acol[:, 1:2 * NTT:2], AF.Exp,
                                     bias=0.0, scale=-0.5), ev)
            nc.vector.wait_ge(ev.s, ev.n)
            inc(nc.vector.tensor_mul(acol[:, 0:2 * NTT:2], stats[:, 0:2 * NTT:2], acol[:, 1:2 * NTT:2]), ev)
            nc.vector.wait_ge(ev.s, ev.n)
            inc(nc.vector.tensor_scalar_mul(acol[:, 0:2 * NTT:2], acol[:, 0:2 * NTT:2], -1.0), ev)
            nc.vector.wait_ge(ev.s, ev.n)
            inc(nc.vector.tensor_copy(acolb[:, :], acol[:, :]), ev)
            nc.tensor.wait_ge(ev.s, ev.n)
            for tt in range(NTT):
                for half, col in ((0, 2 * tt + 1), (1, 2 * tt)):
                    nc.tensor.wait_ge(rw.s, rw.n)
                    inc(nc.tensor.transpose(prow[:, :], acolb[:, col:col + 1], identb[:, :]), tr)
                    nc.vector.wait_ge(tr.s, tr.n)
                    inc(nc.vector.tensor_copy(abrow[0:1, half * T + tt * 128: half * T + (tt + 1) * 128],
                                              prow[0:1, :]), rw)
            nc.tensor.wait_ge(rw.s, rw.n)
            nb = 0
            for half in range(2):
                for w0 in range(0, T, TW):
                    pb = pbc0 if nb % 2 == 0 else pbc1
                    if nb >= 2:
                        nc.tensor.wait_ge(bc.s, nb - 1)
                    inc(nc.tensor.matmul(pb[:, :], onesrb[0:1, :], abrow[0:1, half * T + w0: half * T + w0 + TW],
                                         start=True, stop=True), mm)
                    nc.vector.wait_ge(mm.s, mm.n)
                    inc(nc.vector.tensor_copy(abB[:, half * T + w0: half * T + w0 + TW], pb[:, :]), bc)
                    nb += 1
            for dc in range(DC):
                par = dc % 2
                nc.vector.wait_ge(bc.s, bc.n)
                if dc >= 1:
                    nc.vector.wait_ge(nm_.s, 2 * dc)
                inc(nc.vector.tensor_mul(lt[:, :], X[:, dc * T:(dc + 1) * T],
                                         abB[:, 0:T]), nm_)
                nc.vector.wait_ge(nm_.s, nm_.n)
                inc(nc.vector.tensor_add(ln_buf[:, dc * T:(dc + 1) * T], lt[:, :],
                                         abB[:, T:2 * T]), nm_)
            nc.vector.wait_ge(nm_.s, nm_.n)
        barrier()

    def emit_proj_spill(wname, n_oc, spill, ln_buf):
        assert n_oc % 2 == 0
        npair = n_oc // 2
        if wname in gsems:
            nc.sync.wait_ge(gsems[wname].s, gsems[wname].n)
        with (
            nc.sbuf_tensor(tn("pw"), [128, 2 * DC * 128], BF16) as pw,
            nc.sbuf_tensor(tn("pev"), [128, 2 * T], BF16) as pev,
            nc.psum_tensor(tn("pq0"), [128, TW], F32) as pq0,
            nc.psum_tensor(tn("pq1"), [128, TW], F32) as pq1,
            nc.psum_tensor(tn("pq2"), [128, TW], F32) as pq2,
            nc.psum_tensor(tn("pq3"), [128, TW], F32) as pq3,
        ):
            pps = [[pq0, pq1], [pq2, pq3]]
            wl = [ctr("wl0"), ctr("wl1")]
            mm = ctr("pm")
            evd = [ctr("pe0"), ctr("pe1")]
            std = [ctr("ps0"), ctr("ps1")]
            wsrc = cv_full[wname]
            for ip in range(npair):
                for par in range(2):
                    nc.sync.wait_ge(mm.s, max(0, 2 * ip + par - 1))
                    nc.sync.dma_start(
                        pw[:, par * DC * 128:(par + 1) * DC * 128].rearrange("p (c o) -> p c o", o=128),
                        wsrc[:, ds((2 * ip + par) * 128, 128)].rearrange("(c p) o -> p c o", p=128),
                    ).then_inc(wl[par].s, 16)
                    nc.tensor.wait_ge(wl[par].s, 16 * (ip + 1))
                    nc.tensor.wait_ge(evd[par].s, NTW * ip)
                    for tw in range(NTW):
                        for dc in range(DC):
                            mmi = nc.tensor.matmul(pps[par][tw][:, :],
                                             pw[:, par * DC * 128 + dc * 128: par * DC * 128 + (dc + 1) * 128],
                                             ln_buf[:, dc * T + tw * TW: dc * T + tw * TW + TW],
                                             start=(dc == 0), stop=(dc == DC - 1))
                    mmi.then_inc(mm.s, 1)
                    nc.scalar.wait_ge(mm.s, 2 * ip + par + 1)
                    nc.scalar.wait_ge(std[par].s, 16 * ip)
                    for tw in range(NTW):
                        nc.scalar.activation(pev[:, par * T + tw * TW: par * T + (tw + 1) * TW],
                                             pps[par][tw][:, :], AF.Copy, bias=0.0,
                                             scale=1.0).then_inc(evd[par].s, 1)
                    nc.scalar.wait_ge(evd[par].s, NTW * (ip + 1))
                    nc.scalar.dma_start(spill[ds((2 * ip + par) * 128, 128), :],
                                        pev[:, par * T:(par + 1) * T]).then_inc(std[par].s, 16)
            nc.sync.wait_ge(std[0].s, 16 * npair)
            nc.sync.wait_ge(std[1].s, 16 * npair)
        barrier()

    def emit_wo_resid(wname, OT, g):
        with (
            nc.sbuf_tensor(tn("ww"), [128, 2 * DC * 128], BF16) as ww,
            nc.psum_tensor(tn("pw0"), [128, GQ], F32) as pw0,
            nc.psum_tensor(tn("pw1"), [128, GQ], F32) as pw1,
        ):
            wl = [ctr("owl0"), ctr("owl1")]
            mm = ctr("omm")
            if wname in gsems:
                nc.sync.wait_ge(gsems[wname].s, gsems[wname].n)
            ad = [ctr("oad0"), ctr("oad1")]
            wsrc = cv_full[wname]
            pps = [pw0, pw1]
            for ip in range(DC // 2):
                for par in range(2):
                    nc.sync.wait_ge(mm.s, max(0, 2 * ip + par - 1))
                    nc.sync.dma_start(
                        ww[:, par * DC * 128:(par + 1) * DC * 128].rearrange("p (c o) -> p c o", o=128),
                        wsrc[:, ds((2 * ip + par) * 128, 128)].rearrange("(c p) o -> p c o", p=128),
                    ).then_inc(wl[par].s, 16)
                    nc.tensor.wait_ge(wl[par].s, 16 * (ip + 1))
                    nc.tensor.wait_ge(ad[par].s, ip)
                    for dc in range(DC):
                        mmi = nc.tensor.matmul(pps[par][:, :],
                                         ww[:, par * DC * 128 + dc * 128: par * DC * 128 + (dc + 1) * 128],
                                         OT[:, dc * GQ:(dc + 1) * GQ],
                                         start=(dc == 0), stop=(dc == DC - 1))
                    mmi.then_inc(mm.s, 1)
                    nc.vector.wait_ge(mm.s, 2 * ip + par + 1)
                    nc.vector.tensor_add(X[:, ds((2 * ip + par) * T + g * GQ, GQ)],
                                         X[:, ds((2 * ip + par) * T + g * GQ, GQ)],
                                         pps[par][:, :]).then_inc(ad[par].s, 1)
            nc.sync.wait_ge(ad[0].s, DC // 2)
            nc.sync.wait_ge(ad[1].s, DC // 2)
        barrier()

    def emit_ffn():
        with nc.sbuf_tensor(tn("lnf"), [128, DC * T], BF16) as ln_buf:
            emit_ffn_body(ln_buf)

    def emit_ffn_body(ln_buf):
        emit_ln(ln_buf)
        with (
            nc.sbuf_tensor(tn("fw"), [128, 2 * DC * 128], BF16) as fw,
            nc.sbuf_tensor(tn("fsb"), [128, 4], F32) as fsb,
            nc.sbuf_tensor(tn("fst"), [128, 2 * T], F32) as fst,
            nc.psum_tensor(tn("pf0"), [128, TW], F32) as pf0,
            nc.psum_tensor(tn("pf1"), [128, TW], F32) as pf1,
            nc.psum_tensor(tn("pf2"), [128, TW], F32) as pf2,
            nc.psum_tensor(tn("pf3"), [128, TW], F32) as pf3,
        ):
            pps = [[pf0, pf1], [pf2, pf3]]
            wl = [ctr("fwl0"), ctr("fwl1")]
            mm = ctr("fmm")
            nc.sync.wait_ge(gsems["ft"].s, gsems["ft"].n)
            ep = [ctr("fep0"), ctr("fep1")]
            ad = [ctr("fad0"), ctr("fad1")]
            for ip in range(DC // 2):
                for par in range(2):
                    nc.sync.wait_ge(mm.s, max(0, 2 * ip + par - 1))
                    nc.sync.dma_start(
                        fw[:, par * DC * 128:(par + 1) * DC * 128].rearrange("p (c o) -> p c o", o=128),
                        cv_full["ft"][:, ds((2 * ip + par) * 128, 128)].rearrange("(c p) o -> p c o", p=128),
                    ).then_inc(wl[par].s, 16)
                    nc.sync.wait_ge(ep[par].s, NTW * ip)
                    nc.sync.dma_start(fsb[:, 2 * par:2 * par + 1], f_scale[ds(2 * ip + par, 1), :]).then_inc(wl[par].s, 16)
                    nc.sync.dma_start(fsb[:, 2 * par + 1:2 * par + 2], f_bias[ds(2 * ip + par, 1), :]).then_inc(wl[par].s, 16)
                    nc.tensor.wait_ge(wl[par].s, 48 * (ip + 1))
                    nc.tensor.wait_ge(ad[par].s, NTW * ip)
                    for tw in range(NTW):
                        for dc in range(DC):
                            mmi = nc.tensor.matmul(pps[par][tw][:, :],
                                             fw[:, par * DC * 128 + dc * 128: par * DC * 128 + (dc + 1) * 128],
                                             ln_buf[:, dc * T + tw * TW: dc * T + tw * TW + TW],
                                             start=(dc == 0), stop=(dc == DC - 1))
                    mmi.then_inc(mm.s, 1)
                    nc.vector.wait_ge(mm.s, 2 * ip + par + 1)
                    nc.vector.wait_ge(ad[par].s, NTW * ip)
                    nc.vector.wait_ge(wl[par].s, 48 * (ip + 1))
                    for tw in range(NTW):
                        nc.vector.tensor_scalar(fst[:, par * T + tw * TW: par * T + (tw + 1) * TW],
                                                pps[par][tw][:, :],
                                                fsb[:, 2 * par:2 * par + 1],
                                                fsb[:, 2 * par + 1:2 * par + 2],
                                                ALU.mult, ALU.add).then_inc(ep[par].s, 1)
                    nc.vector.wait_ge(ep[par].s, (ip + 1) * NTW)
                    for tw in range(NTW):
                        nc.vector.tensor_add(X[:, ds((2 * ip + par) * T + tw * TW, TW)],
                                             X[:, ds((2 * ip + par) * T + tw * TW, TW)],
                                             fst[:, par * T + tw * TW: par * T + (tw + 1) * TW]).then_inc(ad[par].s, 1)
            nc.sync.wait_ge(ad[0].s, NTW * (DC // 2))
            nc.sync.wait_ge(ad[1].s, NTW * (DC // 2))
        barrier()

    def emit_rotary(sp):
        with (
            nc.sbuf_tensor(tn("rin"), [64, T], BF16) as rin,
            nc.sbuf_tensor(tn("rtm"), [128, NJ * 64], BF16) as rtm,
            nc.sbuf_tensor(tn("rt2"), [128, NJ * 64], BF16) as rt2,
            nc.sbuf_tensor(tn("rt3"), [128, NJ * 64], BF16) as rt3,
            nc.sbuf_tensor(tn("rsin"), [128, NJ * 64], BF16) as rsin,
            nc.sbuf_tensor(tn("rcos"), [128, NJ * 64], BF16) as rcos,
            nc.sbuf_tensor(tn("rout"), [64, T], BF16) as rout,
            nc.psum_tensor(tn("prta"), [128, NJ * 64], BF16) as prta,
            nc.psum_tensor(tn("prtb"), [64, NJ * 128], BF16) as prtb,
        ):
            ld = ctr("rld"); tra = ctr("rtra"); eva = ctr("reva"); mth = ctr("rmth")
            trb = ctr("rtrb"); evb = ctr("revb"); sto = ctr("rsto")
            inc(nc.sync.dma_start(rsin[:, :], rot_sin_in.ap().rearrange("(t p) r -> p t r", p=128)), ld, 16)
            inc(nc.sync.dma_start(rcos[:, :], rot_cos_in.ap().rearrange("(t p) r -> p t r", p=128)), ld, 16)
            for h in range(HB):
                nc.sync.wait_ge(trb.s, trb.n)
                inc(nc.sync.dma_start(rin[:, :], sp[h * 256: h * 256 + 64, :]), ld, 16)
                nc.tensor.wait_ge(ld.s, ld.n)
                nc.tensor.wait_ge(eva.s, eva.n)
                for tt in range(NJ):
                    inc(nc.tensor.transpose(prta[:, tt * 64:(tt + 1) * 64],
                                            rin[:, tt * 128:(tt + 1) * 128], identb[0:64, 0:64]), tra)
                nc.vector.wait_ge(tra.s, tra.n)
                nc.vector.wait_ge(mth.s, mth.n)
                inc(nc.vector.tensor_copy(rtm[:, :], prta[:, :]), eva)
                nc.vector.wait_ge(eva.s, eva.n)
                r4a = rtm.ap().rearrange("p (u v) -> p u v", v=2)
                r4b = rt2.ap().rearrange("p (u v) -> p u v", v=2)
                inc(nc.vector.tensor_copy(r4b[:, :, 0:1], r4a[:, :, 1:2]), mth)
                nc.vector.wait_ge(mth.s, mth.n)
                inc(nc.vector.tensor_copy(r4b[:, :, 1:2], r4a[:, :, 0:1]), mth)
                nc.vector.wait_ge(mth.s, mth.n)
                inc(nc.vector.tensor_mul(rt3[:, :], rtm[:, :], rcos[:, :]), mth)
                nc.vector.wait_ge(mth.s, mth.n)
                inc(nc.vector.tensor_mul(rt2[:, :], rt2[:, :], rsin[:, :]), mth)
                nc.vector.wait_ge(mth.s, mth.n)
                inc(nc.vector.tensor_add(rtm[:, :], rt3[:, :], rt2[:, :]), mth)
                nc.tensor.wait_ge(mth.s, mth.n)
                nc.tensor.wait_ge(evb.s, evb.n)
                for tt in range(NJ):
                    inc(nc.tensor.transpose(prtb[:, tt * 128:(tt + 1) * 128],
                                            rtm[:, tt * 64:(tt + 1) * 64], identb[:, :]), trb)
                nc.vector.wait_ge(trb.s, trb.n)
                nc.vector.wait_ge(sto.s, sto.n)
                inc(nc.vector.tensor_copy(rout.ap().rearrange("p (t q) -> p t q", q=128),
                                          prtb.ap().rearrange("p (t q) -> p t q", q=128)), evb)
                nc.scalar.wait_ge(evb.s, evb.n)
                inc(nc.scalar.dma_start(sp[h * 256: h * 256 + 64, :], rout[:, :]), sto, 16)
            nc.sync.wait_ge(sto.s, sto.n)
        barrier()

    def emit_to_token_major(src_ap, dst, nrc):
        with (
            nc.sbuf_tensor(tn("tmi"), [128, 2 * T], BF16) as tmi,
            nc.sbuf_tensor(tn("tmo"), [128, 2 * NJ * 128], BF16) as tmo,
            nc.psum_tensor(tn("ptm0"), [128, 512], BF16) as ptm0,
            nc.psum_tensor(tn("ptm1"), [128, 512], BF16) as ptm1,
        ):
            ld = [ctr("tml0"), ctr("tml1")]
            tr = ctr("tmt"); ev = ctr("tme"); st = [ctr("tms0"), ctr("tms1")]
            ptms = [ptm0, ptm1]
            for dc in range(nrc):
                par = dc % 2
                nc.sync.wait_ge(tr.s, max(0, tr.n - NJ))
                inc(nc.sync.dma_start(tmi[:, par * T:(par + 1) * T], src_ap[dc * 128:(dc + 1) * 128, :]), ld[par], 16)
                nc.tensor.wait_ge(ld[par].s, ld[par].n)
                nc.tensor.wait_ge(ev.s, ev.n)
                for tt in range(NJ):
                    pt = ptms[tt % 2]
                    inc(nc.tensor.transpose(pt[:, (tt // 2) * 128 % 512: (tt // 2) * 128 % 512 + 128],
                                            tmi[:, par * T + tt * 128: par * T + (tt + 1) * 128],
                                            identb[:, :]), tr)
                nc.vector.wait_ge(tr.s, tr.n)
                nc.vector.wait_ge(st[par].s, st[par].n)
                for tt in range(NJ):
                    pt = ptms[tt % 2]
                    inc(nc.vector.tensor_copy(tmo[:, par * NJ * 128 + tt * 128: par * NJ * 128 + (tt + 1) * 128],
                                              pt[:, (tt // 2) * 128 % 512: (tt // 2) * 128 % 512 + 128]), ev)
                nc.scalar.wait_ge(ev.s, ev.n)
                inc(nc.scalar.dma_start(
                    dst.rearrange("(t p) c -> p t c", p=128)[:, :, dc * 128:(dc + 1) * 128],
                    tmo[:, par * NJ * 128:(par + 1) * NJ * 128].rearrange("p (t q) -> p t q", q=128),
                ), st[par], 16)
            nc.sync.wait_ge(st[0].s, st[0].n)
            nc.sync.wait_ge(st[1].s, st[1].n)
        barrier()

    def emit_gather_async(name, in_ap, out_ap):
        gs = rctr("g_" + name)
        inc(nc.gpsimd.collective_compute("AllGather", ALU.bypass, replica_groups=RG,
                                         ins=[in_ap], outs=[out_ap]), gs)
        gsems[name] = gs

    def slot_block(s):
        return s % NC, s // NC

    def emit_attn_group(g, is_a, OT, KT=None, VO=None):
        HD = 128 if is_a else 256
        NDC = HD // 128
        NH = HA if is_a else HB
        scale = C.scale_a if is_a else C.scale_b
        PVW = HD + 1
        slots1 = [0]
        slots2 = list(range(8 * (SPG * g + SPG - 1) + 8))
        geo = {}
        for s in slots2:
            jmin = max(SPG * g, s // 8)
            w = (SPG * g + SPG - jmin) * 128
            qoff = (jmin - SPG * g) * 128
            bdry = (s // 8) >= SPG * g
            geo[s] = (jmin, w, qoff, bdry)
        b1 = [slots1[i:i + 2] for i in range(0, len(slots1), 2)]
        b2 = [slots2[i:i + 2] for i in range(0, len(slots2), 2)]
        from contextlib import ExitStack
        with ExitStack() as _es:
            def A(g_):
                return _es.enter_context(g_)
            qT = A(nc.sbuf_tensor(tn("qT"), [128, 2 * NDC * GQ], BF16))
            acc = A(nc.sbuf_tensor(tn("acc"), [128, GQ], BF16))
            accb = acc
            crowf = A(nc.sbuf_tensor(tn("crowf"), [1, GQ], F32))
            slqr = A(nc.sbuf_tensor(tn("slqr"), [1, GQ], F32))
            ccol = A(nc.sbuf_tensor(tn("ccol"), [128, SPG], BF16))
            maxB2 = A(nc.sbuf_tensor(tn("maxB2"), [128, 1024], F32))
            stg = A(nc.sbuf_tensor(tn("stg"), [128, 2 * 1024], F32))
            PT = A(nc.sbuf_tensor(tn("PT"), [128, 2 * 1024], BF16))
            alp = A(nc.sbuf_tensor(tn("alp"), [128, 2 * NSLOT], F32)) if is_a else None
            rcp = A(nc.sbuf_tensor(tn("rcp"), [128, SPG], F32))
            osb2 = A(nc.sbuf_tensor(tn("osb2"), [128, SPG * HD], BF16))
            ktb = A(nc.sbuf_tensor(tn("ktb"), [128, 8 * NDC * 128], BF16)) if not is_a else None
            vob = A(nc.sbuf_tensor(tn("vob"), [128, 8 * 260], BF16)) if not is_a else None
            psA = A(nc.psum_tensor(tn("psA"), [128, 1024], F32))
            psB = A(nc.psum_tensor(tn("psB"), [128, 1024], F32))
            pO_guards = [nc.psum_tensor(tn(f"pO{j2}"), [128, 512], F32) for j2 in range(SPG)]
            pOs = [gd.__enter__() for gd in pO_guards]
            pasbs = [pOs[k].ap()[:, 0:64].bitcast(BF16) for k in range(SPG)]
            pasb = pasbs[0]
            pbcv = pOs[0].ap()[:, 0:GQ]
            pss = [psA, psB]
            qld = [ctr("aql0"), ctr("aql1")]; ald = [ctr("aal0"), ctr("aal1")]; sld = [ctr("asl0"), ctr("asl1")]
            kvl = [ctr(f"akl{i}") for i in range(8)]
            m1 = ctr("am1"); v1 = ctr("av1"); ca = ctr("aca"); tr1 = ctr("atr")
            rd1 = ctr("ard"); rw1 = ctr("arw"); bcx = ctr("abc"); mbe = ctr("amb")
            m2 = ctr("am2"); v2 = ctr("av2"); ex = ctr("aex"); msk = ctr("ams")
            pv = ctr("apv"); rc = ctr("arc"); oe = ctr("aoe"); otr = ctr("aot"); ovv = ctr("aov")
            if not is_a:
                nc.vector.memset(vob[:, :], 1.0)
                vm0 = ctr("avm")
                inc(nc.vector.engine_nop(), vm0)
                nc.sync.wait_ge(vm0.s, 1)
                nc.sync.wait_ge(gsems["ktb"].s, gsems["ktb"].n)
                nc.sync.wait_ge(gsems["vb"].s, gsems["vb"].n)
            m2_end = []; ex_end = []; rw1_end = []
            kv_m_end = {}
            kv_pv_end = {}
            for ih in range(NH):
                hp = ih % 2
                if ih >= 2:
                    nc.sync.wait_ge(m2.s, m2_end[ih - 2])
                if is_a:
                    inc(nc.sync.dma_start(qT[:, hp * GQ:(hp + 1) * GQ],
                                          qkv_sp[ih * 128:(ih + 1) * 128, g * GQ:(g + 1) * GQ]), qld[hp], 16)
                    if ih >= 2:
                        nc.sync.wait_ge(ex.s, ex_end[ih - 2])
                    inc(nc.sync.dma_start(alp[:, hp * NSLOT:(hp + 1) * NSLOT],
                                          alibi_post_t[:, ih * NSLOT:(ih + 1) * NSLOT]), ald[hp], 16)
                    if ih >= 1:
                        nc.sync.wait_ge(rw1.s, rw1_end[ih - 1])
                    inc(nc.sync.dma_start(slqr[0:1, :],
                                          slq_in[0:1, ih * T + g * GQ: ih * T + g * GQ + GQ]), sld[hp], 16)
                else:
                    inc(nc.sync.dma_start(qT[:, hp * NDC * GQ: hp * NDC * GQ + GQ],
                                          qb_sp[ih * 256: ih * 256 + 128, g * GQ:(g + 1) * GQ]), qld[hp], 16)
                    inc(nc.sync.dma_start(qT[:, hp * NDC * GQ + GQ: hp * NDC * GQ + 2 * GQ],
                                          qb_sp[ih * 256 + 128: ih * 256 + 256, g * GQ:(g + 1) * GQ]), qld[hp], 16)
                qbase = hp * NDC * GQ if not is_a else hp * GQ
                qld_h = qld[hp].n
                ald_h = ald[hp].n
                sld_h = sld[hp].n
                s = 0
                if not is_a:
                    kb = 0
                    cc, jj = slot_block(s)
                    if kb in kv_m_end:
                        nc.sync.wait_ge(m1.s if kv_m_end[kb][0] == 1 else m2.s, kv_m_end[kb][1])
                    for half in range(NDC):
                        inc(nc.sync.dma_start(
                            ktb[:, kb * NDC * 128 + half * 128: kb * NDC * 128 + (half + 1) * 128],
                            ktb_g[cc * D + ih * 256 + half * 128: cc * D + ih * 256 + (half + 1) * 128,
                                  jj * 128:(jj + 1) * 128]), kvl[kb], 16)
                nc.tensor.wait_ge(qld[hp].s, qld_h)
                nc.tensor.wait_ge(oe.s, oe.n)
                nc.tensor.wait_ge(ovv.s, ovv.n)
                if is_a:
                    mmi = nc.tensor.matmul(pOs[0][:, 0:GQ], KT[:, 0:128],
                                           qT[:, qbase:qbase + GQ], start=True, stop=True)
                else:
                    kb = 0
                    nc.tensor.wait_ge(kvl[kb].s, kvl[kb].n)
                    for half in range(NDC):
                        mmi = nc.tensor.matmul(pOs[0][:, 0:GQ],
                                               ktb[:, kb * NDC * 128 + half * 128: kb * NDC * 128 + (half + 1) * 128],
                                               qT[:, qbase + half * GQ: qbase + half * GQ + GQ],
                                               start=(half == 0), stop=(half == NDC - 1))
                    kv_m_end[kb] = (1, m1.n + 1)
                inc(mmi, m1)
                nc.vector.wait_ge(m1.s, m1.n)
                nc.vector.wait_ge(tr1.s, tr1.n)
                inc(nc.vector.tensor_add(acc[:, :], pOs[0][:, 0:GQ], p1m[:, :]), v1)
                def emit_assembly():
                    nc.vector.wait_ge(v1.s, v1.n)
                    inc(nc.vector.engine_nop(), ca)
                    nc.tensor.wait_ge(ca.s, ca.n)
                    nc.tensor.wait_ge(rw1.s, rw1.n)
                    for j2 in range(SPG):
                        inc(nc.tensor.transpose(pasbs[j2][:, :], accb[:, j2 * 128:(j2 + 1) * 128], identb[:, :]), tr1)
                    nc.vector.wait_ge(tr1.s, tr1.n)
                    for j2 in range(SPG):
                        inc(nc.vector.reduce_max(ccol[:, j2:j2 + 1], pasbs[j2][:, :], AX.X), rd1)
                    crow = accb
                    nc.tensor.wait_ge(rd1.s, rd1.n)
                    for j2 in range(SPG):
                        inc(nc.tensor.transpose(pasbs[j2][0:1, :], ccol[:, j2:j2 + 1], identb[:, :]), tr1)
                    nc.vector.wait_ge(tr1.s, tr1.n)
                    for j2 in range(SPG):
                        inc(nc.vector.tensor_copy(crow[0:1, j2 * 128:(j2 + 1) * 128], pasbs[j2][0:1, :]), rw1)
                    nc.vector.wait_ge(rw1.s, rw1.n)
                    nc.vector.wait_ge(bcx.s, bcx.n)
                    if is_a:
                        nc.vector.wait_ge(sld[hp].s, sld_h)
                        inc(nc.vector.tensor_add(crowf[0:1, :], crow[0:1, :], slqr[0:1, :]), rw1)
                    else:
                        inc(nc.vector.tensor_copy(crowf[0:1, :], crow[0:1, :]), rw1)
                    rw1_end.append(rw1.n)
                    nc.tensor.wait_ge(rw1.s, rw1.n)
                    nc.tensor.wait_ge(mbe.s, mbe.n)
                    inc(nc.tensor.matmul(pbcv, onesr[0:1, :], crowf[0:1, :],
                                             start=True, stop=True), bcx)
                    nc.vector.wait_ge(bcx.s, bcx.n)
                    inc(nc.vector.tensor_copy(maxB2[:, 0:GQ], pbcv), mbe)
                    nc.vector.wait_ge(mbe.s, mbe.n)
                    inc(nc.vector.tensor_copy(maxB2[:, 512:512 + GQ], maxB2[:, 0:GQ]), mbe)
                NB = len(b2)

                def emit_mm2(bi):
                    par = bi % 2
                    ps = pss[par]
                    bb = b2[bi]
                    jmin, w, qoff, bdry = geo[bb[0]]
                    if not is_a:
                        for i, s in enumerate(bb):
                            kb = s % 8
                            cc, jj = slot_block(s)
                            if kb in kv_m_end:
                                nc.sync.wait_ge(m1.s if kv_m_end[kb][0] == 1 else m2.s, kv_m_end[kb][1])
                            for half in range(NDC):
                                inc(nc.sync.dma_start(
                                    ktb[:, kb * NDC * 128 + half * 128: kb * NDC * 128 + (half + 1) * 128],
                                    ktb_g[cc * D + ih * 256 + half * 128: cc * D + ih * 256 + (half + 1) * 128,
                                          jj * 128:(jj + 1) * 128]), kvl[kb], 16)
                            if kb in kv_pv_end:
                                nc.sync.wait_ge(pv.s, kv_pv_end[kb])
                            inc(nc.sync.dma_start(vob[:, kb * 260: kb * 260 + 256],
                                                  vb_g[cc * T + jj * 128: cc * T + jj * 128 + 128,
                                                       ih * 256:(ih + 1) * 256]), kvl[kb], 16)
                    if bi == 0 and ih > 0:
                        nc.tensor.wait_ge(v2.s, v2.n)
                    if bi >= 2:
                        nc.tensor.wait_ge(v2.s, v2_mark[bi - 2])
                    for i, s in enumerate(bb):
                        if is_a:
                            mmi = nc.tensor.matmul(ps[:, i * 512: i * 512 + w], KT[:, s * 128:(s + 1) * 128],
                                                   qT[:, qbase + qoff: qbase + qoff + w], start=True, stop=True)
                        else:
                            kb = s % 8
                            nc.tensor.wait_ge(kvl[kb].s, kvl[kb].n)
                            for half in range(NDC):
                                mmi = nc.tensor.matmul(ps[:, i * 512: i * 512 + w],
                                                       ktb[:, kb * NDC * 128 + half * 128: kb * NDC * 128 + (half + 1) * 128],
                                                       qT[:, qbase + half * GQ + qoff: qbase + half * GQ + qoff + w],
                                                       start=(half == 0), stop=(half == NDC - 1))
                            kv_m_end[kb] = (2, m2.n + 1)
                    inc(mmi, m2)

                def emit_subexp(bi):
                    par = bi % 2
                    ps = pss[par]
                    bb = b2[bi]
                    jmin, w, qoff, bdry = geo[bb[0]]
                    nb_ = len(bb)
                    nc.vector.wait_ge(m2.s, m2_mark[bi])
                    if bi == 0:
                        nc.vector.wait_ge(mbe.s, mbe.n)
                    if bi >= 2:
                        nc.vector.wait_ge(ex.s, ex_hist[bi - 2])
                    inc(nc.vector.tensor_tensor(
                        stg.ap().rearrange("p (b s q) -> p b s q", b=2, s=2)[:, par, 0:nb_, qoff:qoff + w],
                        ps.ap().rearrange("p (s q) -> p s q", s=2)[:, 0:nb_, 0:w],
                        maxB2.ap().rearrange("p (s q) -> p s q", s=2)[:, 0:nb_, qoff:qoff + w],
                        ALU.subtract), v2)
                    v2_mark[bi] = v2.n
                    nc.scalar.wait_ge(v2.s, v2.n)
                    if is_a and bi == 0:
                        nc.scalar.wait_ge(ald[hp].s, ald_h)
                    if bi >= 2:
                        nc.scalar.wait_ge(pv.s, pv_hist[bi - 2])
                    for i, s in enumerate(bb):
                        if is_a:
                            inc(nc.scalar.activation(PT[:, par * 1024 + i * 512 + qoff: par * 1024 + i * 512 + qoff + w],
                                                     stg[:, par * 1024 + i * 512 + qoff: par * 1024 + i * 512 + qoff + w],
                                                     AF.Exp, bias=alp[:, hp * NSLOT + s: hp * NSLOT + s + 1],
                                                     scale=scale), ex)
                        else:
                            inc(nc.scalar.activation(PT[:, par * 1024 + i * 512 + qoff: par * 1024 + i * 512 + qoff + w],
                                                     stg[:, par * 1024 + i * 512 + qoff: par * 1024 + i * 512 + qoff + w],
                                                     AF.Exp, bias=0.0, scale=scale), ex)
                    ex_hist[bi] = ex.n

                def emit_masks(bi):
                    par = bi % 2
                    bb = b2[bi]
                    for i, s in enumerate(bb):
                        if geo[s][3]:
                            nc.vector.wait_ge(ex.s, ex_hist[bi])
                            _, w, qoff, _bd = geo[s]
                            dp_ = s % 8
                            inc(nc.vector.tensor_tensor(
                                PT.ap().bitcast(U16)[:, par * 1024 + i * 512 + qoff: par * 1024 + i * 512 + qoff + 128],
                                PT.ap().bitcast(U16)[:, par * 1024 + i * 512 + qoff: par * 1024 + i * 512 + qoff + 128],
                                amask[:, dp_ * 128:(dp_ + 1) * 128], ALU.bitwise_and), msk)
                    msk_hist[bi] = msk.n

                def emit_pv2(bi):
                    par = bi % 2
                    bb = b2[bi]
                    nc.tensor.wait_ge(ex.s, ex_hist[bi])
                    nc.tensor.wait_ge(msk.s, msk_hist[bi])
                    if bb[0] == 0:
                        nc.tensor.wait_ge(mbe.s, mbe.n)
                    for i_, s_ in enumerate(bb):
                        jm_, w_, qoff_, _b = geo[s_]
                        for j_ in range(jm_, SPG * g + SPG):
                            j2_ = j_ - SPG * g
                            rhs_ = vob[:, (s_ % 8) * 260: (s_ % 8) * 260 + PVW] if not is_a else VO[:, s_ * 129: s_ * 129 + PVW]
                            mmi_ = nc.tensor.matmul(pOs[j2_][:, 0:PVW],
                                                    PT[:, par * 1024 + i_ * 512 + qoff_ + (j_ - jm_) * 128:
                                                       par * 1024 + i_ * 512 + qoff_ + (j_ - jm_ + 1) * 128],
                                                    rhs_, start=(s_ == 0), stop=(s_ == 8 * j_ + 7))
                            inc(mmi_, pv)
                        if not is_a:
                            kv_pv_end[s_ % 8] = pv.n
                    pv_hist[bi] = pv.n

                m2_mark = [0] * NB
                msk_hist = [0] * NB
                v2_mark = [0] * NB
                ex_hist = [0] * NB
                pv_hist = [0] * NB
                for k in range(NB + 3):
                    if k < NB:
                        emit_mm2(k)
                        m2_mark[k] = m2.n
                    if k == 1:
                        emit_assembly()
                    if 0 <= k - 3 < NB:
                        emit_pv2(k - 3)
                    if 0 <= k - 1 < NB:
                        emit_subexp(k - 1)
                    if 0 <= k - 2 < NB:
                        emit_masks(k - 2)
                m2_end.append(m2.n)
                ex_end.append(ex.n)
                nc.vector.wait_ge(pv.s, pv.n)
                for j2 in range(SPG):
                    inc(nc.vector.reciprocal(rcp[:, j2:j2 + 1], pOs[j2][:, HD:HD + 1]), rc)
                nc.scalar.wait_ge(rc.s, rc.n)
                nc.scalar.wait_ge(otr.s, otr.n)
                for j2 in range(SPG):
                    inc(nc.scalar.activation(osb2[:, j2 * HD:(j2 + 1) * HD], pOs[j2][:, 0:HD], AF.Copy,
                                             bias=0.0, scale=rcp[:, j2:j2 + 1]), oe)
                nc.tensor.wait_ge(oe.s, oe.n)
                tiles = [(j2, half) for j2 in range(SPG) for half in range(NDC)]
                for r0 in range(0, len(tiles), SPG):
                    nc.tensor.wait_ge(ovv.s, ovv.n)
                    for bk, (j2, half) in enumerate(tiles[r0:r0 + SPG]):
                        inc(nc.tensor.transpose(pasbs[bk][:, :],
                                                osb2[:, j2 * HD + half * 128: j2 * HD + (half + 1) * 128],
                                                identb[:, :]), otr)
                    nc.vector.wait_ge(otr.s, otr.n)
                    for bk, (j2, half) in enumerate(tiles[r0:r0 + SPG]):
                        dcx = ih * NDC + half
                        inc(nc.vector.tensor_copy(OT[:, dcx * GQ + j2 * 128: dcx * GQ + (j2 + 1) * 128],
                                                  pasbs[bk][:, :]), ovv)
            nc.sync.wait_ge(ovv.s, ovv.n)
            for gd in reversed(pO_guards):
                gd.__exit__(None, None, None)
        barrier()

    with nc.sbuf_tensor(tn("ln1"), [128, DC * T], BF16) as ln_buf:
        emit_ln(ln_buf)
        emit_proj_spill("qkv", OA, qkv_sp, ln_buf)
    emit_gather_async("kta", qkv_sp.ap()[DC * 128:(DC + 1) * 128, :], kta_g.ap())
    emit_to_token_major(qkv_sp.ap()[(DC + 1) * 128:(DC + 2) * 128, :], va_loc.ap(), 1)
    emit_gather_async("va", va_loc.ap(), va_g.ap())
    with (
        nc.sbuf_tensor(tn("KTa"), [128, NSLOT * 128], BF16) as KTa,
        nc.sbuf_tensor(tn("VOa"), [128, NSLOT * 130], BF16) as VOa,
        nc.sbuf_tensor(tn("OTa"), [128, DC * GQ], BF16) as OTa,
    ):
        kvc = ctr("kvc")
        vmc = ctr("vmc")
        inc(nc.vector.memset(VOa[:, :], 1.0), vmc)
        nc.sync.wait_ge(vmc.s, 1)
        nc.sync.wait_ge(gsems["kta"].s, gsems["kta"].n)
        nc.sync.wait_ge(gsems["va"].s, gsems["va"].n)
        for s in range(NSLOT):
            cc, jj = slot_block(s)
            inc(nc.sync.dma_start(KTa[:, s * 128:(s + 1) * 128],
                                  kta_g[cc * 128:(cc + 1) * 128, jj * 128:(jj + 1) * 128]), kvc, 16)
            inc(nc.sync.dma_start(VOa[:, s * 129: s * 129 + 128],
                                  va_g[cc * T + jj * 128: cc * T + jj * 128 + 128, :]), kvc, 16)
        nc.sync.wait_ge(kvc.s, kvc.n)
        barrier()
        for g in range(NG):
            emit_attn_group(g, True, OTa, KT=KTa, VO=VOa)
            emit_wo_resid("oa", OTa, g)

    if n_sub >= 2:
        emit_ffn()

    if n_sub >= 3:
        with nc.sbuf_tensor(tn("ln3"), [128, DC * T], BF16) as ln_buf:
            emit_ln(ln_buf)
            emit_proj_spill("q", DC, qb_sp, ln_buf)
            emit_proj_spill("k", DC, ktb_loc, ln_buf)
            emit_proj_spill("v", DC, vtb_sp, ln_buf)
        emit_rotary(ktb_loc)
        emit_gather_async("ktb", ktb_loc.ap(), ktb_g.ap())
        emit_rotary(qb_sp)
        emit_to_token_major(vtb_sp.ap(), vb_loc.ap(), DC)
        emit_gather_async("vb", vb_loc.ap(), vb_g.ap())
        with nc.sbuf_tensor(tn("OTb"), [128, DC * GQ], BF16) as OTb:
            for g in range(NG):
                emit_attn_group(g, False, OTb)
                emit_wo_resid("ob", OTb, g)

    if n_sub >= 4:
        emit_ffn()

    so = ctr("so")
    for dc in range(DC):
        inc(nc.scalar.dma_start(outT[dc * 128:(dc + 1) * 128, :], X[:, dc * T:(dc + 1) * T]), so, 16)
    nc.scalar.wait_ge(so.s, so.n)
    for p in reversed(persist):
        p.__exit__(None, None, None)
    return nc


def kernel(L0_attn_a_hidden_states, L0_attn_a_Wqkv, L0_attn_a_Wo,
           L0_attn_b_Wq, L0_attn_b_Wk, L0_attn_b_Wv, L0_attn_b_Wo,
           L0_ffn_int8_weight, L0_ffn_scale, L0_ffn_bias, _cfg=None, _n_sub=4):
    cfg = _cfg or Cfg()
    NCn, D, T, NJ, DC = cfg.NC, cfg.D, cfg.T, cfg.NJ, cfg.DC
    x = np.asarray(L0_attn_a_hidden_states).reshape(cfg.SEQ, D)
    wqkv = np.ascontiguousarray(np.asarray(L0_attn_a_Wqkv, dtype=np.float32))
    woa = np.ascontiguousarray(np.asarray(L0_attn_a_Wo, dtype=np.float32))
    wq = np.ascontiguousarray(np.asarray(L0_attn_b_Wq, dtype=np.float32))
    wk = np.ascontiguousarray(np.asarray(L0_attn_b_Wk, dtype=np.float32))
    wv = np.ascontiguousarray(np.asarray(L0_attn_b_Wv, dtype=np.float32))
    wob = np.ascontiguousarray(np.asarray(L0_attn_b_Wo, dtype=np.float32))
    w8 = np.ascontiguousarray(np.asarray(L0_ffn_int8_weight))
    fs = np.asarray(L0_ffn_scale, dtype=np.float32).reshape(D)
    fb = np.asarray(L0_ffn_bias, dtype=np.float32).reshape(D)
    fs_t = np.ascontiguousarray(fs.reshape(DC, 128))
    fb_t = np.ascontiguousarray(fb.reshape(DC, 128))

    nc = build_program(cfg, n_sub=_n_sub)
    in_maps = []
    RSL = D // NCn
    for c in range(NCn):
        tks = np.concatenate([np.arange(128) + (8 * j + c) * 128 for j in range(NJ)])
        xc = np.ascontiguousarray(x[tks, :].T.astype(np.float32))
        t = host_tables(cfg, c)
        rsl = slice(c * RSL, (c + 1) * RSL)
        in_maps.append({
            "xT": xc, "w_qkv": np.ascontiguousarray(wqkv[rsl]), "w_oa": np.ascontiguousarray(woa[rsl]),
            "w_q": np.ascontiguousarray(wq[rsl]), "w_k": np.ascontiguousarray(wk[rsl]),
            "w_v": np.ascontiguousarray(wv[rsl]),
            "w_ob": np.ascontiguousarray(wob[rsl]), "w_f8": np.ascontiguousarray(w8[:, rsl]),
            "f_scale": fs_t, "f_bias": fb_t,
            "alibi_raw": t["alibi_raw"], "alibi_post": t["alibi_post"],
            "p1mask": t["p1mask"], "andmask": t["andmask"],
            "rot_sin": t["rot_sin"], "rot_cos": t["rot_cos"],
            "ident_bf": t["ident_bf"],
            "ones_f32": t["ones_f32"], "ones_row": t["ones_row"], "ones_row_bf": t["ones_row_bf"], "slq": t["slq"],
        })
    res = run_bass_kernel_spmd(nc, in_maps, core_ids=list(range(NCn)))
    out = np.zeros((cfg.SEQ, D), dtype=np.float32)
    for c in range(NCn):
        tks = np.concatenate([np.arange(128) + (8 * j + c) * 128 for j in range(NJ)])
        out[tks, :] = np.asarray(res.results[c]["outT"]).T
    return out.reshape(1, cfg.SEQ, D)


# version 19 (final)
# speedup vs baseline: 1.3250x; 1.2047x over previous
import math
import numpy as np
import ml_dtypes
import concourse.bass as bass
import concourse.mybir as mybir
from concourse.bass_utils import run_bass_kernel_spmd

F32 = mybir.dt.float32
BF16 = mybir.dt.bfloat16
U16 = mybir.dt.uint16
U32 = mybir.dt.uint32
I8 = mybir.dt.int8
AF = mybir.ActivationFunctionType
ALU = mybir.AluOpType
AX = mybir.AxisListType
ds = bass.ds

NEG = -1.0e30


class Cfg:
    def __init__(self, D=4096, SEQ=8192):
        self.NC = 8
        self.D = D
        self.SEQ = SEQ
        self.DC = D // 128
        self.T = SEQ // self.NC
        self.NJ = self.T // 128
        self.SPG = min(4, self.NJ)
        self.NG = self.NJ // self.SPG
        self.GQ = self.SPG * 128
        self.HA = D // 128
        self.HB = D // 256
        self.OA = self.DC + 2
        self.NSLOT = SEQ // 128
        self.scale_a = 1.0 / math.sqrt(128.0)
        self.scale_b = 1.0 / 16.0
        self.ROT = 64


def alibi_slopes(n):
    cp2 = 2 ** int(math.floor(math.log2(n)))
    base = 2.0 ** (-(2.0 ** (-(math.log2(cp2) - 3))))
    slopes = base ** np.arange(1, cp2 + 1, dtype=np.float64)
    if cp2 != n:
        extra_base = 2.0 ** (-(2.0 ** (-(math.log2(2 * cp2) - 3))))
        rem = min(cp2, n - cp2)
        extra = extra_base ** np.arange(1, 1 + 2 * rem, 2, dtype=np.float64)
        slopes = np.concatenate([slopes, extra])
    return slopes.astype(np.float64)


def host_tables(cfg, core):
    t = {}
    sl = alibi_slopes(cfg.HA)
    k = (np.arange(cfg.NSLOT)[None, :] * 128 + np.arange(128)[:, None]).astype(np.float64)
    ali = k[:, None, :] * sl[None, :, None]
    ali = ali.reshape(128, cfg.HA * cfg.NSLOT)
    t["alibi_post"] = ali.astype(np.float32)
    t["alibi_raw"] = (ali / cfg.scale_a).astype(np.float32)
    m1 = np.zeros((128, cfg.GQ), np.float32)
    if core == 0:
        p = np.arange(128)[:, None]
        q = np.arange(128)[None, :]
        m1[:, 0:128] = np.where(p > q, np.float32(NEG), np.float32(0.0))
    t["p1mask"] = m1.astype(ml_dtypes.bfloat16)
    am = np.zeros((128, 8 * 128), np.uint16)
    p = np.arange(128)[:, None]
    q = np.arange(128)[None, :]
    for dpos in range(8):
        blk = am[:, dpos * 128:(dpos + 1) * 128]
        if dpos < core:
            blk[:] = 0xFFFF
        elif dpos == core:
            blk[:] = np.where(p <= q, 0xFFFF, 0).astype(np.uint16)
    t["andmask"] = am
    pos = np.zeros((cfg.T,), np.int64)
    for j in range(cfg.NJ):
        pos[j * 128:(j + 1) * 128] = (8 * j + core) * 128 + np.arange(128)
    inv = 1.0 / (10000.0 ** (np.arange(0, cfg.ROT, 2, dtype=np.float64) / cfg.ROT))
    ang = pos[:, None].astype(np.float64) * inv[None, :]
    t["rot_sin"] = np.repeat(np.sin(ang), 2, axis=1).astype(ml_dtypes.bfloat16)
    t["rot_cos"] = np.repeat(np.cos(ang), 2, axis=1).astype(ml_dtypes.bfloat16)
    t["ident_bf"] = np.eye(128, dtype=ml_dtypes.bfloat16)
    t["ident_f32"] = np.eye(128, dtype=np.float32)
    t["ones_f32"] = np.ones((128, 1), np.float32)
    t["ones_row"] = np.ones((1, 128), np.float32)
    t["ones_row_bf"] = np.ones((1, 128), ml_dtypes.bfloat16)
    slq = (sl[:, None] * pos[None, :].astype(np.float64) / cfg.scale_a).reshape(1, cfg.HA * cfg.T)
    t["slq"] = slq.astype(np.float32)
    return t


_CTR_ID = [0]
_TN_ID = [0]


def tn(name):
    _TN_ID[0] += 1
    return f"{name}_{_TN_ID[0]}"


class Ctr:
    def __init__(self, nc, name):
        _CTR_ID[0] += 1
        self.s = nc.alloc_semaphore(name=f"{name}_{_CTR_ID[0]}")
        self.n = 0


def inc(instr, c, k=1):
    instr.then_inc(c.s, k)
    c.n += k
    return instr


DBG = False
DBG_SLOT = 0


def build_program(cfg, n_sub=4):
    nc = bass.Bass("TRN2", target_bir_lowering=False, dynamic_dma_scratch_size=32)
    C = cfg
    NC, D, DC, T, NJ, SPG, NG, GQ = C.NC, C.D, C.DC, C.T, C.NJ, C.SPG, C.NG, C.GQ
    HA, HB, OA, NSLOT = C.HA, C.HB, C.OA, C.NSLOT
    TW = min(512, T)
    NTW = T // TW
    NTT = T // 128

    xT = nc.dram_tensor("xT", [D, T], F32, kind="ExternalInput")
    RSL = D // NC
    w_qkv = nc.dram_tensor("w_qkv", [RSL, OA * 128], F32, kind="ExternalInput")
    w_oa = nc.dram_tensor("w_oa", [RSL, D], F32, kind="ExternalInput")
    w_q = nc.dram_tensor("w_q", [RSL, D], F32, kind="ExternalInput")
    w_k = nc.dram_tensor("w_k", [RSL, D], F32, kind="ExternalInput")
    w_v = nc.dram_tensor("w_v", [RSL, D], F32, kind="ExternalInput")
    w_ob = nc.dram_tensor("w_ob", [RSL, D], F32, kind="ExternalInput")
    w_f8 = nc.dram_tensor("w_f8", [D, RSL], I8, kind="ExternalInput")
    f_scale = nc.dram_tensor("f_scale", [DC, 128], F32, kind="ExternalInput")
    f_bias = nc.dram_tensor("f_bias", [DC, 128], F32, kind="ExternalInput")
    alibi_raw_t = nc.dram_tensor("alibi_raw", [128, HA * NSLOT], F32, kind="ExternalInput")
    alibi_post_t = nc.dram_tensor("alibi_post", [128, HA * NSLOT], F32, kind="ExternalInput")
    p1mask_in = nc.dram_tensor("p1mask", [128, GQ], BF16, kind="ExternalInput")
    andmask_in = nc.dram_tensor("andmask", [128, 8 * 128], U16, kind="ExternalInput")
    rot_sin_in = nc.dram_tensor("rot_sin", [T, 64], BF16, kind="ExternalInput")
    rot_cos_in = nc.dram_tensor("rot_cos", [T, 64], BF16, kind="ExternalInput")
    ident_bf_in = nc.dram_tensor("ident_bf", [128, 128], BF16, kind="ExternalInput")
    ones_f32_in = nc.dram_tensor("ones_f32", [128, 1], F32, kind="ExternalInput")
    ones_row_in = nc.dram_tensor("ones_row", [1, 128], F32, kind="ExternalInput")
    ones_row_bf_in = nc.dram_tensor("ones_row_bf", [1, 128], BF16, kind="ExternalInput")
    slq_in = nc.dram_tensor("slq", [1, HA * T], F32, kind="ExternalInput")
    outT = nc.dram_tensor("outT", [D, T], F32, kind="ExternalOutput")
    dbg_acc = nc.dram_tensor("dbg_acc", [128, GQ], F32)
    dbg_maxB = nc.dram_tensor("dbg_maxB", [128, GQ], F32)
    dbg_stg = nc.dram_tensor("dbg_stg", [128, GQ], F32)
    dbg_PT = nc.dram_tensor("dbg_PT", [128, GQ], BF16)
    dbg_pO = nc.dram_tensor("dbg_pO", [128, 260], F32)

    cv_sl, cv_full = {}, {}
    for nm, ocols in (("qkv", OA * 128), ("oa", D), ("q", D), ("k", D), ("v", D), ("ob", D), ("ft", D)):
        cv_sl[nm] = nc.dram_tensor(f"cv_sl_{nm}", [D // NC, ocols], BF16)
        cv_full[nm] = nc.dram_tensor(f"cv_{nm}", [D, ocols], BF16)
    qkv_sp = nc.dram_tensor("qkv_sp", [OA * 128, T], BF16)
    va_loc = nc.dram_tensor("va_loc", [T, 128], BF16)
    kta_g = nc.dram_tensor("kta_g", [NC * 128, T], BF16)
    va_g = nc.dram_tensor("va_g", [NC * T, 128], BF16)
    qb_sp = nc.dram_tensor("qb_sp", [D, T], BF16)
    ktb_loc = nc.dram_tensor("ktb_loc", [D, T], BF16)
    vtb_sp = nc.dram_tensor("vtb_sp", [D, T], BF16)
    vb_loc = nc.dram_tensor("vb_loc", [T, D], BF16)
    ktb_g = nc.dram_tensor("ktb_g", [NC * D, T], BF16)
    vb_g = nc.dram_tensor("vb_g", [NC * T, D], BF16)
    RG = [list(range(NC))]
    gsems = {}

    persist = []
    def sbp(name, shape, dtype):
        g = nc.sbuf_tensor(name, shape, dtype)
        h = g.__enter__()
        persist.append(g)
        return h

    X = sbp("X", [128, DC * T], F32)
    identb = sbp("identb", [128, 128], BF16)
    onesf = sbp("onesf", [128, 1], F32)
    onesr = sbp("onesr", [1, 128], F32)
    onesrb = sbp("onesrb", [1, 128], BF16)
    amask = sbp("amask", [128, 8 * 128], U16)
    p1m = sbp("p1m", [128, GQ], BF16)
    eps_col = sbp("eps_col", [128, 1], F32)

    allh = [nc.alloc_semaphore(name=f"pool{i}") for i in range(nc.free_len())]
    allh.sort(key=lambda h: h.num)
    NRES = 14
    reserved = allh[:NRES]
    pool = allh[NRES:]
    pool_ids = sorted(h.num for h in pool)
    assert pool_ids == list(range(pool_ids[0], pool_ids[0] + len(pool))), "pool ids not contiguous"
    pool_range = range(pool_ids[0], pool_ids[0] + len(pool))
    free_sems = list(pool)
    used_sems = []

    def rctr(name):
        h = reserved.pop()
        c = Ctr.__new__(Ctr)
        c.s = h
        c.n = 0
        return c

    def ctr(name):
        h = free_sems.pop()
        used_sems.append(h)
        c = Ctr.__new__(Ctr)
        c.s = h
        c.n = 0
        return c

    def barrier():
        nc.all_engine_barrier()
        if used_sems:
            nc.gpsimd.dma_reset(pool_range)
            nc.gpsimd.sem_clear(pool_range)
            free_sems.extend(used_sems)
            used_sems.clear()
            nc.all_engine_barrier()

    s0 = ctr("s0")
    for dc in range(DC):
        inc(nc.sync.dma_start(X[:, dc * T:(dc + 1) * T], xT[dc * 128:(dc + 1) * 128, :]), s0, 16)
    for dst, src in ((identb, ident_bf_in), (onesf, ones_f32_in),
                     (onesr, ones_row_in), (onesrb, ones_row_bf_in),
                     (amask, andmask_in), (p1m, p1mask_in)):
        inc(nc.sync.dma_start(dst[:, :], src[:, :]), s0, 16)
    nc.sync.wait_ge(s0.s, s0.n)
    nc.vector.memset(eps_col[:, :], 1e-5)
    barrier()

    RS = D // NC
    NRT = RS // 128
    with (
        nc.sbuf_tensor("cwi", [128, 2 * (OA * 128)], F32) as cwi,
        nc.sbuf_tensor("cwo", [128, 2 * (OA * 128)], BF16) as cwo,
        nc.sbuf_tensor("ci8", [128, 2 * RS], I8) as ci8,
        nc.sbuf_tensor("cb16", [128, 2 * RS], BF16) as cb16,
        nc.sbuf_tensor("ctr8", [128, 2 * RS], BF16) as ctr8,
        nc.psum_tensor("ptr0", [128, 128], BF16) as ptr0,
        nc.psum_tensor("ptr1", [128, 128], BF16) as ptr1,
    ):
        ld = [ctr("cvl0"), ctr("cvl1")]
        cvs = [ctr("cvc0"), ctr("cvc1")]
        st = [ctr("cvs0"), ctr("cvs1")]
        for wi, (nm, src, ocols) in enumerate((("qkv", w_qkv, OA * 128), ("oa", w_oa, D),
                                               ("q", w_q, D), ("k", w_k, D), ("v", w_v, D), ("ob", w_ob, D))):
            for r in range(NRT):
                par = (wi * NRT + r) % 2
                nc.sync.wait_ge(st[par].s, st[par].n)
                inc(nc.sync.dma_start(cwi[:, par * (OA * 128): par * (OA * 128) + ocols],
                                      src[r * 128:(r + 1) * 128, :]), ld[par], 16)
                nc.vector.wait_ge(ld[par].s, ld[par].n)
                inc(nc.vector.tensor_copy(cwo[:, par * (OA * 128): par * (OA * 128) + ocols],
                                          cwi[:, par * (OA * 128): par * (OA * 128) + ocols]), cvs[par])
                nc.scalar.wait_ge(cvs[par].s, cvs[par].n)
                inc(nc.scalar.dma_start(cv_sl[nm][r * 128:(r + 1) * 128, :],
                                        cwo[:, par * (OA * 128): par * (OA * 128) + ocols]), st[par], 16)
        l8 = [ctr("l80"), ctr("l81")]
        c8 = [ctr("c80"), ctr("c81")]
        t8 = [ctr("t80"), ctr("t81")]
        e8 = [ctr("e80"), ctr("e81")]
        sp8 = [ctr("sp80"), ctr("sp81")]
        for oc in range(DC):
            par = oc % 2
            nc.sync.wait_ge(sp8[par].s, sp8[par].n)
            inc(nc.sync.dma_start(ci8[:, par * RS: par * RS + RS],
                                  w_f8[oc * 128:(oc + 1) * 128, :]), l8[par], 16)
            nc.vector.wait_ge(l8[par].s, l8[par].n)
            nc.vector.wait_ge(t8[par].s, t8[par].n)
            inc(nc.vector.tensor_copy(cb16[:, par * RS: par * RS + RS],
                                      ci8[:, par * RS: par * RS + RS]), c8[par])
            nc.tensor.wait_ge(c8[par].s, c8[par].n)
            ptr = ptr0 if par == 0 else ptr1
            for tt in range(NRT):
                nc.tensor.wait_ge(e8[par].s, e8[par].n)
                inc(nc.tensor.transpose(ptr[:, :], cb16[:, par * RS + tt * 128: par * RS + (tt + 1) * 128],
                                        identb[:, :]), t8[par])
                nc.vector.wait_ge(t8[par].s, t8[par].n)
                inc(nc.vector.tensor_copy(ctr8[:, par * RS + tt * 128: par * RS + (tt + 1) * 128],
                                          ptr[:, :]), e8[par])
            nc.scalar.wait_ge(e8[par].s, e8[par].n)
            inc(nc.scalar.dma_start(
                cv_sl["ft"].ap().rearrange("(t p) o -> p t o", p=128)[:, :, oc * 128:(oc + 1) * 128],
                ctr8[:, par * RS: par * RS + RS].rearrange("p (t q) -> p t q", q=128),
            ), sp8[par], 16)
        for par in range(2):
            nc.gpsimd.wait_ge(st[par].s, st[par].n)
            nc.gpsimd.wait_ge(sp8[par].s, sp8[par].n)
        for nm in ("qkv", "oa", "ft", "q", "k", "v", "ob"):
            gs = rctr("g_" + nm)
            inc(nc.gpsimd.collective_compute("AllGather", ALU.bypass, replica_groups=RG,
                                             ins=[cv_sl[nm].ap()], outs=[cv_full[nm].ap()]), gs)
            gsems[nm] = gs
    barrier()

    def emit_ln(ln_buf):
        with (
            nc.sbuf_tensor(tn("x2"), [128, 2 * T], F32) as x2,
            nc.sbuf_tensor(tn("stats"), [128, 2 * NTT], F32) as stats,
            nc.sbuf_tensor(tn("abrow"), [1, 2 * T], BF16) as abrow,
            nc.sbuf_tensor(tn("abB"), [128, 2 * T], BF16) as abB,
            nc.sbuf_tensor(tn("acol"), [128, 2 * NTT], F32) as acol,
            nc.sbuf_tensor(tn("acolb"), [128, 2 * NTT], BF16) as acolb,
            nc.sbuf_tensor(tn("lt"), [128, T], F32) as lt,
            nc.psum_tensor(tn("pstt"), [128, 2 * NTT * DC], F32) as pstt,
            nc.psum_tensor(tn("prow"), [1, 128], BF16) as prow,
            nc.psum_tensor(tn("pbc0"), [128, TW], F32) as pbc0,
            nc.psum_tensor(tn("pbc1"), [128, TW], F32) as pbc1,
        ):
            sq = ctr("lnsq"); mm = ctr("lnmm"); ev = ctr("lnev"); tr = ctr("lntr")
            rw = ctr("lnrw"); bc = ctr("lnbc"); nm_ = ctr("lnnm")
            for dc in range(DC):
                par = dc % 2
                if dc >= 2:
                    nc.scalar.wait_ge(mm.s, (dc - 1) * NTT)
                inc(nc.scalar.activation(x2[:, par * T:(par + 1) * T], X[:, dc * T:(dc + 1) * T],
                                         AF.Square), sq)
                nc.tensor.wait_ge(sq.s, sq.n)
                for tt in range(NTT):
                    nc.tensor.matmul(pstt[:, (2 * tt) * DC + dc:(2 * tt) * DC + dc + 1],
                                     X[:, dc * T + tt * 128: dc * T + (tt + 1) * 128],
                                     onesf[:, :], start=True, stop=True)
                    inc(nc.tensor.matmul(pstt[:, (2 * tt + 1) * DC + dc:(2 * tt + 1) * DC + dc + 1],
                                         x2[:, par * T + tt * 128: par * T + (tt + 1) * 128],
                                         onesf[:, :], start=True, stop=True), mm)
            nc.vector.wait_ge(mm.s, mm.n)
            inc(nc.vector.reduce_sum(stats[:, :], pstt.ap().rearrange("p (s d) -> p s d", d=DC), AX.X), ev)
            nc.vector.wait_ge(ev.s, ev.n)
            inc(nc.vector.tensor_scalar_mul(stats[:, :], stats[:, :], 1.0 / D), ev)
            nc.vector.wait_ge(ev.s, ev.n)
            inc(nc.vector.tensor_mul(acol[:, 0:2 * NTT:2], stats[:, 0:2 * NTT:2], stats[:, 0:2 * NTT:2]), ev)
            nc.vector.wait_ge(ev.s, ev.n)
            inc(nc.vector.tensor_sub(acol[:, 1:2 * NTT:2], stats[:, 1:2 * NTT:2], acol[:, 0:2 * NTT:2]), ev)
            nc.scalar.wait_ge(ev.s, ev.n)
            inc(nc.scalar.activation(acol[:, 1:2 * NTT:2], acol[:, 1:2 * NTT:2], AF.Ln,
                                     bias=eps_col[:, 0:1], scale=1.0), ev)
            nc.scalar.wait_ge(ev.s, ev.n)
            inc(nc.scalar.activation(acol[:, 1:2 * NTT:2], acol[:, 1:2 * NTT:2], AF.Exp,
                                     bias=0.0, scale=-0.5), ev)
            nc.vector.wait_ge(ev.s, ev.n)
            inc(nc.vector.tensor_mul(acol[:, 0:2 * NTT:2], stats[:, 0:2 * NTT:2], acol[:, 1:2 * NTT:2]), ev)
            nc.vector.wait_ge(ev.s, ev.n)
            inc(nc.vector.tensor_scalar_mul(acol[:, 0:2 * NTT:2], acol[:, 0:2 * NTT:2], -1.0), ev)
            nc.vector.wait_ge(ev.s, ev.n)
            inc(nc.vector.tensor_copy(acolb[:, :], acol[:, :]), ev)
            nc.tensor.wait_ge(ev.s, ev.n)
            for tt in range(NTT):
                for half, col in ((0, 2 * tt + 1), (1, 2 * tt)):
                    nc.tensor.wait_ge(rw.s, rw.n)
                    inc(nc.tensor.transpose(prow[:, :], acolb[:, col:col + 1], identb[:, :]), tr)
                    nc.vector.wait_ge(tr.s, tr.n)
                    inc(nc.vector.tensor_copy(abrow[0:1, half * T + tt * 128: half * T + (tt + 1) * 128],
                                              prow[0:1, :]), rw)
            nc.tensor.wait_ge(rw.s, rw.n)
            nb = 0
            for half in range(2):
                for w0 in range(0, T, TW):
                    pb = pbc0 if nb % 2 == 0 else pbc1
                    if nb >= 2:
                        nc.tensor.wait_ge(bc.s, nb - 1)
                    inc(nc.tensor.matmul(pb[:, :], onesrb[0:1, :], abrow[0:1, half * T + w0: half * T + w0 + TW],
                                         start=True, stop=True), mm)
                    nc.vector.wait_ge(mm.s, mm.n)
                    inc(nc.vector.tensor_copy(abB[:, half * T + w0: half * T + w0 + TW], pb[:, :]), bc)
                    nb += 1
            for dc in range(DC):
                par = dc % 2
                nc.vector.wait_ge(bc.s, bc.n)
                if dc >= 1:
                    nc.vector.wait_ge(nm_.s, 2 * dc)
                inc(nc.vector.tensor_mul(lt[:, :], X[:, dc * T:(dc + 1) * T],
                                         abB[:, 0:T]), nm_)
                nc.vector.wait_ge(nm_.s, nm_.n)
                inc(nc.vector.tensor_add(ln_buf[:, dc * T:(dc + 1) * T], lt[:, :],
                                         abB[:, T:2 * T]), nm_)
            nc.vector.wait_ge(nm_.s, nm_.n)
        barrier()

    def emit_proj_spill(wname, n_oc, spill, ln_buf):
        assert n_oc % 2 == 0
        npair = n_oc // 2
        if wname in gsems:
            nc.sync.wait_ge(gsems[wname].s, gsems[wname].n)
        with (
            nc.sbuf_tensor(tn("pw"), [128, 2 * DC * 128], BF16) as pw,
            nc.sbuf_tensor(tn("pev"), [128, 2 * T], BF16) as pev,
            nc.psum_tensor(tn("pq0"), [128, TW], F32) as pq0,
            nc.psum_tensor(tn("pq1"), [128, TW], F32) as pq1,
            nc.psum_tensor(tn("pq2"), [128, TW], F32) as pq2,
            nc.psum_tensor(tn("pq3"), [128, TW], F32) as pq3,
        ):
            pps = [[pq0, pq1], [pq2, pq3]]
            wl = [ctr("wl0"), ctr("wl1")]
            mm = ctr("pm")
            evd = [ctr("pe0"), ctr("pe1")]
            std = [ctr("ps0"), ctr("ps1")]
            wsrc = cv_full[wname]
            for ip in range(npair):
                for par in range(2):
                    nc.sync.wait_ge(mm.s, max(0, 2 * ip + par - 1))
                    nc.sync.dma_start(
                        pw[:, par * DC * 128:(par + 1) * DC * 128].rearrange("p (c o) -> p c o", o=128),
                        wsrc[:, ds((2 * ip + par) * 128, 128)].rearrange("(c p) o -> p c o", p=128),
                    ).then_inc(wl[par].s, 16)
                    nc.tensor.wait_ge(wl[par].s, 16 * (ip + 1))
                    nc.tensor.wait_ge(evd[par].s, NTW * ip)
                    for tw in range(NTW):
                        for dc in range(DC):
                            mmi = nc.tensor.matmul(pps[par][tw][:, :],
                                             pw[:, par * DC * 128 + dc * 128: par * DC * 128 + (dc + 1) * 128],
                                             ln_buf[:, dc * T + tw * TW: dc * T + tw * TW + TW],
                                             start=(dc == 0), stop=(dc == DC - 1))
                    mmi.then_inc(mm.s, 1)
                    nc.scalar.wait_ge(mm.s, 2 * ip + par + 1)
                    nc.scalar.wait_ge(std[par].s, 16 * ip)
                    for tw in range(NTW):
                        nc.scalar.activation(pev[:, par * T + tw * TW: par * T + (tw + 1) * TW],
                                             pps[par][tw][:, :], AF.Copy, bias=0.0,
                                             scale=1.0).then_inc(evd[par].s, 1)
                    nc.scalar.wait_ge(evd[par].s, NTW * (ip + 1))
                    nc.scalar.dma_start(spill[ds((2 * ip + par) * 128, 128), :],
                                        pev[:, par * T:(par + 1) * T]).then_inc(std[par].s, 16)
            nc.sync.wait_ge(std[0].s, 16 * npair)
            nc.sync.wait_ge(std[1].s, 16 * npair)
        barrier()

    def emit_wo_resid(wname, OT, g):
        with (
            nc.sbuf_tensor(tn("ww"), [128, 2 * DC * 128], BF16) as ww,
            nc.psum_tensor(tn("pw0"), [128, GQ], F32) as pw0,
            nc.psum_tensor(tn("pw1"), [128, GQ], F32) as pw1,
        ):
            wl = [ctr("owl0"), ctr("owl1")]
            mm = ctr("omm")
            if wname in gsems:
                nc.sync.wait_ge(gsems[wname].s, gsems[wname].n)
            ad = [ctr("oad0"), ctr("oad1")]
            wsrc = cv_full[wname]
            pps = [pw0, pw1]
            for ip in range(DC // 2):
                for par in range(2):
                    nc.sync.wait_ge(mm.s, max(0, 2 * ip + par - 1))
                    nc.sync.dma_start(
                        ww[:, par * DC * 128:(par + 1) * DC * 128].rearrange("p (c o) -> p c o", o=128),
                        wsrc[:, ds((2 * ip + par) * 128, 128)].rearrange("(c p) o -> p c o", p=128),
                    ).then_inc(wl[par].s, 16)
                    nc.tensor.wait_ge(wl[par].s, 16 * (ip + 1))
                    nc.tensor.wait_ge(ad[par].s, ip)
                    for dc in range(DC):
                        mmi = nc.tensor.matmul(pps[par][:, :],
                                         ww[:, par * DC * 128 + dc * 128: par * DC * 128 + (dc + 1) * 128],
                                         OT[:, dc * GQ:(dc + 1) * GQ],
                                         start=(dc == 0), stop=(dc == DC - 1))
                    mmi.then_inc(mm.s, 1)
                    nc.vector.wait_ge(mm.s, 2 * ip + par + 1)
                    nc.vector.tensor_add(X[:, ds((2 * ip + par) * T + g * GQ, GQ)],
                                         X[:, ds((2 * ip + par) * T + g * GQ, GQ)],
                                         pps[par][:, :]).then_inc(ad[par].s, 1)
            nc.sync.wait_ge(ad[0].s, DC // 2)
            nc.sync.wait_ge(ad[1].s, DC // 2)
        barrier()

    def emit_ffn():
        with nc.sbuf_tensor(tn("lnf"), [128, DC * T], BF16) as ln_buf:
            emit_ffn_body(ln_buf)

    def emit_ffn_body(ln_buf):
        emit_ln(ln_buf)
        with (
            nc.sbuf_tensor(tn("fw"), [128, 2 * DC * 128], BF16) as fw,
            nc.sbuf_tensor(tn("fsb"), [128, 4], F32) as fsb,
            nc.sbuf_tensor(tn("fst"), [128, 2 * T], F32) as fst,
            nc.psum_tensor(tn("pf0"), [128, TW], F32) as pf0,
            nc.psum_tensor(tn("pf1"), [128, TW], F32) as pf1,
            nc.psum_tensor(tn("pf2"), [128, TW], F32) as pf2,
            nc.psum_tensor(tn("pf3"), [128, TW], F32) as pf3,
        ):
            pps = [[pf0, pf1], [pf2, pf3]]
            wl = [ctr("fwl0"), ctr("fwl1")]
            mm = ctr("fmm")
            nc.sync.wait_ge(gsems["ft"].s, gsems["ft"].n)
            ep = [ctr("fep0"), ctr("fep1")]
            ad = [ctr("fad0"), ctr("fad1")]
            for ip in range(DC // 2):
                for par in range(2):
                    nc.sync.wait_ge(mm.s, max(0, 2 * ip + par - 1))
                    nc.sync.dma_start(
                        fw[:, par * DC * 128:(par + 1) * DC * 128].rearrange("p (c o) -> p c o", o=128),
                        cv_full["ft"][:, ds((2 * ip + par) * 128, 128)].rearrange("(c p) o -> p c o", p=128),
                    ).then_inc(wl[par].s, 16)
                    nc.sync.wait_ge(ep[par].s, NTW * ip)
                    nc.sync.dma_start(fsb[:, 2 * par:2 * par + 1], f_scale[ds(2 * ip + par, 1), :]).then_inc(wl[par].s, 16)
                    nc.sync.dma_start(fsb[:, 2 * par + 1:2 * par + 2], f_bias[ds(2 * ip + par, 1), :]).then_inc(wl[par].s, 16)
                    nc.tensor.wait_ge(wl[par].s, 48 * (ip + 1))
                    nc.tensor.wait_ge(ad[par].s, NTW * ip)
                    for tw in range(NTW):
                        for dc in range(DC):
                            mmi = nc.tensor.matmul(pps[par][tw][:, :],
                                             fw[:, par * DC * 128 + dc * 128: par * DC * 128 + (dc + 1) * 128],
                                             ln_buf[:, dc * T + tw * TW: dc * T + tw * TW + TW],
                                             start=(dc == 0), stop=(dc == DC - 1))
                    mmi.then_inc(mm.s, 1)
                    nc.vector.wait_ge(mm.s, 2 * ip + par + 1)
                    nc.vector.wait_ge(ad[par].s, NTW * ip)
                    nc.vector.wait_ge(wl[par].s, 48 * (ip + 1))
                    for tw in range(NTW):
                        nc.vector.tensor_scalar(fst[:, par * T + tw * TW: par * T + (tw + 1) * TW],
                                                pps[par][tw][:, :],
                                                fsb[:, 2 * par:2 * par + 1],
                                                fsb[:, 2 * par + 1:2 * par + 2],
                                                ALU.mult, ALU.add).then_inc(ep[par].s, 1)
                    nc.vector.wait_ge(ep[par].s, (ip + 1) * NTW)
                    for tw in range(NTW):
                        nc.vector.tensor_add(X[:, ds((2 * ip + par) * T + tw * TW, TW)],
                                             X[:, ds((2 * ip + par) * T + tw * TW, TW)],
                                             fst[:, par * T + tw * TW: par * T + (tw + 1) * TW]).then_inc(ad[par].s, 1)
            nc.sync.wait_ge(ad[0].s, NTW * (DC // 2))
            nc.sync.wait_ge(ad[1].s, NTW * (DC // 2))
        barrier()

    def emit_rotary(sp):
        with (
            nc.sbuf_tensor(tn("rin"), [64, T], BF16) as rin,
            nc.sbuf_tensor(tn("rtm"), [128, NJ * 64], BF16) as rtm,
            nc.sbuf_tensor(tn("rt2"), [128, NJ * 64], BF16) as rt2,
            nc.sbuf_tensor(tn("rt3"), [128, NJ * 64], BF16) as rt3,
            nc.sbuf_tensor(tn("rsin"), [128, NJ * 64], BF16) as rsin,
            nc.sbuf_tensor(tn("rcos"), [128, NJ * 64], BF16) as rcos,
            nc.sbuf_tensor(tn("rout"), [64, T], BF16) as rout,
            nc.psum_tensor(tn("prta"), [128, NJ * 64], BF16) as prta,
            nc.psum_tensor(tn("prtb"), [64, NJ * 128], BF16) as prtb,
        ):
            ld = ctr("rld"); tra = ctr("rtra"); eva = ctr("reva"); mth = ctr("rmth")
            trb = ctr("rtrb"); evb = ctr("revb"); sto = ctr("rsto")
            inc(nc.sync.dma_start(rsin[:, :], rot_sin_in.ap().rearrange("(t p) r -> p t r", p=128)), ld, 16)
            inc(nc.sync.dma_start(rcos[:, :], rot_cos_in.ap().rearrange("(t p) r -> p t r", p=128)), ld, 16)
            for h in range(HB):
                nc.sync.wait_ge(trb.s, trb.n)
                inc(nc.sync.dma_start(rin[:, :], sp[h * 256: h * 256 + 64, :]), ld, 16)
                nc.tensor.wait_ge(ld.s, ld.n)
                nc.tensor.wait_ge(eva.s, eva.n)
                for tt in range(NJ):
                    inc(nc.tensor.transpose(prta[:, tt * 64:(tt + 1) * 64],
                                            rin[:, tt * 128:(tt + 1) * 128], identb[0:64, 0:64]), tra)
                nc.vector.wait_ge(tra.s, tra.n)
                nc.vector.wait_ge(mth.s, mth.n)
                inc(nc.vector.tensor_copy(rtm[:, :], prta[:, :]), eva)
                nc.vector.wait_ge(eva.s, eva.n)
                r4a = rtm.ap().rearrange("p (u v) -> p u v", v=2)
                r4b = rt2.ap().rearrange("p (u v) -> p u v", v=2)
                inc(nc.vector.tensor_copy(r4b[:, :, 0:1], r4a[:, :, 1:2]), mth)
                nc.vector.wait_ge(mth.s, mth.n)
                inc(nc.vector.tensor_copy(r4b[:, :, 1:2], r4a[:, :, 0:1]), mth)
                nc.vector.wait_ge(mth.s, mth.n)
                inc(nc.vector.tensor_mul(rt3[:, :], rtm[:, :], rcos[:, :]), mth)
                nc.vector.wait_ge(mth.s, mth.n)
                inc(nc.vector.tensor_mul(rt2[:, :], rt2[:, :], rsin[:, :]), mth)
                nc.vector.wait_ge(mth.s, mth.n)
                inc(nc.vector.tensor_add(rtm[:, :], rt3[:, :], rt2[:, :]), mth)
                nc.tensor.wait_ge(mth.s, mth.n)
                nc.tensor.wait_ge(evb.s, evb.n)
                for tt in range(NJ):
                    inc(nc.tensor.transpose(prtb[:, tt * 128:(tt + 1) * 128],
                                            rtm[:, tt * 64:(tt + 1) * 64], identb[:, :]), trb)
                nc.vector.wait_ge(trb.s, trb.n)
                nc.vector.wait_ge(sto.s, sto.n)
                inc(nc.vector.tensor_copy(rout.ap().rearrange("p (t q) -> p t q", q=128),
                                          prtb.ap().rearrange("p (t q) -> p t q", q=128)), evb)
                nc.scalar.wait_ge(evb.s, evb.n)
                inc(nc.scalar.dma_start(sp[h * 256: h * 256 + 64, :], rout[:, :]), sto, 16)
            nc.sync.wait_ge(sto.s, sto.n)
        barrier()

    def emit_to_token_major(src_ap, dst, nrc):
        with (
            nc.sbuf_tensor(tn("tmi"), [128, 2 * T], BF16) as tmi,
            nc.sbuf_tensor(tn("tmo"), [128, 2 * NJ * 128], BF16) as tmo,
            nc.psum_tensor(tn("ptm0"), [128, 512], BF16) as ptm0,
            nc.psum_tensor(tn("ptm1"), [128, 512], BF16) as ptm1,
        ):
            ld = [ctr("tml0"), ctr("tml1")]
            tr = ctr("tmt"); ev = ctr("tme"); st = [ctr("tms0"), ctr("tms1")]
            ptms = [ptm0, ptm1]
            for dc in range(nrc):
                par = dc % 2
                nc.sync.wait_ge(tr.s, max(0, tr.n - NJ))
                inc(nc.sync.dma_start(tmi[:, par * T:(par + 1) * T], src_ap[dc * 128:(dc + 1) * 128, :]), ld[par], 16)
                nc.tensor.wait_ge(ld[par].s, ld[par].n)
                nc.tensor.wait_ge(ev.s, ev.n)
                for tt in range(NJ):
                    pt = ptms[tt % 2]
                    inc(nc.tensor.transpose(pt[:, (tt // 2) * 128 % 512: (tt // 2) * 128 % 512 + 128],
                                            tmi[:, par * T + tt * 128: par * T + (tt + 1) * 128],
                                            identb[:, :]), tr)
                nc.vector.wait_ge(tr.s, tr.n)
                nc.vector.wait_ge(st[par].s, st[par].n)
                for tt in range(NJ):
                    pt = ptms[tt % 2]
                    inc(nc.vector.tensor_copy(tmo[:, par * NJ * 128 + tt * 128: par * NJ * 128 + (tt + 1) * 128],
                                              pt[:, (tt // 2) * 128 % 512: (tt // 2) * 128 % 512 + 128]), ev)
                nc.scalar.wait_ge(ev.s, ev.n)
                inc(nc.scalar.dma_start(
                    dst.rearrange("(t p) c -> p t c", p=128)[:, :, dc * 128:(dc + 1) * 128],
                    tmo[:, par * NJ * 128:(par + 1) * NJ * 128].rearrange("p (t q) -> p t q", q=128),
                ), st[par], 16)
            nc.sync.wait_ge(st[0].s, st[0].n)
            nc.sync.wait_ge(st[1].s, st[1].n)
        barrier()

    def emit_gather_async(name, in_ap, out_ap):
        gs = rctr("g_" + name)
        inc(nc.gpsimd.collective_compute("AllGather", ALU.bypass, replica_groups=RG,
                                         ins=[in_ap], outs=[out_ap]), gs)
        gsems[name] = gs

    def slot_block(s):
        return s % NC, s // NC

    def emit_attn_group(g, is_a, OT, KT=None, VO=None):
        HD = 128 if is_a else 256
        NDC = HD // 128
        NH = HA if is_a else HB
        scale = C.scale_a if is_a else C.scale_b
        PVW = HD + 1
        slots1 = [0]
        slots2 = list(range(8 * (SPG * g + SPG - 1) + 8))
        geo = {}
        for s in slots2:
            jmin = max(SPG * g, s // 8)
            w = (SPG * g + SPG - jmin) * 128
            qoff = (jmin - SPG * g) * 128
            bdry = (s // 8) >= SPG * g
            geo[s] = (jmin, w, qoff, bdry)
        b1 = [slots1[i:i + 2] for i in range(0, len(slots1), 2)]
        b2 = [slots2[i:i + 2] for i in range(0, len(slots2), 2)]
        from contextlib import ExitStack
        with ExitStack() as _es:
            def A(g_):
                return _es.enter_context(g_)
            qT = A(nc.sbuf_tensor(tn("qT"), [128, 2 * NDC * GQ], BF16))
            acc = A(nc.sbuf_tensor(tn("acc"), [128, GQ], BF16))
            accb = acc
            crowf = A(nc.sbuf_tensor(tn("crowf"), [1, GQ], F32))
            slqr = A(nc.sbuf_tensor(tn("slqr"), [1, GQ], F32))
            ccol = A(nc.sbuf_tensor(tn("ccol"), [128, SPG], BF16))
            maxB2 = A(nc.sbuf_tensor(tn("maxB2"), [128, 1024], F32))
            stg = A(nc.sbuf_tensor(tn("stg"), [128, 2 * 1024], F32))
            PT = A(nc.sbuf_tensor(tn("PT"), [128, 2 * 1024], BF16))
            alp = A(nc.sbuf_tensor(tn("alp"), [128, 2 * NSLOT], F32)) if is_a else None
            rcp = A(nc.sbuf_tensor(tn("rcp"), [128, SPG], F32))
            osb2 = A(nc.sbuf_tensor(tn("osb2"), [128, SPG * HD], BF16))
            ktb = A(nc.sbuf_tensor(tn("ktb"), [128, 8 * NDC * 128], BF16)) if not is_a else None
            vob = A(nc.sbuf_tensor(tn("vob"), [128, 8 * 260], BF16)) if not is_a else None
            psA = A(nc.psum_tensor(tn("psA"), [128, 1024], F32))
            psB = A(nc.psum_tensor(tn("psB"), [128, 1024], F32))
            pO_guards = [nc.psum_tensor(tn(f"pO{j2}"), [128, 512], F32) for j2 in range(SPG)]
            pOs = [gd.__enter__() for gd in pO_guards]
            pasbs = [pOs[k].ap()[:, 0:64].bitcast(BF16) for k in range(SPG)]
            pasb = pasbs[0]
            pbcv = pOs[0].ap()[:, 0:GQ]
            pss = [psA, psB]
            qld = [ctr("aql0"), ctr("aql1")]; ald = [ctr("aal0"), ctr("aal1")]; sld = [ctr("asl0"), ctr("asl1")]
            kvlK = [ctr(f"akk{i}") for i in range(8)]
            kvlV = [ctr(f"akv{i}") for i in range(8)]
            ktb_src = ktb_g.ap().rearrange("(a p) t -> p a t", p=128) if not is_a else None
            ktb_dst = ktb.ap().rearrange("p (b x) -> p b x", x=128) if not is_a else None
            m1 = ctr("am1"); v1 = ctr("av1"); ca = ctr("aca"); tr1 = ctr("atr")
            rd1 = ctr("ard"); rw1 = ctr("arw"); bcx = ctr("abc"); mbe = ctr("amb")
            m2 = ctr("am2"); v2 = ctr("av2"); ex = ctr("aex"); msk = ctr("ams")
            pv = ctr("apv"); rc = ctr("arc"); oe = ctr("aoe"); otr = ctr("aot"); ovv = ctr("aov")
            if not is_a:
                nc.vector.memset(vob[:, :], 1.0)
                vm0 = ctr("avm")
                inc(nc.vector.engine_nop(), vm0)
                nc.sync.wait_ge(gsems["ktb"].s, gsems["ktb"].n)
                nc.scalar.wait_ge(vm0.s, 1)
                nc.scalar.wait_ge(gsems["vb"].s, gsems["vb"].n)
            m2_end = []; ex_end = []; rw1_end = []
            kv_m_end = {}
            kv_pv_end = {}
            kv_v_mark = {}
            for ih in range(NH):
                hp = ih % 2
                if ih >= 2:
                    nc.sync.wait_ge(m2.s, m2_end[ih - 2])
                if is_a:
                    inc(nc.sync.dma_start(qT[:, hp * GQ:(hp + 1) * GQ],
                                          qkv_sp[ih * 128:(ih + 1) * 128, g * GQ:(g + 1) * GQ]), qld[hp], 16)
                    if ih >= 2:
                        nc.sync.wait_ge(ex.s, ex_end[ih - 2])
                    inc(nc.sync.dma_start(alp[:, hp * NSLOT:(hp + 1) * NSLOT],
                                          alibi_post_t[:, ih * NSLOT:(ih + 1) * NSLOT]), ald[hp], 16)
                    if ih >= 1:
                        nc.sync.wait_ge(rw1.s, rw1_end[ih - 1])
                    inc(nc.sync.dma_start(slqr[0:1, :],
                                          slq_in[0:1, ih * T + g * GQ: ih * T + g * GQ + GQ]), sld[hp], 16)
                else:
                    inc(nc.sync.dma_start(qT[:, hp * NDC * GQ: hp * NDC * GQ + GQ],
                                          qb_sp[ih * 256: ih * 256 + 128, g * GQ:(g + 1) * GQ]), qld[hp], 16)
                    inc(nc.sync.dma_start(qT[:, hp * NDC * GQ + GQ: hp * NDC * GQ + 2 * GQ],
                                          qb_sp[ih * 256 + 128: ih * 256 + 256, g * GQ:(g + 1) * GQ]), qld[hp], 16)
                qbase = hp * NDC * GQ if not is_a else hp * GQ
                qld_h = qld[hp].n
                ald_h = ald[hp].n
                sld_h = sld[hp].n
                s = 0
                if not is_a:
                    kb = 0
                    cc, jj = slot_block(s)
                    if kb in kv_m_end:
                        nc.sync.wait_ge(m1.s if kv_m_end[kb][0] == 1 else m2.s, kv_m_end[kb][1])
                    a0 = cc * (D // 128) + ih * NDC
                    inc(nc.sync.dma_start(
                        ktb_dst[:, kb * NDC: kb * NDC + NDC, :],
                        ktb_src[:, a0: a0 + NDC, jj * 128:(jj + 1) * 128]), kvlK[kb], 16)
                nc.tensor.wait_ge(qld[hp].s, qld_h)
                nc.tensor.wait_ge(oe.s, oe.n)
                nc.tensor.wait_ge(ovv.s, ovv.n)
                if is_a:
                    mmi = nc.tensor.matmul(pOs[0][:, 0:GQ], KT[:, 0:128],
                                           qT[:, qbase:qbase + GQ], start=True, stop=True)
                else:
                    kb = 0
                    nc.tensor.wait_ge(kvlK[kb].s, kvlK[kb].n)
                    for half in range(NDC):
                        mmi = nc.tensor.matmul(pOs[0][:, 0:GQ],
                                               ktb[:, kb * NDC * 128 + half * 128: kb * NDC * 128 + (half + 1) * 128],
                                               qT[:, qbase + half * GQ: qbase + half * GQ + GQ],
                                               start=(half == 0), stop=(half == NDC - 1))
                    kv_m_end[kb] = (1, m1.n + 1)
                inc(mmi, m1)
                nc.vector.wait_ge(m1.s, m1.n)
                nc.vector.wait_ge(tr1.s, tr1.n)
                inc(nc.vector.tensor_add(acc[:, :], pOs[0][:, 0:GQ], p1m[:, :]), v1)
                def emit_assembly():
                    nc.vector.wait_ge(v1.s, v1.n)
                    inc(nc.vector.engine_nop(), ca)
                    nc.tensor.wait_ge(ca.s, ca.n)
                    nc.tensor.wait_ge(rw1.s, rw1.n)
                    for j2 in range(SPG):
                        inc(nc.tensor.transpose(pasbs[j2][:, :], accb[:, j2 * 128:(j2 + 1) * 128], identb[:, :]), tr1)
                    nc.vector.wait_ge(tr1.s, tr1.n)
                    for j2 in range(SPG):
                        inc(nc.vector.reduce_max(ccol[:, j2:j2 + 1], pasbs[j2][:, :], AX.X), rd1)
                    crow = accb
                    nc.tensor.wait_ge(rd1.s, rd1.n)
                    for j2 in range(SPG):
                        inc(nc.tensor.transpose(pasbs[j2][0:1, :], ccol[:, j2:j2 + 1], identb[:, :]), tr1)
                    nc.vector.wait_ge(tr1.s, tr1.n)
                    for j2 in range(SPG):
                        inc(nc.vector.tensor_copy(crow[0:1, j2 * 128:(j2 + 1) * 128], pasbs[j2][0:1, :]), rw1)
                    nc.vector.wait_ge(rw1.s, rw1.n)
                    nc.vector.wait_ge(bcx.s, bcx.n)
                    if is_a:
                        nc.vector.wait_ge(sld[hp].s, sld_h)
                        inc(nc.vector.tensor_add(crowf[0:1, :], crow[0:1, :], slqr[0:1, :]), rw1)
                    else:
                        inc(nc.vector.tensor_copy(crowf[0:1, :], crow[0:1, :]), rw1)
                    rw1_end.append(rw1.n)
                    nc.tensor.wait_ge(rw1.s, rw1.n)
                    nc.tensor.wait_ge(mbe.s, mbe.n)
                    inc(nc.tensor.matmul(pbcv, onesr[0:1, :], crowf[0:1, :],
                                             start=True, stop=True), bcx)
                    nc.vector.wait_ge(bcx.s, bcx.n)
                    inc(nc.vector.tensor_copy(maxB2[:, 0:GQ], pbcv), mbe)
                    nc.vector.wait_ge(mbe.s, mbe.n)
                    inc(nc.vector.tensor_copy(maxB2[:, 512:512 + GQ], maxB2[:, 0:GQ]), mbe)
                NB = len(b2)

                def emit_mm2(bi):
                    par = bi % 2
                    ps = pss[par]
                    bb = b2[bi]
                    jmin, w, qoff, bdry = geo[bb[0]]
                    if not is_a:
                        for i, s in enumerate(bb):
                            kb = s % 8
                            cc, jj = slot_block(s)
                            if kb in kv_m_end:
                                nc.sync.wait_ge(m1.s if kv_m_end[kb][0] == 1 else m2.s, kv_m_end[kb][1])
                            a0 = cc * (D // 128) + ih * NDC
                            inc(nc.sync.dma_start(
                                ktb_dst[:, kb * NDC: kb * NDC + NDC, :],
                                ktb_src[:, a0: a0 + NDC, jj * 128:(jj + 1) * 128]), kvlK[kb], 16)
                            if kb in kv_pv_end:
                                nc.scalar.wait_ge(pv.s, kv_pv_end[kb])
                            inc(nc.scalar.dma_start(vob[:, kb * 260: kb * 260 + 256],
                                                    vb_g[cc * T + jj * 128: cc * T + jj * 128 + 128,
                                                         ih * 256:(ih + 1) * 256]), kvlV[kb], 16)
                            kv_v_mark[s] = kvlV[kb].n
                    if bi == 0 and ih > 0:
                        nc.tensor.wait_ge(v2.s, v2.n)
                    if bi >= 2:
                        nc.tensor.wait_ge(v2.s, v2_mark[bi - 2])
                    for i, s in enumerate(bb):
                        if is_a:
                            mmi = nc.tensor.matmul(ps[:, i * 512: i * 512 + w], KT[:, s * 128:(s + 1) * 128],
                                                   qT[:, qbase + qoff: qbase + qoff + w], start=True, stop=True)
                        else:
                            kb = s % 8
                            nc.tensor.wait_ge(kvlK[kb].s, kvlK[kb].n)
                            for half in range(NDC):
                                mmi = nc.tensor.matmul(ps[:, i * 512: i * 512 + w],
                                                       ktb[:, kb * NDC * 128 + half * 128: kb * NDC * 128 + (half + 1) * 128],
                                                       qT[:, qbase + half * GQ + qoff: qbase + half * GQ + qoff + w],
                                                       start=(half == 0), stop=(half == NDC - 1))
                            kv_m_end[kb] = (2, m2.n + 1)
                    inc(mmi, m2)

                def emit_subexp(bi):
                    par = bi % 2
                    ps = pss[par]
                    bb = b2[bi]
                    jmin, w, qoff, bdry = geo[bb[0]]
                    nb_ = len(bb)
                    nc.vector.wait_ge(m2.s, m2_mark[bi])
                    if bi == 0:
                        nc.vector.wait_ge(mbe.s, mbe.n)
                    if bi >= 2:
                        nc.vector.wait_ge(ex.s, ex_hist[bi - 2])
                    inc(nc.vector.tensor_tensor(
                        stg.ap().rearrange("p (b s q) -> p b s q", b=2, s=2)[:, par, 0:nb_, qoff:qoff + w],
                        ps.ap().rearrange("p (s q) -> p s q", s=2)[:, 0:nb_, 0:w],
                        maxB2.ap().rearrange("p (s q) -> p s q", s=2)[:, 0:nb_, qoff:qoff + w],
                        ALU.subtract), v2)
                    v2_mark[bi] = v2.n
                    nc.scalar.wait_ge(v2.s, v2.n)
                    if is_a and bi == 0:
                        nc.scalar.wait_ge(ald[hp].s, ald_h)
                    if bi >= 2:
                        nc.scalar.wait_ge(pv.s, pv_hist[bi - 2])
                    for i, s in enumerate(bb):
                        if is_a:
                            inc(nc.scalar.activation(PT[:, par * 1024 + i * 512 + qoff: par * 1024 + i * 512 + qoff + w],
                                                     stg[:, par * 1024 + i * 512 + qoff: par * 1024 + i * 512 + qoff + w],
                                                     AF.Exp, bias=alp[:, hp * NSLOT + s: hp * NSLOT + s + 1],
                                                     scale=scale), ex)
                        else:
                            inc(nc.scalar.activation(PT[:, par * 1024 + i * 512 + qoff: par * 1024 + i * 512 + qoff + w],
                                                     stg[:, par * 1024 + i * 512 + qoff: par * 1024 + i * 512 + qoff + w],
                                                     AF.Exp, bias=0.0, scale=scale), ex)
                    ex_hist[bi] = ex.n

                def emit_masks(bi):
                    par = bi % 2
                    bb = b2[bi]
                    for i, s in enumerate(bb):
                        if geo[s][3]:
                            nc.vector.wait_ge(ex.s, ex_hist[bi])
                            _, w, qoff, _bd = geo[s]
                            dp_ = s % 8
                            inc(nc.vector.tensor_tensor(
                                PT.ap().bitcast(U16)[:, par * 1024 + i * 512 + qoff: par * 1024 + i * 512 + qoff + 128],
                                PT.ap().bitcast(U16)[:, par * 1024 + i * 512 + qoff: par * 1024 + i * 512 + qoff + 128],
                                amask[:, dp_ * 128:(dp_ + 1) * 128], ALU.bitwise_and), msk)
                    msk_hist[bi] = msk.n

                def emit_pv2(bi):
                    par = bi % 2
                    bb = b2[bi]
                    nc.tensor.wait_ge(ex.s, ex_hist[bi])
                    nc.tensor.wait_ge(msk.s, msk_hist[bi])
                    if not is_a:
                        for s_ in bb:
                            nc.tensor.wait_ge(kvlV[s_ % 8].s, kv_v_mark[s_])
                    if bb[0] == 0:
                        nc.tensor.wait_ge(mbe.s, mbe.n)
                    for i_, s_ in enumerate(bb):
                        jm_, w_, qoff_, _b = geo[s_]
                        for j_ in range(jm_, SPG * g + SPG):
                            j2_ = j_ - SPG * g
                            rhs_ = vob[:, (s_ % 8) * 260: (s_ % 8) * 260 + PVW] if not is_a else VO[:, s_ * 129: s_ * 129 + PVW]
                            mmi_ = nc.tensor.matmul(pOs[j2_][:, 0:PVW],
                                                    PT[:, par * 1024 + i_ * 512 + qoff_ + (j_ - jm_) * 128:
                                                       par * 1024 + i_ * 512 + qoff_ + (j_ - jm_ + 1) * 128],
                                                    rhs_, start=(s_ == 0), stop=(s_ == 8 * j_ + 7))
                            inc(mmi_, pv)
                        if not is_a:
                            kv_pv_end[s_ % 8] = pv.n
                    pv_hist[bi] = pv.n

                m2_mark = [0] * NB
                msk_hist = [0] * NB
                v2_mark = [0] * NB
                ex_hist = [0] * NB
                pv_hist = [0] * NB
                for k in range(NB + 3):
                    if k < NB:
                        emit_mm2(k)
                        m2_mark[k] = m2.n
                    if k == 1:
                        emit_assembly()
                    if 0 <= k - 3 < NB:
                        emit_pv2(k - 3)
                    if 0 <= k - 1 < NB:
                        emit_subexp(k - 1)
                    if 0 <= k - 2 < NB:
                        emit_masks(k - 2)
                m2_end.append(m2.n)
                ex_end.append(ex.n)
                nc.vector.wait_ge(pv.s, pv.n)
                for j2 in range(SPG):
                    inc(nc.vector.reciprocal(rcp[:, j2:j2 + 1], pOs[j2][:, HD:HD + 1]), rc)
                nc.scalar.wait_ge(rc.s, rc.n)
                nc.scalar.wait_ge(otr.s, otr.n)
                for j2 in range(SPG):
                    inc(nc.scalar.activation(osb2[:, j2 * HD:(j2 + 1) * HD], pOs[j2][:, 0:HD], AF.Copy,
                                             bias=0.0, scale=rcp[:, j2:j2 + 1]), oe)
                nc.tensor.wait_ge(oe.s, oe.n)
                tiles = [(j2, half) for j2 in range(SPG) for half in range(NDC)]
                for r0 in range(0, len(tiles), SPG):
                    nc.tensor.wait_ge(ovv.s, ovv.n)
                    for bk, (j2, half) in enumerate(tiles[r0:r0 + SPG]):
                        inc(nc.tensor.transpose(pasbs[bk][:, :],
                                                osb2[:, j2 * HD + half * 128: j2 * HD + (half + 1) * 128],
                                                identb[:, :]), otr)
                    nc.vector.wait_ge(otr.s, otr.n)
                    for bk, (j2, half) in enumerate(tiles[r0:r0 + SPG]):
                        dcx = ih * NDC + half
                        inc(nc.vector.tensor_copy(OT[:, dcx * GQ + j2 * 128: dcx * GQ + (j2 + 1) * 128],
                                                  pasbs[bk][:, :]), ovv)
            nc.sync.wait_ge(ovv.s, ovv.n)
            for gd in reversed(pO_guards):
                gd.__exit__(None, None, None)
        barrier()

    with nc.sbuf_tensor(tn("ln1"), [128, DC * T], BF16) as ln_buf:
        emit_ln(ln_buf)
        emit_proj_spill("qkv", OA, qkv_sp, ln_buf)
    emit_gather_async("kta", qkv_sp.ap()[DC * 128:(DC + 1) * 128, :], kta_g.ap())
    emit_to_token_major(qkv_sp.ap()[(DC + 1) * 128:(DC + 2) * 128, :], va_loc.ap(), 1)
    emit_gather_async("va", va_loc.ap(), va_g.ap())
    with (
        nc.sbuf_tensor(tn("KTa"), [128, NSLOT * 128], BF16) as KTa,
        nc.sbuf_tensor(tn("VOa"), [128, NSLOT * 129], BF16) as VOa,
        nc.sbuf_tensor(tn("OTa"), [128, DC * GQ], BF16) as OTa,
    ):
        kvc = ctr("kvc")
        vmc = ctr("vmc")
        inc(nc.vector.memset(VOa[:, :], 1.0), vmc)
        nc.sync.wait_ge(vmc.s, 1)
        nc.sync.wait_ge(gsems["kta"].s, gsems["kta"].n)
        nc.sync.wait_ge(gsems["va"].s, gsems["va"].n)
        for cc in range(NC):
            inc(nc.sync.dma_start(
                KTa.ap().rearrange("p (j y) -> p j y", y=NC * 128)[:, :, cc * 128: cc * 128 + 128],
                kta_g[cc * 128:(cc + 1) * 128, :].rearrange("p (j x) -> p j x", x=128)), kvc, 16)
            inc(nc.sync.dma_start(
                VOa.ap().rearrange("p (j y) -> p j y", y=NC * 129)[:, :, cc * 129: cc * 129 + 128],
                va_g[cc * T: cc * T + T, :].rearrange("(j p) x -> p j x", p=128)), kvc, 16)
        nc.sync.wait_ge(kvc.s, kvc.n)
        barrier()
        for g in range(NG):
            emit_attn_group(g, True, OTa, KT=KTa, VO=VOa)
            emit_wo_resid("oa", OTa, g)

    if n_sub >= 2:
        emit_ffn()

    if n_sub >= 3:
        with nc.sbuf_tensor(tn("ln3"), [128, DC * T], BF16) as ln_buf:
            emit_ln(ln_buf)
            emit_proj_spill("q", DC, qb_sp, ln_buf)
            emit_proj_spill("k", DC, ktb_loc, ln_buf)
            emit_proj_spill("v", DC, vtb_sp, ln_buf)
        emit_rotary(ktb_loc)
        emit_gather_async("ktb", ktb_loc.ap(), ktb_g.ap())
        emit_rotary(qb_sp)
        emit_to_token_major(vtb_sp.ap(), vb_loc.ap(), DC)
        emit_gather_async("vb", vb_loc.ap(), vb_g.ap())
        with nc.sbuf_tensor(tn("OTb"), [128, DC * GQ], BF16) as OTb:
            for g in range(NG):
                emit_attn_group(g, False, OTb)
                emit_wo_resid("ob", OTb, g)

    if n_sub >= 4:
        emit_ffn()

    so = ctr("so")
    for dc in range(DC):
        inc(nc.scalar.dma_start(outT[dc * 128:(dc + 1) * 128, :], X[:, dc * T:(dc + 1) * T]), so, 16)
    nc.scalar.wait_ge(so.s, so.n)
    for p in reversed(persist):
        p.__exit__(None, None, None)
    return nc


def kernel(L0_attn_a_hidden_states, L0_attn_a_Wqkv, L0_attn_a_Wo,
           L0_attn_b_Wq, L0_attn_b_Wk, L0_attn_b_Wv, L0_attn_b_Wo,
           L0_ffn_int8_weight, L0_ffn_scale, L0_ffn_bias, _cfg=None, _n_sub=4):
    cfg = _cfg or Cfg()
    NCn, D, T, NJ, DC = cfg.NC, cfg.D, cfg.T, cfg.NJ, cfg.DC
    x = np.asarray(L0_attn_a_hidden_states).reshape(cfg.SEQ, D)
    wqkv = np.ascontiguousarray(np.asarray(L0_attn_a_Wqkv, dtype=np.float32))
    woa = np.ascontiguousarray(np.asarray(L0_attn_a_Wo, dtype=np.float32))
    wq = np.ascontiguousarray(np.asarray(L0_attn_b_Wq, dtype=np.float32))
    wk = np.ascontiguousarray(np.asarray(L0_attn_b_Wk, dtype=np.float32))
    wv = np.ascontiguousarray(np.asarray(L0_attn_b_Wv, dtype=np.float32))
    wob = np.ascontiguousarray(np.asarray(L0_attn_b_Wo, dtype=np.float32))
    w8 = np.ascontiguousarray(np.asarray(L0_ffn_int8_weight))
    fs = np.asarray(L0_ffn_scale, dtype=np.float32).reshape(D)
    fb = np.asarray(L0_ffn_bias, dtype=np.float32).reshape(D)
    fs_t = np.ascontiguousarray(fs.reshape(DC, 128))
    fb_t = np.ascontiguousarray(fb.reshape(DC, 128))

    nc = build_program(cfg, n_sub=_n_sub)
    in_maps = []
    RSL = D // NCn
    for c in range(NCn):
        tks = np.concatenate([np.arange(128) + (8 * j + c) * 128 for j in range(NJ)])
        xc = np.ascontiguousarray(x[tks, :].T.astype(np.float32))
        t = host_tables(cfg, c)
        rsl = slice(c * RSL, (c + 1) * RSL)
        in_maps.append({
            "xT": xc, "w_qkv": np.ascontiguousarray(wqkv[rsl]), "w_oa": np.ascontiguousarray(woa[rsl]),
            "w_q": np.ascontiguousarray(wq[rsl]), "w_k": np.ascontiguousarray(wk[rsl]),
            "w_v": np.ascontiguousarray(wv[rsl]),
            "w_ob": np.ascontiguousarray(wob[rsl]), "w_f8": np.ascontiguousarray(w8[:, rsl]),
            "f_scale": fs_t, "f_bias": fb_t,
            "alibi_raw": t["alibi_raw"], "alibi_post": t["alibi_post"],
            "p1mask": t["p1mask"], "andmask": t["andmask"],
            "rot_sin": t["rot_sin"], "rot_cos": t["rot_cos"],
            "ident_bf": t["ident_bf"],
            "ones_f32": t["ones_f32"], "ones_row": t["ones_row"], "ones_row_bf": t["ones_row_bf"], "slq": t["slq"],
        })
    res = run_bass_kernel_spmd(nc, in_maps, core_ids=list(range(NCn)))
    out = np.zeros((cfg.SEQ, D), dtype=np.float32)
    for c in range(NCn):
        tks = np.concatenate([np.arange(128) + (8 * j + c) * 128 for j in range(NJ)])
        out[tks, :] = np.asarray(res.results[c]["outT"]).T
    return out.reshape(1, cfg.SEQ, D)
